# version 25
# baseline (speedup 1.0000x reference)
import os
import numpy as np
import ml_dtypes
import concourse.bass as bass
import concourse.mybir as mybir
from concourse.bass_utils import run_bass_kernel_spmd
from contextlib import ExitStack

F32 = mybir.dt.float32
BF16 = mybir.dt.bfloat16
AF = mybir.ActivationFunctionType
ALU = mybir.AluOpType
AX = mybir.AxisListType

B, S, D = 4, 4096, 1024
H, DH, DHR = 16, 64, 32
DLAT = 512
FI = 2816
NQ = 2048
QT = 512
NPOS = 4
EXT = (8, 16, 24, 32)
TILES_BY_POS = {0: [0, 3, 4, 7], 1: [1, 2, 5, 6]}
DQK = 96
EPS = 1e-5

SC = float(np.float32(1.0) / np.sqrt(np.float32(96.0)))
CLIP_HI = float(np.exp(np.float64(np.float32(80.0) * np.float32(SC))))
CLIP_LO = float(np.exp(np.float64(np.float32(-80.0) * np.float32(SC))))

DEBUG = os.environ.get("MK_DEBUG", "0") == "1"


class Pipe:
    ENGS = ("sync", "scalar", "vector", "tensor", "gpsimd")

    def __init__(self, nc, stack):
        self.nc = nc
        self.stack = stack
        self.ops = {e: [] for e in self.ENGS}
        self.csem = {e: stack.enter_context(nc.semaphore(f"C{e[:2]}")) for e in self.ENGS}
        self.ccnt = {e: 0 for e in self.ENGS}
        self.dsems = {}

    def _dsem(self, name):
        if name not in self.dsems:
            self.dsems[name] = [self.stack.enter_context(self.nc.semaphore(f"D{name}")), 0]
        return self.dsems[name]

    force_serial = False

    def op(self, engine, fn, deps=()):
        if self.force_serial:
            deps = list(deps) + self.barrier()
        self.ccnt[engine] += 1
        self.ops[engine].append((fn, tuple(deps), self.csem[engine], 1))
        return (self.csem[engine], self.ccnt[engine], engine)

    def dma(self, engine, fn, stream, deps=()):
        if self.force_serial:
            deps = list(deps) + self.barrier()
        s = self._dsem(stream)
        s[1] += 16
        self.ops[engine].append((fn, tuple(deps), s[0], 16))
        return (s[0], s[1], None)

    def last(self, engine):
        return (self.csem[engine], self.ccnt[engine], engine)

    def barrier(self):
        toks = []
        for e in self.ENGS:
            if self.ccnt[e]:
                toks.append((self.csem[e], self.ccnt[e], e))
        for name, (sem, cnt) in self.dsems.items():
            if cnt:
                toks.append((sem, cnt, None))
        return toks

    self_wait = False
    self_wait_lag = 1
    self_wait_engines = ("scalar", "vector", "gpsimd")

    def build(self, block):
        def body_for(name):
            def run(eng):
                waited = {}
                own = 0
                own_sem = self.csem[name]
                sw = self.self_wait and name in self.self_wait_engines
                lag = self.self_wait_lag
                for (fn, deps, sem, inc) in self.ops[name]:
                    if sw and own - lag + 1 > 0 and sem is own_sem:
                        eng.wait_ge(own_sem, own - lag + 1)
                    for (dsem, dval, deng) in deps:
                        if deng == name:
                            continue
                        k = id(dsem)
                        if waited.get(k, 0) >= dval:
                            continue
                        eng.wait_ge(dsem, dval)
                        waited[k] = dval
                    fn(eng).then_inc(sem, inc)
                    if sem is own_sem:
                        own += inc
            return run
        for name, meth in (("sync", block.sync), ("scalar", block.scalar),
                           ("vector", block.vector), ("tensor", block.tensor),
                           ("gpsimd", block.gpsimd)):
            if self.ops[name]:
                meth(body_for(name))


def build_program():
    nc = bass.Bass("TRN2", target_bir_lowering=False)
    DI = nc.dram_tensor

    x_full = DI("x_full", [S, D], F32, kind="ExternalInput").ap()
    xq_in = DI("xq_in", [NQ, D], F32, kind="ExternalInput").ap()
    wdq_t = DI("wdq_t", [D, DLAT], F32, kind="ExternalInput").ap()
    wuq_t = DI("wuq_t", [DLAT, D], F32, kind="ExternalInput").ap()
    wqr1_t = DI("wqr1_t", [DLAT, 256], F32, kind="ExternalInput").ap()
    wqr2_t = DI("wqr2_t", [DLAT, 256], F32, kind="ExternalInput").ap()
    wdkv_t = DI("wdkv_t", [D, DLAT], F32, kind="ExternalInput").ap()
    wuk_t = DI("wuk_t", [DLAT, D], F32, kind="ExternalInput").ap()
    wuv_t = DI("wuv_t", [DLAT, D], F32, kind="ExternalInput").ap()
    wkr1_t = DI("wkr1_t", [D, 16], F32, kind="ExternalInput").ap()
    wkr2_t = DI("wkr2_t", [D, 16], F32, kind="ExternalInput").ap()
    wo_t = DI("wo_t", [D, D], F32, kind="ExternalInput").ap()
    wg_n = DI("wg_n", [FI, D], F32, kind="ExternalInput").ap()
    wu_n = DI("wu_n", [FI, D], F32, kind="ExternalInput").ap()
    wd_n = DI("wd_n", [D, FI], F32, kind="ExternalInput").ap()
    cosk = DI("cosk", [16, S], F32, kind="ExternalInput").ap()
    sink = DI("sink", [16, S], F32, kind="ExternalInput").ap()
    c8q = DI("c8q", [128, NQ], F32, kind="ExternalInput").ap()
    s8q = DI("s8q", [128, NQ], F32, kind="ExternalInput").ap()
    masks_in = DI("masks_in", [NPOS, 8, 128, QT], F32, kind="ExternalInput").ap()
    consts = DI("consts", [128, 16], F32, kind="ExternalInput").ap()

    out = DI("out", [NQ, D], F32, kind="ExternalOutput").ap()

    skind = "ExternalOutput" if DEBUG else "Internal"
    kscr = DI("kscr", [D, S], BF16, kind=skind).ap()
    krscr = DI("krscr", [32, S], BF16, kind=skind).ap()
    vscr = DI("vscr", [D, S], BF16, kind="Internal").ap()
    vtok = DI("vtok", [S, D], BF16, kind=skind).ap()
    qscr = DI("qscr", [H, DQK, NQ], BF16, kind=skind).ap()
    x2scr = DI("x2scr", [NQ, D], F32, kind=skind).ap()
    xlnscr = DI("xlnscr", [NQ, D], F32, kind="Internal").ap()
    xnscr = DI("xnscr", [2, 512, D], BF16, kind="Internal").ap()
    xqscr = DI("xqscr", [2, 128, D], BF16, kind="Internal").ap()
    hqscr = DI("hqscr", [2, 128, FI], BF16, kind="Internal").ap()
    rcpscr = DI("rcpscr", [1, QT], F32, kind="Internal").ap()
    wgq_n = DI("wgq_n", [FI, D], BF16, kind=skind).ap()
    wuq_n = DI("wuq_n", [FI, D], BF16, kind="Internal").ap()
    wdq_n = DI("wdq_n", [D, FI], BF16, kind="Internal").ap()

    stP = ExitStack()
    with stP:
        s_c = stP.enter_context(nc.sbuf_tensor("s_c", [128, 16], F32))
        s_id = stP.enter_context(nc.sbuf_tensor("s_id", [128, 128], F32))
        block = stP.enter_context(nc.Block())
        pr = Pipe(nc, stP)
        pr.force_serial = os.environ.get("MK_SER", "") == "all"
        pr.self_wait = os.environ.get("MK_SE", "1") == "1"
        pr.self_wait_lag = int(os.environ.get("MK_SWD", "1"))
        pr.self_wait_engines = tuple(os.environ.get("MK_SWE", "vector,gpsimd").split(","))
        OP, DMA = pr.op, pr.dma

        tok_c = DMA("sync", lambda e: e.dma_start(out=s_c[:], in_=consts), "cst")
        OP("gpsimd", lambda e: e.memset(s_id[:], 0.0))
        tok_id = OP("gpsimd", lambda e: e.affine_select(
            out=s_id[:], in_=s_id[:], compare_op=ALU.not_equal, fill=1.0,
            base=0, pattern=[[-1, 128]], channel_multiplier=1))

        stA = ExitStack()
        s_x = [stA.enter_context(nc.sbuf_tensor(f"s_x{i}", [128, 4, D], F32)) for i in range(2)]
        s_xn = stA.enter_context(nc.sbuf_tensor("s_xn", [128, 4, D], BF16))
        s_xnf = [stA.enter_context(nc.sbuf_tensor(f"s_xnf{i}", [128, 8, 512], BF16,
                                                  is_transpose_dest=True)) for i in range(2)]
        s_wdq = stA.enter_context(nc.sbuf_tensor("s_wdq", [128, 8, DLAT], BF16))
        s_wuq = stA.enter_context(nc.sbuf_tensor("s_wuq", [128, 4, D], BF16))
        s_wqr1 = stA.enter_context(nc.sbuf_tensor("s_wqr1", [128, 4, 256], BF16))
        s_wqr2 = stA.enter_context(nc.sbuf_tensor("s_wqr2", [128, 4, 256], BF16))
        s_wdkv = stA.enter_context(nc.sbuf_tensor("s_wdkv", [128, 8, DLAT], BF16))
        s_wuk = stA.enter_context(nc.sbuf_tensor("s_wuk", [128, 4, D], BF16))
        s_wuv = stA.enter_context(nc.sbuf_tensor("s_wuv", [128, 4, D], BF16))
        s_wkr1 = stA.enter_context(nc.sbuf_tensor("s_wkr1", [128, 8, 16], BF16))
        s_wkr2 = stA.enter_context(nc.sbuf_tensor("s_wkr2", [128, 8, 16], BF16))
        s_ckv = stA.enter_context(nc.sbuf_tensor("s_ckv", [128, 4, 512], BF16))
        s_tmp = [stA.enter_context(nc.sbuf_tensor(f"s_tmpA{i}", [128, 512], BF16)) for i in range(2)]
        s_cs = stA.enter_context(nc.sbuf_tensor("s_cs", [48, S], F32))
        s_c8q = stA.enter_context(nc.sbuf_tensor("s_c8q", [128, NQ], F32))
        s_s8q = stA.enter_context(nc.sbuf_tensor("s_s8q", [128, NQ], F32))
        s_bn6 = [stA.enter_context(nc.sbuf_tensor(f"s_bn6{i}", [128, 12], F32)) for i in range(2)]
        s_mv = [stA.enter_context(nc.sbuf_tensor(f"s_mv{i}", [128, 2], F32)) for i in range(2)]
        s_sd = [stA.enter_context(nc.sbuf_tensor(f"s_sd{i}", [128, 1], F32)) for i in range(2)]
        s_rsd = [stA.enter_context(nc.sbuf_tensor(f"s_rsd{i}", [128, 1], F32)) for i in range(2)]
        s_ta = stA.enter_context(nc.sbuf_tensor("s_ta", [128, 512], F32))
        s_tb = stA.enter_context(nc.sbuf_tensor("s_tb", [128, 512], F32))
        s_ra = stA.enter_context(nc.sbuf_tensor("s_ra", [128, 512], BF16))
        s_rb = stA.enter_context(nc.sbuf_tensor("s_rb", [128, 512], BF16))
        pM = [stA.enter_context(nc.psum_tensor(f"pM{i}", [128, 512], F32)) for i in range(2)]
        pQ1 = stA.enter_context(nc.psum_tensor("pQ1", [128, 512], F32))
        pQ2 = stA.enter_context(nc.psum_tensor("pQ2", [128, 512], F32))


        def wload(dst, src, pat, **kw):
            return DMA("gpsimd", lambda e: e.dma_start(out=dst[:], in_=src.rearrange(pat, **kw)), "wl")
        wload_all = [
            wload(s_wdq, wdq_t, "(fc p) l -> p fc l", p=128),
            wload(s_wuq, wuq_t, "(lc p) o -> p lc o", p=128),
            wload(s_wqr1, wqr1_t, "(lc p) o -> p lc o", p=128),
            wload(s_wqr2, wqr2_t, "(lc p) o -> p lc o", p=128),
            wload(s_wdkv, wdkv_t, "(fc p) l -> p fc l", p=128),
            wload(s_wuk, wuk_t, "(lc p) o -> p lc o", p=128),
            wload(s_wuv, wuv_t, "(lc p) o -> p lc o", p=128),
            wload(s_wkr1, wkr1_t, "(fc p) o -> p fc o", p=128),
            wload(s_wkr2, wkr2_t, "(fc p) o -> p fc o", p=128),
            DMA("sync", lambda e: e.dma_start(out=s_cs[0:16, :], in_=cosk), "tbl"),
            DMA("sync", lambda e: e.dma_start(out=s_cs[32:48, :], in_=sink), "tbl"),
            DMA("sync", lambda e: e.dma_start(out=s_c8q[:], in_=c8q), "tbl"),
            DMA("sync", lambda e: e.dma_start(out=s_s8q[:], in_=s8q), "tbl"),
        ]

        def emit_quant(x_ap, q_ap, Fw, t, dep_in):
            g = emit_quant_gen(x_ap, q_ap, Fw, t, dep_in)
            r = None
            while True:
                try:
                    r = g.send(None) or r
                except StopIteration:
                    break
            return r

        def emit_quant_gen(x_ap, q_ap, Fw, t, dep_in):
            nb = Fw // 16
            P_ = x_ap.shape[0] if hasattr(x_ap, "shape") else 128
            P_ = x_ap.shape[0]
            x3 = x_ap.rearrange("p (b w) -> p b w", w=16)
            t_amx = OP("vector", lambda e: e.tensor_reduce(
                out=t["am"][0:P_, 0:nb], in_=x3, axis=AX.X, op=ALU.max,
                apply_absolute_value=True), deps=dep_in)
            yield
            OP("vector", lambda e: e.tensor_scalar(
                out=t["sc"][0:P_, 0:nb], in0=t["am"][0:P_, 0:nb], scalar1=1.0 / 12.0,
                scalar2=5e-13, op0=ALU.mult, op1=ALU.max), deps=[t_amx])
            t_r2 = OP("vector", lambda e: e.reciprocal(
                out=t["rs"][0:P_, 0:nb], in_=t["sc"][0:P_, 0:nb]))
            yield
            rsb = t["rs"][0:P_, 0:nb].rearrange("p (b one) -> p b one", one=1).broadcast_to([P_, nb, 16])
            scb = t["sc"][0:P_, 0:nb].rearrange("p (b one) -> p b one", one=1).broadcast_to([P_, nb, 16])
            au = t["au"][0:P_, 0:Fw]
            au3 = au.rearrange("p (b w) -> p b w", w=16)
            t_abs = OP("scalar", lambda e: e.activation(out=au, in_=x_ap, func=AF.Abs),
                       deps=dep_in)
            yield
            OP("vector", lambda e: e.tensor_tensor(out=au3, in0=au3, in1=rsb, op=ALU.mult),
               deps=[t_abs, t_r2])
            yield
            L = t["L"][0:P_, 0:Fw]
            t1 = t["t1"][0:P_, 0:Fw]

            def cmp(dst, thr):
                OP("vector", lambda e, dst=dst, thr=thr: e.tensor_scalar(
                    out=dst, in0=au, scalar1=float(thr), scalar2=None, op0=ALU.is_gt))
            cmp(L, 0.5)
            yield
            for thr in (1.5, 2.5, 3.5):
                cmp(t1, thr)
                yield
                OP("vector", lambda e: e.tensor_add(out=L, in0=L, in1=t1))
                yield
            for thr, w in ((5.0, 2.0), (7.0, 2.0), (10.0, 4.0)):
                cmp(t1, thr)
                yield
                OP("vector", lambda e, w=w: e.scalar_tensor_tensor(
                    out=L, in0=t1, scalar=w, in1=L, op0=ALU.mult, op1=ALU.add))
                yield
            t_lastL = pr.last("vector")
            tsg = OP("scalar", lambda e: e.activation(out=t1, in_=x_ap, func=AF.Sign),
                     deps=[t_lastL])
            yield
            OP("vector", lambda e: e.tensor_mul(out=L, in0=L, in1=t1), deps=[tsg])
            yield
            tq = OP("vector", lambda e: e.tensor_tensor(
                out=q_ap.rearrange("p (b w) -> p b w", w=16),
                in0=L.rearrange("p (b w) -> p b w", w=16), in1=scb, op=ALU.mult))
            yield (tq, tsg)

        ln_exp_war = {0: (), 1: ()}

        def emit_ln_sub(x_ap, xn_ap, dep_in, lp):
            OP("vector", lambda e, x_ap=x_ap, lp=lp: e.bn_stats(
                out=s_bn6[lp][:, 0:6], in_=x_ap[:, 0:512]), deps=list(dep_in))
            OP("vector", lambda e, x_ap=x_ap, lp=lp: e.bn_stats(
                out=s_bn6[lp][:, 6:12], in_=x_ap[:, 512:1024]))
            t2_ = OP("vector", lambda e, lp=lp: e.bn_aggr(out=s_mv[lp][:], in_=s_bn6[lp][:]),
                     deps=list(ln_exp_war[lp]))
            t3_ = OP("scalar", lambda e, lp=lp: e.activation(
                out=s_sd[lp][:], in_=s_mv[lp][:, 1:2], func=AF.Ln, bias=s_c[:, 1:2]),
                deps=[t2_, tok_c])
            t4_ = OP("scalar", lambda e, lp=lp: e.activation(
                out=s_rsd[lp][:], in_=s_sd[lp][:], func=AF.Exp, scale=s_c[:, 10:11]))
            ln_exp_war[lp] = [t4_]
            tn = OP("vector", lambda e, x_ap=x_ap, xn_ap=xn_ap, lp=lp: e.tensor_scalar(
                out=xn_ap, in0=x_ap, scalar1=s_mv[lp][:, 0:1], scalar2=s_rsd[lp][:],
                op0=ALU.subtract, op1=ALU.mult), deps=[t4_])
            return tn

        xload_free = {0: (), 1: ()}
        xnf_free = {0: (), 1: ()}
        xnscr_free = {0: (), 1: ()}
        xn_free = [()]
        evac_tok = {0: None, 1: None}
        tmp_free = {0: (), 1: ()}
        kv_store_toks = []
        q_store_toks = []
        rope_war = [()]
        mm_g = [0]

        def mm_group(lhs_list, rhs_list, extra_deps):
            g = mm_g[0]
            mm_g[0] += 1
            p = pM[g % 2]
            dep_war = [evac_tok[g % 2]] if evac_tok[g % 2] else []
            n = len(lhs_list)
            tok = None
            for i, (lf, rf) in enumerate(zip(lhs_list, rhs_list)):
                d = (list(extra_deps) + dep_war) if i == 0 else ()
                tok = OP("tensor", lambda e, lf=lf, rf=rf, i=i, n=n, p=p: e.matmul(
                    p[:, :], lf, rf, start=(i == 0), stop=(i == n - 1)), deps=d)
            return g, p, tok

        for ti in range(12):
            par = ti % 2
            is_q = ti >= 8
            src = x_full[(ti * 512):(ti + 1) * 512, :] if not is_q else \
                xq_in[(ti - 8) * 512:(ti - 7) * 512, :]
            t0 = (ti * 512) if not is_q else (ti - 8) * 512
            xl = DMA("sync", lambda e, src=src, par=par: e.dma_start(
                out=s_x[par][:], in_=src.rearrange("(s p) d -> p s d", p=128)),
                f"xl{par}", deps=xload_free[par])
            last_ln = None
            for sub in range(4):
                d = [xl] + (list(xn_free[0]) if sub == 0 else [])
                last_ln = emit_ln_sub(s_x[par][:, sub, :], s_xn[:, sub, :], d, sub % 2)
            xload_free[par] = [last_ln]
            xst = DMA("gpsimd", lambda e, par=par: e.dma_start(
                out=xnscr[par].rearrange("(s p) d -> p s d", p=128), in_=s_xn[:]),
                f"xns{par}", deps=[last_ln] + list(xnscr_free[par]))
            xn_free[0] = [xst]
            xt = DMA("sync", lambda e, par=par: e.dma_start(
                out=s_xnf[par][:], in_=xnscr[par], transpose=True),
                f"xnt{par}", deps=[xst] + list(xnf_free[par]))
            xnscr_free[par] = [xt]
            deps0 = [xt] + (wload_all if ti == 0 else [])
            wdown = s_wdkv if not is_q else s_wdq
            ck_toks = []
            for oc in range(4):
                g, p, tk = mm_group(
                    [wdown[:, fc, oc * 128:(oc + 1) * 128] for fc in range(8)],
                    [s_xnf[par][:, fc, :] for fc in range(8)], deps0)
                te = OP("scalar", lambda e, p=p, oc=oc: e.copy(out=s_ckv[:, oc, :], in_=p[:, :]),
                        deps=[tk])
                evac_tok[g % 2] = te
                ck_toks.append(te)
            ck_dep = [ck_toks[-1]]
            if not is_q:
                for (wsb, scr, stream) in ((s_wuk, kscr, "kst"), (s_wuv, vscr, "vst")):
                    for oc8 in range(8):
                        g, p, tk = mm_group(
                            [wsb[:, lc, oc8 * 128:(oc8 + 1) * 128] for lc in range(4)],
                            [s_ckv[:, lc, :] for lc in range(4)], ck_dep)
                        te = OP("scalar", lambda e, p=p, g=g: e.copy(out=s_tmp[g % 2][:], in_=p[:, :]),
                                deps=[tk] + list(tmp_free[g % 2]))
                        evac_tok[g % 2] = te
                        td = DMA("gpsimd", lambda e, scr=scr, oc8=oc8, t0=t0, g=g: e.dma_start(
                            out=scr[oc8 * 128:(oc8 + 1) * 128, t0:t0 + 512], in_=s_tmp[g % 2][:]),
                            stream, deps=[te])
                        tmp_free[g % 2] = [td]
                        kv_store_toks.append(td)
                tkr = None
                for half_i, wkr in ((0, s_wkr1), (1, s_wkr2)):
                    rows = slice(0, 16) if half_i == 0 else slice(32, 48)
                    for fc in range(8):
                        d = (deps0 + list(rope_war[0])) if (half_i == 0 and fc == 0) else ()
                        tkr = OP("tensor", lambda e, fc=fc, rows=rows, wkr=wkr, par=par: e.matmul(
                            pQ1[rows, :], wkr[:, fc, :], s_xnf[par][:, fc, :],
                            start=(fc == 0), stop=(fc == 7)), deps=d)
                ck_ = s_cs[0:16, t0:t0 + 512]
                sk_ = s_cs[32:48, t0:t0 + 512]
                OP("vector", lambda e, ck_=ck_: e.tensor_mul(out=s_ta[0:16, :], in0=pQ1[0:16, :], in1=ck_),
                   deps=[tkr] + list(rope_war[0]))
                OP("vector", lambda e, sk_=sk_: e.tensor_mul(out=s_tb[0:16, :], in0=pQ1[32:48, :], in1=sk_))
                OP("vector", lambda e: e.tensor_sub(out=s_ra[0:16, :], in0=s_ta[0:16, :], in1=s_tb[0:16, :]))
                OP("vector", lambda e, sk_=sk_: e.tensor_mul(out=s_ta[0:16, :], in0=pQ1[0:16, :], in1=sk_))
                OP("vector", lambda e, ck_=ck_: e.tensor_mul(out=s_tb[0:16, :], in0=pQ1[32:48, :], in1=ck_))
                trb = OP("vector", lambda e: e.tensor_add(out=s_rb[0:16, :], in0=s_ta[0:16, :], in1=s_tb[0:16, :]))
                d1 = DMA("gpsimd", lambda e, t0=t0: e.dma_start(
                    out=krscr[0:16, t0:t0 + 512], in_=s_ra[0:16, :]), "krst", deps=[trb])
                d2 = DMA("gpsimd", lambda e, t0=t0: e.dma_start(
                    out=krscr[16:32, t0:t0 + 512], in_=s_rb[0:16, :]), "krst", deps=[trb])
                rope_war[0] = [d1, d2]
                kv_store_toks += [d1, d2]
            else:
                for oc8 in range(8):
                    g, p, tk = mm_group(
                        [s_wuq[:, lc, oc8 * 128:(oc8 + 1) * 128] for lc in range(4)],
                        [s_ckv[:, lc, :] for lc in range(4)], ck_dep)
                    te = OP("scalar", lambda e, p=p, g=g: e.copy(out=s_tmp[g % 2][:], in_=p[:, :]),
                            deps=[tk] + list(tmp_free[g % 2]))
                    evac_tok[g % 2] = te
                    td = DMA("gpsimd", lambda e, oc8=oc8, t0=t0, g=g: e.dma_start(
                        out=qscr[2 * oc8:2 * oc8 + 2, 0:64, t0:t0 + 512], in_=s_tmp[g % 2][:]),
                        "qst", deps=[te])
                    tmp_free[g % 2] = [td]
                    q_store_toks.append(td)
                for oc2 in range(2):
                    tq1 = None
                    for pnum, wq_, pp in ((0, s_wqr1, pQ1), (1, s_wqr2, pQ2)):
                        for lc in range(4):
                            d = (ck_dep + list(rope_war[0])) if (pnum == 0 and lc == 0) else ()
                            tq1 = OP("tensor", lambda e, oc2=oc2, lc=lc, wq_=wq_, pp=pp: e.matmul(
                                pp[:, :], wq_[:, lc, oc2 * 128:(oc2 + 1) * 128], s_ckv[:, lc, :],
                                start=(lc == 0), stop=(lc == 3)), deps=d)
                    cq_ = s_c8q[:, t0:t0 + 512]
                    sq_ = s_s8q[:, t0:t0 + 512]
                    OP("vector", lambda e, cq_=cq_: e.tensor_mul(out=s_ta[:], in0=pQ1[:, :], in1=cq_),
                       deps=[tq1] + list(rope_war[0]))
                    OP("vector", lambda e, sq_=sq_: e.tensor_mul(out=s_tb[:], in0=pQ2[:, :], in1=sq_))
                    OP("vector", lambda e: e.tensor_sub(out=s_ra[:], in0=s_ta[:], in1=s_tb[:]))
                    OP("vector", lambda e, sq_=sq_: e.tensor_mul(out=s_ta[:], in0=pQ1[:, :], in1=sq_))
                    OP("vector", lambda e, cq_=cq_: e.tensor_mul(out=s_tb[:], in0=pQ2[:, :], in1=cq_))
                    trb = OP("vector", lambda e: e.tensor_add(out=s_rb[:], in0=s_ta[:], in1=s_tb[:]))
                    d1 = DMA("gpsimd", lambda e, oc2=oc2, t0=t0: e.dma_start(
                        out=qscr[oc2 * 8:(oc2 + 1) * 8, 64:80, t0:t0 + 512], in_=s_ra[:]),
                        "qst", deps=[trb])
                    d2 = DMA("gpsimd", lambda e, oc2=oc2, t0=t0: e.dma_start(
                        out=qscr[oc2 * 8:(oc2 + 1) * 8, 80:96, t0:t0 + 512], in_=s_rb[:]),
                        "qst", deps=[trb])
                    rope_war[0] = [d1, d2]
                    q_store_toks += [d1, d2]
            xnf_free[par] = [pr.last("tensor")]
            
        barA = pr.barrier()
        stA.close()

        stV = ExitStack()
        s_vtmp = [stV.enter_context(nc.sbuf_tensor(f"s_vtmp{i}", [128, D], BF16,
                                                   is_transpose_dest=True)) for i in range(2)]
        vt_prev = {0: barA, 1: barA}
        vtok_toks = []
        for kc in range(32):
            pv_ = kc % 2
            tl = DMA("sync", lambda e, kc=kc, pv_=pv_: e.dma_start(
                out=s_vtmp[pv_][:], in_=vscr[:, kc * 128:(kc + 1) * 128], transpose=True),
                f"vtt{pv_}", deps=vt_prev[pv_])
            ts = DMA("gpsimd", lambda e, kc=kc, pv_=pv_: e.dma_start(
                out=vtok[kc * 128:(kc + 1) * 128, :], in_=s_vtmp[pv_][:]), "vts", deps=[tl])
            vt_prev[pv_] = [ts]
            vtok_toks.append(ts)
        stV.close()

        stB = ExitStack()
        s_k = [stB.enter_context(nc.sbuf_tensor(f"s_k{i}", [96, S], BF16)) for i in range(2)]
        s_q = [stB.enter_context(nc.sbuf_tensor(f"s_q{i}", [96, NQ], BF16)) for i in range(2)]
        s_v1 = [stB.enter_context(nc.sbuf_tensor(f"s_v1{i}", [128, 32, 65], BF16)) for i in range(2)]
        s_p = [stB.enter_context(nc.sbuf_tensor(f"s_p{i}", [128, 1024], BF16)) for i in range(3)]
        s_attn = stB.enter_context(nc.sbuf_tensor("s_attn", [64, H, NQ], BF16))
        s_masks = stB.enter_context(nc.sbuf_tensor("s_masks", [128, NPOS, 8, QT], BF16))
        s_rcp = stB.enter_context(nc.sbuf_tensor("s_rcp", [1, QT], F32))
        s_bc = stB.enter_context(nc.sbuf_tensor("s_bc", [64, QT], F32))
        s_ao = stB.enter_context(nc.sbuf_tensor("s_ao", [128, 8, QT], F32))

        stBq = ExitStack()
        WQF = 2816
        s_wf = stBq.enter_context(nc.sbuf_tensor("s_wf", [128, WQF], F32))
        s_am = stBq.enter_context(nc.sbuf_tensor("s_am", [128, WQF // 16], F32))
        s_sc = stBq.enter_context(nc.sbuf_tensor("s_sc", [128, WQF // 16], F32))
        s_rs = stBq.enter_context(nc.sbuf_tensor("s_rs", [128, WQF // 16], F32))
        s_au = stBq.enter_context(nc.sbuf_tensor("s_au", [128, WQF], F32))
        s_L = stBq.enter_context(nc.sbuf_tensor("s_L", [128, WQF], BF16))
        s_t1 = stBq.enter_context(nc.sbuf_tensor("s_t1", [128, WQF], BF16))
        s_qq = stBq.enter_context(nc.sbuf_tensor("s_qq", [128, WQF], BF16))
        tQ = dict(am=s_am, sc=s_sc, rs=s_rs, au=s_au, L=s_L, t1=s_t1)

        wq_batches = []
        for (src_, dst_, nch, Fw) in ((wg_n, wgq_n, FI // 128, D),
                                      (wu_n, wuq_n, FI // 128, D),
                                      (wd_n, wdq_n, D // 128, FI)):
            per = WQF // Fw
            ch = 0
            while ch < nch:
                n_ = min(per, nch - ch)
                wq_batches.append((src_, dst_, [128 * (ch + i) for i in range(n_)], Fw))
                ch += n_
        wq_prev = [()]
        wq_store_toks = []

        def wq_driver():
            for (src_, dst_, r0s, Fw) in wq_batches:
                tl = None
                for i, r0 in enumerate(r0s):
                    tl = DMA("sync", lambda e, src_=src_, r0=r0, i=i, Fw=Fw: e.dma_start(
                        out=s_wf[:, i * Fw:(i + 1) * Fw], in_=src_[r0:r0 + 128, :]), "wf",
                        deps=wq_prev[0] if i == 0 else ())
                Ft = Fw * len(r0s)
                res = None
                gen = emit_quant_gen(s_wf[:, 0:Ft], s_qq[:, 0:Ft], Ft, tQ, [tl])
                while True:
                    try:
                        r = gen.send(None)
                        if r is not None:
                            res = r
                        yield
                    except StopIteration:
                        break
                tq, tsg = res
                ts = None
                for i, r0 in enumerate(r0s):
                    ts = DMA("gpsimd", lambda e, dst_=dst_, r0=r0, i=i, Fw=Fw: e.dma_start(
                        out=dst_[r0:r0 + 128, :], in_=s_qq[:, i * Fw:(i + 1) * Fw]), "wqs",
                        deps=[tq] if i == 0 else ())
                wq_prev[0] = [tq, tsg, ts]
                wq_store_toks.append(ts)
                yield

        wq_it = [wq_driver()]

        def emit_wq_some(n):
            for _ in range(n):
                try:
                    next(wq_it[0])
                except StopIteration:
                    return

        stBp = ExitStack()
        pAT = [stBp.enter_context(nc.psum_tensor(f"pAT{i}", [128, 512], F32)) for i in range(2)]
        pS = [stBp.enter_context(nc.psum_tensor(f"pS{i}", [128, 1024], F32)) for i in range(3)]

        tmask = DMA("gpsimd", lambda e: e.dma_start(
            out=s_masks[:], in_=masks_in.rearrange("a b p q -> p a b q")), "bmk", deps=barA)
        tv1 = [OP("gpsimd", lambda e, i=i: e.memset(s_v1[i][:], 1.0), deps=barA) for i in range(2)]

        kv_bar = [kv_store_toks[-1], q_store_toks[-1], vtok_toks[-1]]
        head_free = {0: [tv1[0]] + kv_bar, 1: [tv1[1]] + kv_bar}
        head_toks = {}

        def load_head(h, par, deps):
            DMA("sync", lambda e, h=h, par=par: e.dma_start(
                out=s_k[par][0:64, :], in_=kscr[h * 64:(h + 1) * 64, :]), f"bk{par}", deps=deps)
            DMA("sync", lambda e, par=par: e.dma_start(out=s_k[par][64:96, :], in_=krscr), f"bk{par}")
            DMA("sync", lambda e, h=h, par=par: e.dma_start(out=s_q[par][:], in_=qscr[h]), f"bk{par}")
            t4_ = DMA("sync", lambda e, h=h, par=par: e.dma_start(
                out=s_v1[par][:, :, 0:64],
                in_=vtok[:, h * 64:(h + 1) * 64].rearrange("(kc p) o -> p kc o", p=128)), f"bk{par}")
            return [t4_]

        head_toks[0] = load_head(0, 0, head_free[0])

        pS_free = {0: (), 1: (), 2: ()}
        pP_free = {0: (), 1: (), 2: ()}
        pAT_free = {0: (), 1: ()}
        bc_free = [()]
        bc2_free = [()]
        bc3_free = [()]
        at_i = [0]

        for h in range(16):
            par = h % 2
            if h + 1 < 16:
                head_toks[h + 1] = load_head(h + 1, (h + 1) % 2, head_free[(h + 1) % 2])
            kload = head_toks[h]
            last_S = None
            last_PV = None
            groups = []
            for pos in range(NPOS):
                ai = at_i[0] % 2
                at_i[0] += 1
                ng = EXT[pos] // 2
                for g in range(ng):
                    groups.append((pos, g, ai))
            LOOK = 2
            gi = [0]
            tm_of = {}

            def emit_S(idx):
                pos, g, ai = groups[idx]
                ext = EXT[pos]
                qsl = slice(pos * QT, (pos + 1) * QT)
                gp = idx % 3
                st_ = None
                for kk in range(2):
                    kc = 2 * g + kk
                    d = (list(pS_free[gp]) + list(kload)) if kk == 0 else ()
                    st_ = OP("tensor", lambda e, kc=kc, gp=gp, kk=kk, par=par, qsl=qsl: e.matmul(
                        pS[gp][:, kk * 512:(kk + 1) * 512],
                        s_k[par][:, kc * 128:(kc + 1) * 128], s_q[par][:, qsl],
                        start=True, stop=True), deps=d)
                texp = OP("scalar", lambda e, gp=gp: e.activation(
                    out=s_p[gp][:], in_=pS[gp][:, :], func=AF.Exp, scale=s_c[:, 0:1]),
                    deps=[st_, tok_c] + list(pP_free[gp]))
                pS_free[gp] = [texp]
                tm = OP("vector", lambda e, gp=gp: e.tensor_scalar(
                    out=s_p[gp][:], in0=s_p[gp][:], scalar1=CLIP_HI, scalar2=CLIP_LO,
                    op0=ALU.min, op1=ALU.max), deps=[texp])
                j0 = 2 * g - (ext - 8)
                if j0 >= 0:
                    tm = OP("vector", lambda e, gp=gp, pos=pos, j0=j0: e.tensor_mul(
                        out=s_p[gp][:], in0=s_p[gp][:],
                        in1=s_masks[:, pos, j0:j0 + 2, :].rearrange("p a q -> p (a q)")),
                        deps=[tmask])
                tm_of[idx] = tm
                return st_

            def emit_PV(idx):
                pos, g, ai = groups[idx]
                ext = EXT[pos]
                gp = idx % 3
                tm = tm_of.pop(idx)
                pv_tok = None
                for kk in range(2):
                    kc = 2 * g + kk
                    d = [tm] if kk == 0 else ()
                    if kc == 0:
                        d = [tm] + list(pAT_free[ai])
                    pv_tok = OP("tensor", lambda e, kc=kc, gp=gp, kk=kk, par=par, ai=ai, ext=ext: e.matmul(
                        pAT[ai][0:65, :], s_v1[par][:, kc, :],
                        s_p[gp][:, kk * 512:(kk + 1) * 512],
                        start=(kc == 0), stop=(kc == ext - 1)), deps=d)
                pP_free[gp] = [pv_tok]
                if 2 * g + 1 == ext - 1:
                    emit_divide(pos, ai, pv_tok)
                return pv_tok

            def emit_divide(pos, ai, pv_tok):
                qsl = slice(pos * QT, (pos + 1) * QT)
                trc0 = OP("scalar", lambda e, ai=ai: e.activation(
                    out=s_rcp[:], in_=pAT[ai][64:65, :], func=AF.Ln),
                    deps=[pv_tok] + list(bc_free[0]))
                trc = OP("scalar", lambda e: e.activation(
                    out=s_rcp[:], in_=s_rcp[:], func=AF.Exp, scale=s_c[0:1, 11:12]),
                    deps=[tok_c])
                tst = DMA("gpsimd", lambda e: e.dma_start(out=rcpscr, in_=s_rcp[:]),
                          "rcs", deps=[trc] + list(bc2_free[0]))
                bc_free[0] = [tst]
                tld = DMA("gpsimd", lambda e: e.dma_start(
                    out=s_bc[:], in_=rcpscr.broadcast_to([64, QT])), "rcl",
                    deps=[tst] + list(bc3_free[0]))
                bc2_free[0] = [tld]
                tdv = OP("vector", lambda e, ai=ai, h=h, qsl=qsl: e.tensor_mul(
                    out=s_attn[:, h, qsl], in0=pAT[ai][0:64, :], in1=s_bc[:]), deps=[tld])
                pAT_free[ai] = [tdv]
                bc3_free[0] = [tdv]

            n_g = len(groups)
            for idx in range(n_g + LOOK):
                if idx < n_g:
                    last_S = emit_S(idx)
                if idx >= LOOK:
                    last_PV = emit_PV(idx - LOOK)
                emit_wq_some(1)
            head_free[par] = [last_S, last_PV]
            if h == 15:
                emit_wq_some(9999)

        div_last = pr.last("vector")
        stBp.close()
        stBq.close()
        stBW = ExitStack()
        s_xq4 = [stBW.enter_context(nc.sbuf_tensor(f"s_xq4{i}", [128, D], F32)) for i in range(2)]
        s_x2t = [stBW.enter_context(nc.sbuf_tensor(f"s_x2t{i}", [128, D], F32)) for i in range(2)]
        s_xlnW = stBW.enter_context(nc.sbuf_tensor("s_xlnW", [128, D], F32))
        s_bnW = [stBW.enter_context(nc.sbuf_tensor(f"s_bnW{i}", [128, 12], F32)) for i in range(2)]
        s_mvW = [stBW.enter_context(nc.sbuf_tensor(f"s_mvW{i}", [128, 2], F32)) for i in range(2)]
        s_sdW = [stBW.enter_context(nc.sbuf_tensor(f"s_sdW{i}", [128, 1], F32)) for i in range(2)]
        s_rsdW = [stBW.enter_context(nc.sbuf_tensor(f"s_rsdW{i}", [128, 1], F32)) for i in range(2)]
        s_wo = stBW.enter_context(nc.sbuf_tensor("s_wo", [64, H, D], BF16))
        lnW_war = {0: (), 1: ()}
        xlnW_free = [()]
        xln_store = []

        def emit_lnW(x_ap, tt, lp):
            OP("vector", lambda e, x_ap=x_ap, lp=lp: e.bn_stats(
                out=s_bnW[lp][:, 0:6], in_=x_ap[:, 0:512]))
            OP("vector", lambda e, x_ap=x_ap, lp=lp: e.bn_stats(
                out=s_bnW[lp][:, 6:12], in_=x_ap[:, 512:1024]))
            t2_ = OP("vector", lambda e, lp=lp: e.bn_aggr(out=s_mvW[lp][:], in_=s_bnW[lp][:]),
                     deps=list(lnW_war[lp]))
            t3_ = OP("scalar", lambda e, lp=lp: e.activation(
                out=s_sdW[lp][:], in_=s_mvW[lp][:, 1:2], func=AF.Ln, bias=s_c[:, 1:2]),
                deps=[t2_, tok_c])
            t4_ = OP("scalar", lambda e, lp=lp: e.activation(
                out=s_rsdW[lp][:], in_=s_sdW[lp][:], func=AF.Exp, scale=s_c[:, 10:11]))
            lnW_war[lp] = [t4_]
            tn = OP("vector", lambda e, x_ap=x_ap, lp=lp: e.tensor_scalar(
                out=s_xlnW[:], in0=x_ap, scalar1=s_mvW[lp][:, 0:1], scalar2=s_rsdW[lp][:],
                op0=ALU.subtract, op1=ALU.mult), deps=[t4_] + list(xlnW_free[0]))
            ts_ = DMA("gpsimd", lambda e, tt=tt: e.dma_start(
                out=xlnscr[tt * 128:(tt + 1) * 128, :], in_=s_xlnW[:]), "xlns", deps=[tn])
            xlnW_free[0] = [ts_]
            xln_store.append(ts_)
            return tn
        two = DMA("gpsimd", lambda e: e.dma_start(
            out=s_wo[:], in_=wo_t.rearrange("(h p) d -> p h d", p=64)), "bmk",
            deps=list(barA) + list(wq_prev[0]))
        pW = [stBW.enter_context(nc.psum_tensor(f"pW{i}", [128, 512], F32)) for i in range(2)]
        pT = [stBW.enter_context(nc.psum_tensor(f"pTr{i}", [128, 1024], F32)) for i in range(2)]

        x2_store_toks = []
        wo_evac = {0: (), 1: ()}
        pT_free = {0: (), 1: ()}
        xq4_free = {0: (), 1: ()}
        x2t_free = {0: (), 1: ()}
        for qt in range(NPOS):
            qsl = slice(qt * QT, (qt + 1) * QT)
            ao_toks = []
            for dc in range(8):
                wp = dc % 2
                tk = None
                for hh in range(16):
                    d = ([div_last, two] + list(wo_evac[wp])) if hh == 0 else ()
                    tk = OP("tensor", lambda e, hh=hh, dc=dc, qsl=qsl, wp=wp: e.matmul(
                        pW[wp][:, :], s_wo[:, hh, dc * 128:(dc + 1) * 128], s_attn[:, hh, qsl],
                        start=(hh == 0), stop=(hh == 15)), deps=d)
                te = OP("scalar", lambda e, dc=dc, wp=wp: e.copy(out=s_ao[:, dc, :], in_=pW[wp][:, :]),
                        deps=[tk])
                wo_evac[wp] = [te]
                ao_toks.append(te)
            for sub in range(4):
                tp = sub % 2
                txq = DMA("sync", lambda e, qt=qt, sub=sub, tp=tp: e.dma_start(
                    out=s_xq4[tp][:],
                    in_=xq_in[qt * 512 + sub * 128:qt * 512 + (sub + 1) * 128, :]),
                    f"bxq{tp}", deps=list(xq4_free[tp]) + list(wq_prev[0]))
                tt_ = None
                for dc in range(8):
                    d = [ao_toks[dc], tok_id] + (list(pT_free[tp]) if dc == 0 else [])
                    tt_ = OP("tensor", lambda e, dc=dc, sub=sub, tp=tp: e.transpose(
                        out=pT[tp][:, dc * 128:(dc + 1) * 128],
                        in_=s_ao[:, dc, sub * 128:(sub + 1) * 128], identity=s_id[:]),
                        deps=d)
                ta = OP("vector", lambda e, tp=tp: e.tensor_add(
                    out=s_x2t[tp][:], in0=pT[tp][:, :], in1=s_xq4[tp][:]),
                    deps=[tt_, txq] + list(x2t_free[tp]))
                pT_free[tp] = [ta]
                xq4_free[tp] = [ta]
                ts = DMA("gpsimd", lambda e, qt=qt, sub=sub, tp=tp: e.dma_start(
                    out=x2scr[qt * 512 + sub * 128: qt * 512 + (sub + 1) * 128, :],
                    in_=s_x2t[tp][:]), "x2s", deps=[ta])
                tln = emit_lnW(s_x2t[tp][:], qt * 4 + sub, tp)
                x2t_free[tp] = [ts, tln]
                x2_store_toks.append(ts)

        barB = pr.barrier()
        stBW.close()
        stB.close()

        barC0 = pr.barrier()

        stC = ExitStack()
        s_wg = stC.enter_context(nc.sbuf_tensor("s_wg", [128, 8, FI], BF16, is_transpose_dest=True))
        s_wu = stC.enter_context(nc.sbuf_tensor("s_wu", [128, 8, FI], BF16, is_transpose_dest=True))
        s_wd = stC.enter_context(nc.sbuf_tensor("s_wd", [128, 22, D], BF16, is_transpose_dest=True))
        s_x2b = [stC.enter_context(nc.sbuf_tensor("s_x2b", [128, D], F32))] * 2
        s_xln = [stC.enter_context(nc.sbuf_tensor("s_xlnb", [128, D], F32))] * 2
        s_xq = stC.enter_context(nc.sbuf_tensor("s_xqC", [128, D], BF16))
        s_xqf = [stC.enter_context(nc.sbuf_tensor(f"s_xqf{i}", [128, 8, 128], BF16,
                                                  is_transpose_dest=True)) for i in range(2)]
        s_h = stC.enter_context(nc.sbuf_tensor("s_h", [128, FI], F32))
        s_hq = stC.enter_context(nc.sbuf_tensor("s_hq", [128, FI], BF16))
        s_hqT = [stC.enter_context(nc.sbuf_tensor(f"s_hqT{i}", [128, 22, 128], BF16,
                                                  is_transpose_dest=True)) for i in range(2)]
        s_sil = [stC.enter_context(nc.sbuf_tensor(f"s_sil{i}", [128, 512], F32)) for i in range(2)]
        s_outt = [stC.enter_context(nc.sbuf_tensor("s_outt", [128, D], F32))] * 2
        s_am2 = stC.enter_context(nc.sbuf_tensor("s_am2", [128, FI // 16], F32))
        s_sc2 = stC.enter_context(nc.sbuf_tensor("s_sc2", [128, FI // 16], F32))
        s_rs2 = stC.enter_context(nc.sbuf_tensor("s_rs2", [128, FI // 16], F32))
        s_au2 = stC.enter_context(nc.sbuf_tensor("s_au2", [128, FI], F32))
        s_L2 = stC.enter_context(nc.sbuf_tensor("s_L2", [128, FI], BF16))
        s_t12 = stC.enter_context(nc.sbuf_tensor("s_t12", [128, FI], BF16))
        pG = [stC.enter_context(nc.psum_tensor(f"pG{i}", [128, 512], F32)) for i in range(2)]
        pU = [stC.enter_context(nc.psum_tensor(f"pU{i}", [128, 512], F32)) for i in range(2)]
        pD = [stC.enter_context(nc.psum_tensor(f"pD{i}", [128, 512], F32)) for i in range(2)]
        tC = dict(am=s_am2, sc=s_sc2, rs=s_rs2, au=s_au2, L=s_L2, t1=s_t12)

        wtr = []
        wqdep = barC0 + [wq_store_toks[-1]]
        for fc in range(8):
            wtr.append(DMA("sync", lambda e, fc=fc: e.dma_start(
                out=s_wg[:, fc, :], in_=wgq_n[:, fc * 128:(fc + 1) * 128], transpose=True),
                "wtr", deps=wqdep if fc == 0 else ()))
            wtr.append(DMA("sync", lambda e, fc=fc: e.dma_start(
                out=s_wu[:, fc, :], in_=wuq_n[:, fc * 128:(fc + 1) * 128], transpose=True), "wtr"))
        for ic in range(22):
            wtr.append(DMA("sync", lambda e, ic=ic: e.dma_start(
                out=s_wd[:, ic, :], in_=wdq_n[:, ic * 128:(ic + 1) * 128], transpose=True), "wtr"))
        wtr_tok = wtr[-1]

        x2b_free = {0: barC0, 1: barC0}
        xln_free = {0: barC0, 1: barC0}
        xq_free = [()]
        xqscr_freeC = {0: (), 1: ()}
        hqscr_freeC = {0: (), 1: ()}
        xqf_free = {0: (), 1: ()}
        hq_free = [()]
        hqT_free = {0: (), 1: ()}
        sil_free = {0: (), 1: ()}
        outt_free = {0: (), 1: ()}
        gup_free = {0: (), 1: ()}
        pD_free = {0: (), 1: ()}
        out_toks = []
        st1 = {}
        st2 = {}

        def stage1(tt):
            pc_ = tt % 2
            txl = DMA("sync", lambda e, tt=tt, pc_=pc_: e.dma_start(
                out=s_xln[pc_][:], in_=xlnscr[tt * 128:(tt + 1) * 128, :]), "cl0",
                deps=list(xln_free[0]) + [xln_store[-1]])
            tq, tsgx = emit_quant(s_xln[pc_][:], s_xq[:], D, tC, [txl] + list(xq_free[0]))
            tqs = DMA("gpsimd", lambda e, pc_=pc_: e.dma_start(out=xqscr[pc_], in_=s_xq[:]),
                      f"xqs{pc_}", deps=[tq] + list(xqscr_freeC[pc_]))
            xq_free[0] = [tqs]
            tqt = DMA("sync", lambda e, pc_=pc_: e.dma_start(
                out=s_xqf[pc_][:], in_=xqscr[pc_], transpose=True), f"xqt{pc_}",
                deps=[tqs] + list(xqf_free[pc_]))
            xqscr_freeC[pc_] = [tqt]
            xln_free[0] = [tq, tsgx]
            st1[tt] = tqt

        def stage2(tt):
            pc_ = tt % 2
            tqt = st1.pop(tt)
            h_toks = []
            for ic in range(6):
                iw = 512 if ic < 5 else 256
                isl = slice(ic * 512, ic * 512 + iw)
                gp = ic % 2
                tg = None
                for fc in range(8):
                    d = ([tqt, wtr_tok] + list(gup_free[gp])) if fc == 0 else ()
                    tg = OP("tensor", lambda e, fc=fc, isl=isl, iw=iw, gp=gp, pc_=pc_: e.matmul(
                        pG[gp][:, 0:iw], s_xqf[pc_][:, fc, :], s_wg[:, fc, isl],
                        start=(fc == 0), stop=(fc == 7)), deps=d)
                tu = None
                for fc in range(8):
                    tu = OP("tensor", lambda e, fc=fc, isl=isl, iw=iw, gp=gp, pc_=pc_: e.matmul(
                        pU[gp][:, 0:iw], s_xqf[pc_][:, fc, :], s_wu[:, fc, isl],
                        start=(fc == 0), stop=(fc == 7)))
                tsl = OP("scalar", lambda e, iw=iw, gp=gp: e.activation(
                    out=s_sil[gp][:, 0:iw], in_=pG[gp][:, 0:iw], func=AF.Silu),
                    deps=[tg] + list(sil_free[gp]))
                th = OP("vector", lambda e, isl=isl, iw=iw, gp=gp: e.tensor_mul(
                    out=s_h[:, isl], in0=s_sil[gp][:, 0:iw], in1=pU[gp][:, 0:iw]),
                    deps=[tsl, tu] + (list(hq_free[0]) if ic == 0 else []))
                gup_free[gp] = [th]
                sil_free[gp] = [th]
                h_toks.append(th)
            xqf_free[pc_] = [h_toks[-1]]
            tqh, tsgh = emit_quant(s_h[:], s_hq[:], FI, tC, [h_toks[-1]])
            tqhs = DMA("gpsimd", lambda e, pc_=pc_: e.dma_start(out=hqscr[pc_], in_=s_hq[:]),
                       f"hqs{pc_}", deps=[tqh] + list(hqscr_freeC[pc_]))
            hq_free[0] = [tqhs, tsgh]
            tqht = DMA("sync", lambda e, pc_=pc_: e.dma_start(
                out=s_hqT[pc_][:], in_=hqscr[pc_], transpose=True), f"hqt{pc_}",
                deps=[tqhs] + list(hqT_free[pc_]))
            hqscr_freeC[pc_] = [tqht]
            st2[tt] = tqht

        def stage3(tt):
            pc_ = tt % 2
            tqht = st2.pop(tt)
            tx2 = DMA("sync", lambda e, tt=tt: e.dma_start(
                out=s_x2b[0][:], in_=x2scr[tt * 128:(tt + 1) * 128, :]), "cx0",
                deps=x2b_free[0])
            dtoks = []
            for dh in range(2):
                dsl = slice(dh * 512, (dh + 1) * 512)
                td = None
                for ic in range(22):
                    d = ([tqht] + list(pD_free[dh])) if ic == 0 else ()
                    td = OP("tensor", lambda e, ic=ic, dsl=dsl, dh=dh, pc_=pc_: e.matmul(
                        pD[dh][:, :], s_hqT[pc_][:, ic, :], s_wd[:, ic, dsl],
                        start=(ic == 0), stop=(ic == 21)), deps=d)
                ta2 = OP("vector", lambda e, dsl=dsl, dh=dh: e.tensor_add(
                    out=s_outt[0][:, dsl], in0=pD[dh][:, :], in1=s_x2b[0][:, dsl]),
                    deps=[td, tx2] + (list(outt_free[0]) if dh == 0 else []))
                pD_free[dh] = [ta2]
                dtoks.append(ta2)
            hqT_free[pc_] = [dtoks[-1]]
            x2b_free[0] = [dtoks[-1]]
            to = DMA("gpsimd", lambda e, tt=tt: e.dma_start(
                out=out[tt * 128:(tt + 1) * 128, :], in_=s_outt[0][:]), "outs",
                deps=dtoks)
            outt_free[0] = [to]
            out_toks.append(to)

        for it in range(18):
            if it < 16:
                stage1(it)
            if 1 <= it < 17:
                stage2(it - 1)
            if it >= 2:
                stage3(it - 2)

        OP("vector", lambda e: e.engine_nop(), deps=[out_toks[-1]])
        stC.close()
        pr.build(block)
    return nc



def _qrow_idx(half):
    tiles = TILES_BY_POS[half]
    return np.concatenate([np.arange(t * 512, (t + 1) * 512) for t in tiles])


def _rope_tables():
    j = np.arange(16, dtype=np.float64)
    inv = 10000.0 ** (-2.0 * j / 32.0)
    pos = np.arange(S, dtype=np.float64)
    ang = pos[None, :] * inv[:, None]
    return (np.cos(ang).astype(np.float32), np.sin(ang).astype(np.float32), inv)


def _masks_for(half):
    m = np.zeros((NPOS, 8, 128, QT), dtype=np.float32)
    q = np.arange(QT)
    k = np.arange(128)
    for p in range(NPOS):
        e = EXT[p]
        t = TILES_BY_POS[half][p]
        r = 4 * t + 4
        q0 = t * 512
        for j in range(8):
            kc = e - 8 + j
            if kc < r - 4:
                m[p, j] = 1.0
            elif kc < r:
                m[p, j] = ((q0 + q[None, :]) >= (kc * 128 + k[:, None])).astype(np.float32)
    return m


def _deinterleave_rows(w, dhr, nh):
    w3 = w.reshape(nh, dhr, -1)
    return (np.ascontiguousarray(w3[:, 0::2, :].reshape(nh * (dhr // 2), -1)),
            np.ascontiguousarray(w3[:, 1::2, :].reshape(nh * (dhr // 2), -1)))


def kernel(L0_attn_x, L0_attn_W_DQ, L0_attn_W_UQ, L0_attn_W_QR, L0_attn_W_DKV,
           L0_attn_W_UK, L0_attn_W_UV, L0_attn_W_KR, L0_attn_W_O,
           L0_ffn_w_gate, L0_ffn_w_up, L0_ffn_w_down):
    X = np.ascontiguousarray(np.asarray(L0_attn_x, dtype=np.float32))
    T = lambda a: np.ascontiguousarray(np.asarray(a, dtype=np.float32).T)
    wdq_t = T(L0_attn_W_DQ)
    wuq_t = T(L0_attn_W_UQ)
    wdkv_t = T(L0_attn_W_DKV)
    wuk_t = T(L0_attn_W_UK)
    wuv_t = T(L0_attn_W_UV)
    wo_t = T(L0_attn_W_O)
    qr1, qr2 = _deinterleave_rows(np.asarray(L0_attn_W_QR, np.float32), DHR, H)
    wqr1_t, wqr2_t = T(qr1), T(qr2)
    kr1, kr2 = _deinterleave_rows(np.asarray(L0_attn_W_KR, np.float32), DHR, 1)
    wkr1_t, wkr2_t = T(kr1), T(kr2)
    wg_n = np.ascontiguousarray(np.asarray(L0_ffn_w_gate, np.float32))
    wu_n = np.ascontiguousarray(np.asarray(L0_ffn_w_up, np.float32))
    wd_n = np.ascontiguousarray(np.asarray(L0_ffn_w_down, np.float32))

    cosk, sink, inv = _rope_tables()
    consts = np.zeros((128, 16), dtype=np.float32)
    consts[:, 0] = SC
    consts[:, 1] = EPS
    consts[:, 10] = -0.5
    consts[:, 11] = -1.0

    in_maps = []
    qidx = {}
    for c in range(8):
        b, half = c // 2, c % 2
        qidx[c] = _qrow_idx(half)
        pos = qidx[c].astype(np.float64)
        ang = pos[None, :] * np.tile(inv, 8)[:, None]
        in_maps.append(dict(
            x_full=X[b], xq_in=np.ascontiguousarray(X[b][qidx[c]]),
            wdq_t=wdq_t, wuq_t=wuq_t, wqr1_t=wqr1_t, wqr2_t=wqr2_t,
            wdkv_t=wdkv_t, wuk_t=wuk_t, wuv_t=wuv_t,
            wkr1_t=wkr1_t, wkr2_t=wkr2_t, wo_t=wo_t,
            wg_n=wg_n, wu_n=wu_n, wd_n=wd_n,
            cosk=cosk, sink=sink,
            c8q=np.cos(ang).astype(np.float32),
            s8q=np.sin(ang).astype(np.float32),
            masks_in=_masks_for(half), consts=consts,
        ))

    nc = build_program()
    res = run_bass_kernel_spmd(nc, in_maps, core_ids=list(range(8)))
    full = np.empty((B, S, D), dtype=np.float32)
    for c in range(8):
        full[c // 2][qidx[c]] = np.asarray(res.results[c]["out"], dtype=np.float32)
    if DEBUG:
        kernel.debug_results = res.results
    return full


# version 26
# speedup vs baseline: 1.0665x; 1.0665x over previous
import os
import numpy as np
import ml_dtypes
import concourse.bass as bass
import concourse.mybir as mybir
from concourse.bass_utils import run_bass_kernel_spmd
from contextlib import ExitStack

F32 = mybir.dt.float32
BF16 = mybir.dt.bfloat16
AF = mybir.ActivationFunctionType
ALU = mybir.AluOpType
AX = mybir.AxisListType

B, S, D = 4, 4096, 1024
H, DH, DHR = 16, 64, 32
DLAT = 512
FI = 2816
NQ = 2048
QT = 512
NPOS = 4
EXT = (8, 16, 24, 32)
TILES_BY_POS = {0: [0, 3, 4, 7], 1: [1, 2, 5, 6]}
DQK = 96
EPS = 1e-5

SC = float(np.float32(1.0) / np.sqrt(np.float32(96.0)))
CLIP_HI = float(np.exp(np.float64(np.float32(80.0) * np.float32(SC))))
CLIP_LO = float(np.exp(np.float64(np.float32(-80.0) * np.float32(SC))))

DEBUG = os.environ.get("MK_DEBUG", "0") == "1"


class Pipe:
    ENGS = ("sync", "scalar", "vector", "tensor", "gpsimd")

    def __init__(self, nc, stack):
        self.nc = nc
        self.stack = stack
        self.ops = {e: [] for e in self.ENGS}
        self.csem = {e: stack.enter_context(nc.semaphore(f"C{e[:2]}")) for e in self.ENGS}
        self.ccnt = {e: 0 for e in self.ENGS}
        self.dsems = {}

    def _dsem(self, name):
        if name not in self.dsems:
            self.dsems[name] = [self.stack.enter_context(self.nc.semaphore(f"D{name}")), 0]
        return self.dsems[name]

    force_serial = False

    def op(self, engine, fn, deps=()):
        if self.force_serial:
            deps = list(deps) + self.barrier()
        self.ccnt[engine] += 1
        self.ops[engine].append((fn, tuple(deps), self.csem[engine], 1))
        return (self.csem[engine], self.ccnt[engine], engine)

    def dma(self, engine, fn, stream, deps=()):
        if self.force_serial:
            deps = list(deps) + self.barrier()
        s = self._dsem(stream)
        s[1] += 16
        self.ops[engine].append((fn, tuple(deps), s[0], 16))
        return (s[0], s[1], None)

    def last(self, engine):
        return (self.csem[engine], self.ccnt[engine], engine)

    def barrier(self):
        toks = []
        for e in self.ENGS:
            if self.ccnt[e]:
                toks.append((self.csem[e], self.ccnt[e], e))
        for name, (sem, cnt) in self.dsems.items():
            if cnt:
                toks.append((sem, cnt, None))
        return toks

    self_wait = False
    self_wait_lag = 1
    self_wait_engines = ("scalar", "vector", "gpsimd")

    def build(self, block):
        def body_for(name):
            def run(eng):
                waited = {}
                own = 0
                own_sem = self.csem[name]
                sw = self.self_wait and name in self.self_wait_engines
                lag = self.self_wait_lag
                for (fn, deps, sem, inc) in self.ops[name]:
                    if sw and own - lag + 1 > 0 and sem is own_sem:
                        eng.wait_ge(own_sem, own - lag + 1)
                    for (dsem, dval, deng) in deps:
                        if deng == name:
                            continue
                        k = id(dsem)
                        if waited.get(k, 0) >= dval:
                            continue
                        eng.wait_ge(dsem, dval)
                        waited[k] = dval
                    fn(eng).then_inc(sem, inc)
                    if sem is own_sem:
                        own += inc
            return run
        for name, meth in (("sync", block.sync), ("scalar", block.scalar),
                           ("vector", block.vector), ("tensor", block.tensor),
                           ("gpsimd", block.gpsimd)):
            if self.ops[name]:
                meth(body_for(name))


def build_program():
    nc = bass.Bass("TRN2", target_bir_lowering=False)
    DI = nc.dram_tensor

    x_full = DI("x_full", [S, D], F32, kind="ExternalInput").ap()
    xq_in = DI("xq_in", [NQ, D], F32, kind="ExternalInput").ap()
    wdq_t = DI("wdq_t", [D, DLAT], F32, kind="ExternalInput").ap()
    wuq_t = DI("wuq_t", [DLAT, D], F32, kind="ExternalInput").ap()
    wqr1_t = DI("wqr1_t", [DLAT, 256], F32, kind="ExternalInput").ap()
    wqr2_t = DI("wqr2_t", [DLAT, 256], F32, kind="ExternalInput").ap()
    wdkv_t = DI("wdkv_t", [D, DLAT], F32, kind="ExternalInput").ap()
    wuk_t = DI("wuk_t", [DLAT, D], F32, kind="ExternalInput").ap()
    wuv_t = DI("wuv_t", [DLAT, D], F32, kind="ExternalInput").ap()
    wkr1_t = DI("wkr1_t", [D, 16], F32, kind="ExternalInput").ap()
    wkr2_t = DI("wkr2_t", [D, 16], F32, kind="ExternalInput").ap()
    wo_t = DI("wo_t", [D, D], F32, kind="ExternalInput").ap()
    wg_n = DI("wg_n", [FI, D], F32, kind="ExternalInput").ap()
    wu_n = DI("wu_n", [FI, D], F32, kind="ExternalInput").ap()
    wd_n = DI("wd_n", [D, FI], F32, kind="ExternalInput").ap()
    cosk = DI("cosk", [16, S], F32, kind="ExternalInput").ap()
    sink = DI("sink", [16, S], F32, kind="ExternalInput").ap()
    c8q = DI("c8q", [128, NQ], F32, kind="ExternalInput").ap()
    s8q = DI("s8q", [128, NQ], F32, kind="ExternalInput").ap()
    masks_in = DI("masks_in", [NPOS, 8, 128, QT], F32, kind="ExternalInput").ap()
    consts = DI("consts", [128, 16], F32, kind="ExternalInput").ap()

    out = DI("out", [NQ, D], F32, kind="ExternalOutput").ap()

    skind = "ExternalOutput" if DEBUG else "Internal"
    kscr = DI("kscr", [D, S], BF16, kind=skind).ap()
    krscr = DI("krscr", [32, S], BF16, kind=skind).ap()
    vscr = DI("vscr", [D, S], BF16, kind="Internal").ap()
    vtok = DI("vtok", [S, D], BF16, kind=skind).ap()
    qscr = DI("qscr", [H, DQK, NQ], BF16, kind=skind).ap()
    x2scr = DI("x2scr", [NQ, D], F32, kind=skind).ap()
    xlnscr = DI("xlnscr", [NQ, D], F32, kind="Internal").ap()
    xnscr = DI("xnscr", [2, 512, D], BF16, kind="Internal").ap()
    xqscr = DI("xqscr", [2, 128, D], BF16, kind="Internal").ap()
    hqscr = DI("hqscr", [2, 128, FI], BF16, kind="Internal").ap()
    rcpscr = DI("rcpscr", [1, QT], F32, kind="Internal").ap()
    wgq_n = DI("wgq_n", [FI, D], BF16, kind=skind).ap()
    wuq_n = DI("wuq_n", [FI, D], BF16, kind="Internal").ap()
    wdq_n = DI("wdq_n", [D, FI], BF16, kind="Internal").ap()

    stP = ExitStack()
    with stP:
        s_c = stP.enter_context(nc.sbuf_tensor("s_c", [128, 16], F32))
        s_id = stP.enter_context(nc.sbuf_tensor("s_id", [128, 128], F32))
        block = stP.enter_context(nc.Block())
        pr = Pipe(nc, stP)
        pr.force_serial = os.environ.get("MK_SER", "") == "all"
        pr.self_wait = os.environ.get("MK_SE", "1") == "1"
        pr.self_wait_lag = int(os.environ.get("MK_SWD", "1"))
        pr.self_wait_engines = tuple(os.environ.get("MK_SWE", "vector,gpsimd").split(","))
        OP, DMA = pr.op, pr.dma

        tok_c = DMA("sync", lambda e: e.dma_start(out=s_c[:], in_=consts), "cst")
        OP("gpsimd", lambda e: e.memset(s_id[:], 0.0))
        tok_id = OP("gpsimd", lambda e: e.affine_select(
            out=s_id[:], in_=s_id[:], compare_op=ALU.not_equal, fill=1.0,
            base=0, pattern=[[-1, 128]], channel_multiplier=1))

        stA = ExitStack()
        s_x = [stA.enter_context(nc.sbuf_tensor(f"s_x{i}", [128, 4, D], F32)) for i in range(2)]
        s_xn = stA.enter_context(nc.sbuf_tensor("s_xn", [128, 4, D], BF16))
        s_xnf = [stA.enter_context(nc.sbuf_tensor(f"s_xnf{i}", [128, 8, 512], BF16,
                                                  is_transpose_dest=True)) for i in range(2)]
        s_wdq = stA.enter_context(nc.sbuf_tensor("s_wdq", [128, 8, DLAT], BF16))
        s_wuq = stA.enter_context(nc.sbuf_tensor("s_wuq", [128, 4, D], BF16))
        s_wqr1 = stA.enter_context(nc.sbuf_tensor("s_wqr1", [128, 4, 256], BF16))
        s_wqr2 = stA.enter_context(nc.sbuf_tensor("s_wqr2", [128, 4, 256], BF16))
        s_wdkv = stA.enter_context(nc.sbuf_tensor("s_wdkv", [128, 8, DLAT], BF16))
        s_wuk = stA.enter_context(nc.sbuf_tensor("s_wuk", [128, 4, D], BF16))
        s_wuv = stA.enter_context(nc.sbuf_tensor("s_wuv", [128, 4, D], BF16))
        s_wkr1 = stA.enter_context(nc.sbuf_tensor("s_wkr1", [128, 8, 16], BF16))
        s_wkr2 = stA.enter_context(nc.sbuf_tensor("s_wkr2", [128, 8, 16], BF16))
        s_ckv = stA.enter_context(nc.sbuf_tensor("s_ckv", [128, 4, 512], BF16))
        s_tmp = [stA.enter_context(nc.sbuf_tensor(f"s_tmpA{i}", [128, 512], BF16)) for i in range(2)]
        s_cs = stA.enter_context(nc.sbuf_tensor("s_cs", [48, S], F32))
        s_c8q = stA.enter_context(nc.sbuf_tensor("s_c8q", [128, NQ], F32))
        s_s8q = stA.enter_context(nc.sbuf_tensor("s_s8q", [128, NQ], F32))
        s_bn6 = [stA.enter_context(nc.sbuf_tensor(f"s_bn6{i}", [128, 12], F32)) for i in range(2)]
        s_mv = [stA.enter_context(nc.sbuf_tensor(f"s_mv{i}", [128, 2], F32)) for i in range(2)]
        s_sd = [stA.enter_context(nc.sbuf_tensor(f"s_sd{i}", [128, 1], F32)) for i in range(2)]
        s_rsd = [stA.enter_context(nc.sbuf_tensor(f"s_rsd{i}", [128, 1], F32)) for i in range(2)]
        s_ta = stA.enter_context(nc.sbuf_tensor("s_ta", [128, 512], F32))
        s_tb = stA.enter_context(nc.sbuf_tensor("s_tb", [128, 512], F32))
        s_ra = stA.enter_context(nc.sbuf_tensor("s_ra", [128, 512], BF16))
        s_rb = stA.enter_context(nc.sbuf_tensor("s_rb", [128, 512], BF16))
        pM = [stA.enter_context(nc.psum_tensor(f"pM{i}", [128, 512], F32)) for i in range(2)]
        pQ1 = stA.enter_context(nc.psum_tensor("pQ1", [128, 512], F32))
        pQ2 = stA.enter_context(nc.psum_tensor("pQ2", [128, 512], F32))


        def wload(dst, src, pat, **kw):
            return DMA("gpsimd", lambda e: e.dma_start(out=dst[:], in_=src.rearrange(pat, **kw)), "wl")
        wload_all = [
            wload(s_wdq, wdq_t, "(fc p) l -> p fc l", p=128),
            wload(s_wuq, wuq_t, "(lc p) o -> p lc o", p=128),
            wload(s_wqr1, wqr1_t, "(lc p) o -> p lc o", p=128),
            wload(s_wqr2, wqr2_t, "(lc p) o -> p lc o", p=128),
            wload(s_wdkv, wdkv_t, "(fc p) l -> p fc l", p=128),
            wload(s_wuk, wuk_t, "(lc p) o -> p lc o", p=128),
            wload(s_wuv, wuv_t, "(lc p) o -> p lc o", p=128),
            wload(s_wkr1, wkr1_t, "(fc p) o -> p fc o", p=128),
            wload(s_wkr2, wkr2_t, "(fc p) o -> p fc o", p=128),
            DMA("sync", lambda e: e.dma_start(out=s_cs[0:16, :], in_=cosk), "tbl"),
            DMA("sync", lambda e: e.dma_start(out=s_cs[32:48, :], in_=sink), "tbl"),
            DMA("sync", lambda e: e.dma_start(out=s_c8q[:], in_=c8q), "tbl"),
            DMA("sync", lambda e: e.dma_start(out=s_s8q[:], in_=s8q), "tbl"),
        ]

        def emit_quant(x_ap, q_ap, Fw, t, dep_in):
            g = emit_quant_gen(x_ap, q_ap, Fw, t, dep_in)
            r = None
            while True:
                try:
                    r = g.send(None) or r
                except StopIteration:
                    break
            return r

        def emit_quant_gen(x_ap, q_ap, Fw, t, dep_in):
            nb = Fw // 16
            P_ = x_ap.shape[0] if hasattr(x_ap, "shape") else 128
            P_ = x_ap.shape[0]
            x3 = x_ap.rearrange("p (b w) -> p b w", w=16)
            t_amx = OP("vector", lambda e: e.tensor_reduce(
                out=t["am"][0:P_, 0:nb], in_=x3, axis=AX.X, op=ALU.max,
                apply_absolute_value=True), deps=dep_in)
            yield
            OP("vector", lambda e: e.tensor_scalar(
                out=t["sc"][0:P_, 0:nb], in0=t["am"][0:P_, 0:nb], scalar1=1.0 / 12.0,
                scalar2=5e-13, op0=ALU.mult, op1=ALU.max), deps=[t_amx])
            t_r2 = OP("vector", lambda e: e.reciprocal(
                out=t["rs"][0:P_, 0:nb], in_=t["sc"][0:P_, 0:nb]))
            yield
            rsb = t["rs"][0:P_, 0:nb].rearrange("p (b one) -> p b one", one=1).broadcast_to([P_, nb, 16])
            scb = t["sc"][0:P_, 0:nb].rearrange("p (b one) -> p b one", one=1).broadcast_to([P_, nb, 16])
            au = t["au"][0:P_, 0:Fw]
            au3 = au.rearrange("p (b w) -> p b w", w=16)
            t_abs = OP("scalar", lambda e: e.activation(out=au, in_=x_ap, func=AF.Abs),
                       deps=dep_in)
            yield
            OP("vector", lambda e: e.tensor_tensor(out=au3, in0=au3, in1=rsb, op=ALU.mult),
               deps=[t_abs, t_r2])
            yield
            L = t["L"][0:P_, 0:Fw]
            t1 = t["t1"][0:P_, 0:Fw]

            def cmp(dst, thr):
                OP("vector", lambda e, dst=dst, thr=thr: e.tensor_scalar(
                    out=dst, in0=au, scalar1=float(thr), scalar2=None, op0=ALU.is_gt))
            cmp(L, 0.5)
            yield
            for thr in (1.5, 2.5, 3.5):
                cmp(t1, thr)
                yield
                OP("vector", lambda e: e.tensor_add(out=L, in0=L, in1=t1))
                yield
            for thr, w in ((5.0, 2.0), (7.0, 2.0), (10.0, 4.0)):
                cmp(t1, thr)
                yield
                OP("vector", lambda e, w=w: e.scalar_tensor_tensor(
                    out=L, in0=t1, scalar=w, in1=L, op0=ALU.mult, op1=ALU.add))
                yield
            t_lastL = pr.last("vector")
            tsg = OP("scalar", lambda e: e.activation(out=t1, in_=x_ap, func=AF.Sign),
                     deps=[t_lastL])
            yield
            OP("vector", lambda e: e.tensor_mul(out=L, in0=L, in1=t1), deps=[tsg])
            yield
            tq = OP("vector", lambda e: e.tensor_tensor(
                out=q_ap.rearrange("p (b w) -> p b w", w=16),
                in0=L.rearrange("p (b w) -> p b w", w=16), in1=scb, op=ALU.mult))
            yield (tq, tsg)

        ln_exp_war = {0: (), 1: ()}

        def emit_ln_sub(x_ap, xn_ap, dep_in, lp):
            OP("vector", lambda e, x_ap=x_ap, lp=lp: e.bn_stats(
                out=s_bn6[lp][:, 0:6], in_=x_ap[:, 0:512]), deps=list(dep_in))
            OP("vector", lambda e, x_ap=x_ap, lp=lp: e.bn_stats(
                out=s_bn6[lp][:, 6:12], in_=x_ap[:, 512:1024]))
            t2_ = OP("vector", lambda e, lp=lp: e.bn_aggr(out=s_mv[lp][:], in_=s_bn6[lp][:]),
                     deps=list(ln_exp_war[lp]))
            t3_ = OP("scalar", lambda e, lp=lp: e.activation(
                out=s_sd[lp][:], in_=s_mv[lp][:, 1:2], func=AF.Ln, bias=s_c[:, 1:2]),
                deps=[t2_, tok_c])
            t4_ = OP("scalar", lambda e, lp=lp: e.activation(
                out=s_rsd[lp][:], in_=s_sd[lp][:], func=AF.Exp, scale=s_c[:, 10:11]))
            ln_exp_war[lp] = [t4_]
            tn = OP("vector", lambda e, x_ap=x_ap, xn_ap=xn_ap, lp=lp: e.tensor_scalar(
                out=xn_ap, in0=x_ap, scalar1=s_mv[lp][:, 0:1], scalar2=s_rsd[lp][:],
                op0=ALU.subtract, op1=ALU.mult), deps=[t4_])
            return tn

        xload_free = {0: (), 1: ()}
        xnf_free = {0: (), 1: ()}
        xnscr_free = {0: (), 1: ()}
        xn_free = [()]
        evac_tok = {0: None, 1: None}
        tmp_free = {0: (), 1: ()}
        kv_store_toks = []
        q_store_toks = []
        rope_war = [()]
        mm_g = [0]

        def mm_group(lhs_list, rhs_list, extra_deps):
            g = mm_g[0]
            mm_g[0] += 1
            p = pM[g % 2]
            dep_war = [evac_tok[g % 2]] if evac_tok[g % 2] else []
            n = len(lhs_list)
            tok = None
            for i, (lf, rf) in enumerate(zip(lhs_list, rhs_list)):
                d = (list(extra_deps) + dep_war) if i == 0 else ()
                tok = OP("tensor", lambda e, lf=lf, rf=rf, i=i, n=n, p=p: e.matmul(
                    p[:, :], lf, rf, start=(i == 0), stop=(i == n - 1)), deps=d)
            return g, p, tok

        for ti in range(12):
            par = ti % 2
            is_q = ti >= 8
            src = x_full[(ti * 512):(ti + 1) * 512, :] if not is_q else \
                xq_in[(ti - 8) * 512:(ti - 7) * 512, :]
            t0 = (ti * 512) if not is_q else (ti - 8) * 512
            xl = DMA("sync", lambda e, src=src, par=par: e.dma_start(
                out=s_x[par][:], in_=src.rearrange("(s p) d -> p s d", p=128)),
                f"xl{par}", deps=xload_free[par])
            last_ln = None
            for sub in range(4):
                d = [xl] + (list(xn_free[0]) if sub == 0 else [])
                last_ln = emit_ln_sub(s_x[par][:, sub, :], s_xn[:, sub, :], d, sub % 2)
            xload_free[par] = [last_ln]
            xst = DMA("gpsimd", lambda e, par=par: e.dma_start(
                out=xnscr[par].rearrange("(s p) d -> p s d", p=128), in_=s_xn[:]),
                f"xns{par}", deps=[last_ln] + list(xnscr_free[par]))
            xn_free[0] = [xst]
            xt = DMA("sync", lambda e, par=par: e.dma_start(
                out=s_xnf[par][:], in_=xnscr[par], transpose=True),
                f"xnt{par}", deps=[xst] + list(xnf_free[par]))
            xnscr_free[par] = [xt]
            deps0 = [xt] + (wload_all if ti == 0 else [])
            wdown = s_wdkv if not is_q else s_wdq
            ck_toks = []
            for oc in range(4):
                g, p, tk = mm_group(
                    [wdown[:, fc, oc * 128:(oc + 1) * 128] for fc in range(8)],
                    [s_xnf[par][:, fc, :] for fc in range(8)], deps0)
                te = OP("scalar", lambda e, p=p, oc=oc: e.copy(out=s_ckv[:, oc, :], in_=p[:, :]),
                        deps=[tk])
                evac_tok[g % 2] = te
                ck_toks.append(te)
            ck_dep = [ck_toks[-1]]
            if not is_q:
                tkr = None
                for half_i, wkr in ((0, s_wkr1), (1, s_wkr2)):
                    rows = slice(0, 16) if half_i == 0 else slice(32, 48)
                    for fc in range(8):
                        d = (deps0 + list(rope_war[0])) if (half_i == 0 and fc == 0) else ()
                        tkr = OP("tensor", lambda e, fc=fc, rows=rows, wkr=wkr, par=par: e.matmul(
                            pQ1[rows, :], wkr[:, fc, :], s_xnf[par][:, fc, :],
                            start=(fc == 0), stop=(fc == 7)), deps=d)
                ck_ = s_cs[0:16, t0:t0 + 512]
                sk_ = s_cs[32:48, t0:t0 + 512]
                OP("vector", lambda e, ck_=ck_: e.tensor_mul(out=s_ta[0:16, :], in0=pQ1[0:16, :], in1=ck_),
                   deps=[tkr] + list(rope_war[0]))
                OP("vector", lambda e, sk_=sk_: e.tensor_mul(out=s_tb[0:16, :], in0=pQ1[32:48, :], in1=sk_))
                OP("vector", lambda e: e.tensor_sub(out=s_ra[0:16, :], in0=s_ta[0:16, :], in1=s_tb[0:16, :]))
                OP("vector", lambda e, sk_=sk_: e.tensor_mul(out=s_ta[0:16, :], in0=pQ1[0:16, :], in1=sk_))
                OP("vector", lambda e, ck_=ck_: e.tensor_mul(out=s_tb[0:16, :], in0=pQ1[32:48, :], in1=ck_))
                trb = OP("vector", lambda e: e.tensor_add(out=s_rb[0:16, :], in0=s_ta[0:16, :], in1=s_tb[0:16, :]))
                d1 = DMA("gpsimd", lambda e, t0=t0: e.dma_start(
                    out=krscr[0:16, t0:t0 + 512], in_=s_ra[0:16, :]), "krst", deps=[trb])
                d2 = DMA("gpsimd", lambda e, t0=t0: e.dma_start(
                    out=krscr[16:32, t0:t0 + 512], in_=s_rb[0:16, :]), "krst", deps=[trb])
                rope_war[0] = [d1, d2]
                kv_store_toks += [d1, d2]
                for (wsb, scr, stream) in ((s_wuk, kscr, "kst"), (s_wuv, vscr, "vst")):
                    for oc8 in range(8):
                        g, p, tk = mm_group(
                            [wsb[:, lc, oc8 * 128:(oc8 + 1) * 128] for lc in range(4)],
                            [s_ckv[:, lc, :] for lc in range(4)], ck_dep)
                        te = OP("scalar", lambda e, p=p, g=g: e.copy(out=s_tmp[g % 2][:], in_=p[:, :]),
                                deps=[tk] + list(tmp_free[g % 2]))
                        evac_tok[g % 2] = te
                        td = DMA("gpsimd", lambda e, scr=scr, oc8=oc8, t0=t0, g=g: e.dma_start(
                            out=scr[oc8 * 128:(oc8 + 1) * 128, t0:t0 + 512], in_=s_tmp[g % 2][:]),
                            stream, deps=[te])
                        tmp_free[g % 2] = [td]
                        kv_store_toks.append(td)
            else:
                for oc2 in range(2):
                    tq1 = None
                    for pnum, wq_, pp in ((0, s_wqr1, pQ1), (1, s_wqr2, pQ2)):
                        for lc in range(4):
                            d = (ck_dep + list(rope_war[0])) if (pnum == 0 and lc == 0) else ()
                            tq1 = OP("tensor", lambda e, oc2=oc2, lc=lc, wq_=wq_, pp=pp: e.matmul(
                                pp[:, :], wq_[:, lc, oc2 * 128:(oc2 + 1) * 128], s_ckv[:, lc, :],
                                start=(lc == 0), stop=(lc == 3)), deps=d)
                    cq_ = s_c8q[:, t0:t0 + 512]
                    sq_ = s_s8q[:, t0:t0 + 512]
                    OP("vector", lambda e, cq_=cq_: e.tensor_mul(out=s_ta[:], in0=pQ1[:, :], in1=cq_),
                       deps=[tq1] + list(rope_war[0]))
                    OP("vector", lambda e, sq_=sq_: e.tensor_mul(out=s_tb[:], in0=pQ2[:, :], in1=sq_))
                    OP("vector", lambda e: e.tensor_sub(out=s_ra[:], in0=s_ta[:], in1=s_tb[:]))
                    OP("vector", lambda e, sq_=sq_: e.tensor_mul(out=s_ta[:], in0=pQ1[:, :], in1=sq_))
                    OP("vector", lambda e, cq_=cq_: e.tensor_mul(out=s_tb[:], in0=pQ2[:, :], in1=cq_))
                    trb = OP("vector", lambda e: e.tensor_add(out=s_rb[:], in0=s_ta[:], in1=s_tb[:]))
                    d1 = DMA("gpsimd", lambda e, oc2=oc2, t0=t0: e.dma_start(
                        out=qscr[oc2 * 8:(oc2 + 1) * 8, 64:80, t0:t0 + 512], in_=s_ra[:]),
                        "qst", deps=[trb])
                    d2 = DMA("gpsimd", lambda e, oc2=oc2, t0=t0: e.dma_start(
                        out=qscr[oc2 * 8:(oc2 + 1) * 8, 80:96, t0:t0 + 512], in_=s_rb[:]),
                        "qst", deps=[trb])
                    rope_war[0] = [d1, d2]
                    q_store_toks += [d1, d2]
                for oc8 in range(8):
                    g, p, tk = mm_group(
                        [s_wuq[:, lc, oc8 * 128:(oc8 + 1) * 128] for lc in range(4)],
                        [s_ckv[:, lc, :] for lc in range(4)], ck_dep)
                    te = OP("scalar", lambda e, p=p, g=g: e.copy(out=s_tmp[g % 2][:], in_=p[:, :]),
                            deps=[tk] + list(tmp_free[g % 2]))
                    evac_tok[g % 2] = te
                    td = DMA("gpsimd", lambda e, oc8=oc8, t0=t0, g=g: e.dma_start(
                        out=qscr[2 * oc8:2 * oc8 + 2, 0:64, t0:t0 + 512], in_=s_tmp[g % 2][:]),
                        "qst", deps=[te])
                    tmp_free[g % 2] = [td]
                    q_store_toks.append(td)
            xnf_free[par] = [pr.last("tensor")]
            
        barA = pr.barrier()
        stA.close()

        stV = ExitStack()
        s_vtmp = [stV.enter_context(nc.sbuf_tensor(f"s_vtmp{i}", [128, D], BF16,
                                                   is_transpose_dest=True)) for i in range(2)]
        vt_prev = {0: barA, 1: barA}
        vtok_toks = []
        for kc in range(32):
            pv_ = kc % 2
            tl = DMA("sync", lambda e, kc=kc, pv_=pv_: e.dma_start(
                out=s_vtmp[pv_][:], in_=vscr[:, kc * 128:(kc + 1) * 128], transpose=True),
                f"vtt{pv_}", deps=vt_prev[pv_])
            ts = DMA("gpsimd", lambda e, kc=kc, pv_=pv_: e.dma_start(
                out=vtok[kc * 128:(kc + 1) * 128, :], in_=s_vtmp[pv_][:]), "vts", deps=[tl])
            vt_prev[pv_] = [ts]
            vtok_toks.append(ts)
        stV.close()

        stB = ExitStack()
        s_k = [stB.enter_context(nc.sbuf_tensor(f"s_k{i}", [96, S], BF16)) for i in range(2)]
        s_q = [stB.enter_context(nc.sbuf_tensor(f"s_q{i}", [96, NQ], BF16)) for i in range(2)]
        s_v1 = [stB.enter_context(nc.sbuf_tensor(f"s_v1{i}", [128, 32, 65], BF16)) for i in range(2)]
        s_p = [stB.enter_context(nc.sbuf_tensor(f"s_p{i}", [128, 1024], BF16)) for i in range(3)]
        s_attn = stB.enter_context(nc.sbuf_tensor("s_attn", [64, H, NQ], BF16))
        s_masks = stB.enter_context(nc.sbuf_tensor("s_masks", [128, NPOS, 8, QT], BF16))
        s_rcp = stB.enter_context(nc.sbuf_tensor("s_rcp", [1, QT], F32))
        s_bc = stB.enter_context(nc.sbuf_tensor("s_bc", [64, QT], F32))
        s_ao = stB.enter_context(nc.sbuf_tensor("s_ao", [128, 8, QT], F32))

        stBq = ExitStack()
        WQF = 2816
        s_wf = stBq.enter_context(nc.sbuf_tensor("s_wf", [128, WQF], F32))
        s_am = stBq.enter_context(nc.sbuf_tensor("s_am", [128, WQF // 16], F32))
        s_sc = stBq.enter_context(nc.sbuf_tensor("s_sc", [128, WQF // 16], F32))
        s_rs = stBq.enter_context(nc.sbuf_tensor("s_rs", [128, WQF // 16], F32))
        s_au = stBq.enter_context(nc.sbuf_tensor("s_au", [128, WQF], F32))
        s_L = stBq.enter_context(nc.sbuf_tensor("s_L", [128, WQF], BF16))
        s_t1 = stBq.enter_context(nc.sbuf_tensor("s_t1", [128, WQF], BF16))
        s_qq = stBq.enter_context(nc.sbuf_tensor("s_qq", [128, WQF], BF16))
        tQ = dict(am=s_am, sc=s_sc, rs=s_rs, au=s_au, L=s_L, t1=s_t1)

        wq_batches = []
        for (src_, dst_, nch, Fw) in ((wg_n, wgq_n, FI // 128, D),
                                      (wu_n, wuq_n, FI // 128, D),
                                      (wd_n, wdq_n, D // 128, FI)):
            per = WQF // Fw
            ch = 0
            while ch < nch:
                n_ = min(per, nch - ch)
                wq_batches.append((src_, dst_, [128 * (ch + i) for i in range(n_)], Fw))
                ch += n_
        wq_prev = [()]
        wq_store_toks = []

        def wq_driver():
            for (src_, dst_, r0s, Fw) in wq_batches:
                tl = None
                for i, r0 in enumerate(r0s):
                    tl = DMA("sync", lambda e, src_=src_, r0=r0, i=i, Fw=Fw: e.dma_start(
                        out=s_wf[:, i * Fw:(i + 1) * Fw], in_=src_[r0:r0 + 128, :]), "wf",
                        deps=wq_prev[0] if i == 0 else ())
                Ft = Fw * len(r0s)
                res = None
                gen = emit_quant_gen(s_wf[:, 0:Ft], s_qq[:, 0:Ft], Ft, tQ, [tl])
                while True:
                    try:
                        r = gen.send(None)
                        if r is not None:
                            res = r
                        yield
                    except StopIteration:
                        break
                tq, tsg = res
                ts = None
                for i, r0 in enumerate(r0s):
                    ts = DMA("gpsimd", lambda e, dst_=dst_, r0=r0, i=i, Fw=Fw: e.dma_start(
                        out=dst_[r0:r0 + 128, :], in_=s_qq[:, i * Fw:(i + 1) * Fw]), "wqs",
                        deps=[tq] if i == 0 else ())
                wq_prev[0] = [tq, tsg, ts]
                wq_store_toks.append(ts)
                yield

        wq_it = [wq_driver()]

        def emit_wq_some(n):
            for _ in range(n):
                try:
                    next(wq_it[0])
                except StopIteration:
                    return

        stBp = ExitStack()
        pAT = [stBp.enter_context(nc.psum_tensor(f"pAT{i}", [128, 512], F32)) for i in range(2)]
        pS = [stBp.enter_context(nc.psum_tensor(f"pS{i}", [128, 1024], F32)) for i in range(3)]

        tmask = DMA("gpsimd", lambda e: e.dma_start(
            out=s_masks[:], in_=masks_in.rearrange("a b p q -> p a b q")), "bmk", deps=barA)
        tv1 = [OP("gpsimd", lambda e, i=i: e.memset(s_v1[i][:], 1.0), deps=barA) for i in range(2)]

        kv_bar = [kv_store_toks[-1], q_store_toks[-1], vtok_toks[-1]]
        head_free = {0: [tv1[0]] + kv_bar, 1: [tv1[1]] + kv_bar}
        head_toks = {}

        def load_head(h, par, deps):
            DMA("sync", lambda e, h=h, par=par: e.dma_start(
                out=s_k[par][0:64, :], in_=kscr[h * 64:(h + 1) * 64, :]), f"bk{par}", deps=deps)
            DMA("sync", lambda e, par=par: e.dma_start(out=s_k[par][64:96, :], in_=krscr), f"bk{par}")
            DMA("sync", lambda e, h=h, par=par: e.dma_start(out=s_q[par][:], in_=qscr[h]), f"bk{par}")
            t4_ = DMA("sync", lambda e, h=h, par=par: e.dma_start(
                out=s_v1[par][:, :, 0:64],
                in_=vtok[:, h * 64:(h + 1) * 64].rearrange("(kc p) o -> p kc o", p=128)), f"bk{par}")
            return [t4_]

        head_toks[0] = load_head(0, 0, head_free[0])

        pS_free = {0: (), 1: (), 2: ()}
        pP_free = {0: (), 1: (), 2: ()}
        pAT_free = {0: (), 1: ()}
        bc_free = [()]
        bc2_free = [()]
        bc3_free = [()]
        at_i = [0]

        for h in range(16):
            par = h % 2
            if h + 1 < 16:
                head_toks[h + 1] = load_head(h + 1, (h + 1) % 2, head_free[(h + 1) % 2])
            kload = head_toks[h]
            last_S = None
            last_PV = None
            groups = []
            for pos in range(NPOS):
                ai = at_i[0] % 2
                at_i[0] += 1
                ng = EXT[pos] // 2
                for g in range(ng):
                    groups.append((pos, g, ai))
            LOOK = 2
            gi = [0]
            tm_of = {}

            def emit_S(idx):
                pos, g, ai = groups[idx]
                ext = EXT[pos]
                qsl = slice(pos * QT, (pos + 1) * QT)
                gp = idx % 3
                st_ = None
                for kk in range(2):
                    kc = 2 * g + kk
                    d = (list(pS_free[gp]) + list(kload)) if kk == 0 else ()
                    st_ = OP("tensor", lambda e, kc=kc, gp=gp, kk=kk, par=par, qsl=qsl: e.matmul(
                        pS[gp][:, kk * 512:(kk + 1) * 512],
                        s_k[par][:, kc * 128:(kc + 1) * 128], s_q[par][:, qsl],
                        start=True, stop=True), deps=d)
                texp = OP("scalar", lambda e, gp=gp: e.activation(
                    out=s_p[gp][:], in_=pS[gp][:, :], func=AF.Exp, scale=s_c[:, 0:1]),
                    deps=[st_, tok_c] + list(pP_free[gp]))
                pS_free[gp] = [texp]
                tm = OP("vector", lambda e, gp=gp: e.tensor_scalar(
                    out=s_p[gp][:], in0=s_p[gp][:], scalar1=CLIP_HI, scalar2=CLIP_LO,
                    op0=ALU.min, op1=ALU.max), deps=[texp])
                j0 = 2 * g - (ext - 8)
                if j0 >= 0:
                    tm = OP("vector", lambda e, gp=gp, pos=pos, j0=j0: e.tensor_mul(
                        out=s_p[gp][:], in0=s_p[gp][:],
                        in1=s_masks[:, pos, j0:j0 + 2, :].rearrange("p a q -> p (a q)")),
                        deps=[tmask])
                tm_of[idx] = tm
                return st_

            def emit_PV(idx):
                pos, g, ai = groups[idx]
                ext = EXT[pos]
                gp = idx % 3
                tm = tm_of.pop(idx)
                pv_tok = None
                for kk in range(2):
                    kc = 2 * g + kk
                    d = [tm] if kk == 0 else ()
                    if kc == 0:
                        d = [tm] + list(pAT_free[ai])
                    pv_tok = OP("tensor", lambda e, kc=kc, gp=gp, kk=kk, par=par, ai=ai, ext=ext: e.matmul(
                        pAT[ai][0:65, :], s_v1[par][:, kc, :],
                        s_p[gp][:, kk * 512:(kk + 1) * 512],
                        start=(kc == 0), stop=(kc == ext - 1)), deps=d)
                pP_free[gp] = [pv_tok]
                if 2 * g + 1 == ext - 1:
                    emit_divide(pos, ai, pv_tok)
                return pv_tok

            def emit_divide(pos, ai, pv_tok):
                qsl = slice(pos * QT, (pos + 1) * QT)
                trc0 = OP("scalar", lambda e, ai=ai: e.activation(
                    out=s_rcp[:], in_=pAT[ai][64:65, :], func=AF.Ln),
                    deps=[pv_tok] + list(bc_free[0]))
                trc = OP("scalar", lambda e: e.activation(
                    out=s_rcp[:], in_=s_rcp[:], func=AF.Exp, scale=s_c[0:1, 11:12]),
                    deps=[tok_c])
                tst = DMA("gpsimd", lambda e: e.dma_start(out=rcpscr, in_=s_rcp[:]),
                          "rcs", deps=[trc] + list(bc2_free[0]))
                bc_free[0] = [tst]
                tld = DMA("gpsimd", lambda e: e.dma_start(
                    out=s_bc[:], in_=rcpscr.broadcast_to([64, QT])), "rcl",
                    deps=[tst] + list(bc3_free[0]))
                bc2_free[0] = [tld]
                tdv = OP("vector", lambda e, ai=ai, h=h, qsl=qsl: e.tensor_mul(
                    out=s_attn[:, h, qsl], in0=pAT[ai][0:64, :], in1=s_bc[:]), deps=[tld])
                pAT_free[ai] = [tdv]
                bc3_free[0] = [tdv]

            n_g = len(groups)
            for idx in range(n_g + LOOK):
                if idx < n_g:
                    last_S = emit_S(idx)
                if idx >= LOOK:
                    last_PV = emit_PV(idx - LOOK)
                emit_wq_some(1)
            head_free[par] = [last_S, last_PV]
            if h == 15:
                emit_wq_some(9999)

        div_last = pr.last("vector")
        stBp.close()
        stBq.close()
        stBW = ExitStack()
        s_xq4 = [stBW.enter_context(nc.sbuf_tensor(f"s_xq4{i}", [128, D], F32)) for i in range(2)]
        s_x2t = [stBW.enter_context(nc.sbuf_tensor(f"s_x2t{i}", [128, D], F32)) for i in range(2)]
        s_xlnW = stBW.enter_context(nc.sbuf_tensor("s_xlnW", [128, D], F32))
        s_bnW = [stBW.enter_context(nc.sbuf_tensor(f"s_bnW{i}", [128, 12], F32)) for i in range(2)]
        s_mvW = [stBW.enter_context(nc.sbuf_tensor(f"s_mvW{i}", [128, 2], F32)) for i in range(2)]
        s_sdW = [stBW.enter_context(nc.sbuf_tensor(f"s_sdW{i}", [128, 1], F32)) for i in range(2)]
        s_rsdW = [stBW.enter_context(nc.sbuf_tensor(f"s_rsdW{i}", [128, 1], F32)) for i in range(2)]
        s_wo = stBW.enter_context(nc.sbuf_tensor("s_wo", [64, H, D], BF16))
        lnW_war = {0: (), 1: ()}
        xlnW_free = [()]
        xln_store = []

        def emit_lnW(x_ap, tt, lp):
            OP("vector", lambda e, x_ap=x_ap, lp=lp: e.bn_stats(
                out=s_bnW[lp][:, 0:6], in_=x_ap[:, 0:512]))
            OP("vector", lambda e, x_ap=x_ap, lp=lp: e.bn_stats(
                out=s_bnW[lp][:, 6:12], in_=x_ap[:, 512:1024]))
            t2_ = OP("vector", lambda e, lp=lp: e.bn_aggr(out=s_mvW[lp][:], in_=s_bnW[lp][:]),
                     deps=list(lnW_war[lp]))
            t3_ = OP("scalar", lambda e, lp=lp: e.activation(
                out=s_sdW[lp][:], in_=s_mvW[lp][:, 1:2], func=AF.Ln, bias=s_c[:, 1:2]),
                deps=[t2_, tok_c])
            t4_ = OP("scalar", lambda e, lp=lp: e.activation(
                out=s_rsdW[lp][:], in_=s_sdW[lp][:], func=AF.Exp, scale=s_c[:, 10:11]))
            lnW_war[lp] = [t4_]
            tn = OP("vector", lambda e, x_ap=x_ap, lp=lp: e.tensor_scalar(
                out=s_xlnW[:], in0=x_ap, scalar1=s_mvW[lp][:, 0:1], scalar2=s_rsdW[lp][:],
                op0=ALU.subtract, op1=ALU.mult), deps=[t4_] + list(xlnW_free[0]))
            ts_ = DMA("gpsimd", lambda e, tt=tt: e.dma_start(
                out=xlnscr[tt * 128:(tt + 1) * 128, :], in_=s_xlnW[:]), "xlns", deps=[tn])
            xlnW_free[0] = [ts_]
            xln_store.append(ts_)
            return tn
        two = DMA("gpsimd", lambda e: e.dma_start(
            out=s_wo[:], in_=wo_t.rearrange("(h p) d -> p h d", p=64)), "bmk",
            deps=list(barA) + list(wq_prev[0]))
        pW = [stBW.enter_context(nc.psum_tensor(f"pW{i}", [128, 512], F32)) for i in range(2)]
        pT = [stBW.enter_context(nc.psum_tensor(f"pTr{i}", [128, 1024], F32)) for i in range(2)]

        x2_store_toks = []
        wo_evac = {0: (), 1: ()}
        pT_free = {0: (), 1: ()}
        xq4_free = {0: (), 1: ()}
        x2t_free = {0: (), 1: ()}
        for qt in range(NPOS):
            qsl = slice(qt * QT, (qt + 1) * QT)
            ao_toks = []
            for dc in range(8):
                wp = dc % 2
                tk = None
                for hh in range(16):
                    d = ([div_last, two] + list(wo_evac[wp])) if hh == 0 else ()
                    tk = OP("tensor", lambda e, hh=hh, dc=dc, qsl=qsl, wp=wp: e.matmul(
                        pW[wp][:, :], s_wo[:, hh, dc * 128:(dc + 1) * 128], s_attn[:, hh, qsl],
                        start=(hh == 0), stop=(hh == 15)), deps=d)
                te = OP("scalar", lambda e, dc=dc, wp=wp: e.copy(out=s_ao[:, dc, :], in_=pW[wp][:, :]),
                        deps=[tk])
                wo_evac[wp] = [te]
                ao_toks.append(te)
            for sub in range(4):
                tp = sub % 2
                txq = DMA("sync", lambda e, qt=qt, sub=sub, tp=tp: e.dma_start(
                    out=s_xq4[tp][:],
                    in_=xq_in[qt * 512 + sub * 128:qt * 512 + (sub + 1) * 128, :]),
                    f"bxq{tp}", deps=list(xq4_free[tp]) + list(wq_prev[0]))
                tt_ = None
                for dc in range(8):
                    d = [ao_toks[dc], tok_id] + (list(pT_free[tp]) if dc == 0 else [])
                    tt_ = OP("tensor", lambda e, dc=dc, sub=sub, tp=tp: e.transpose(
                        out=pT[tp][:, dc * 128:(dc + 1) * 128],
                        in_=s_ao[:, dc, sub * 128:(sub + 1) * 128], identity=s_id[:]),
                        deps=d)
                ta = OP("vector", lambda e, tp=tp: e.tensor_add(
                    out=s_x2t[tp][:], in0=pT[tp][:, :], in1=s_xq4[tp][:]),
                    deps=[tt_, txq] + list(x2t_free[tp]))
                pT_free[tp] = [ta]
                xq4_free[tp] = [ta]
                ts = DMA("gpsimd", lambda e, qt=qt, sub=sub, tp=tp: e.dma_start(
                    out=x2scr[qt * 512 + sub * 128: qt * 512 + (sub + 1) * 128, :],
                    in_=s_x2t[tp][:]), "x2s", deps=[ta])
                tln = emit_lnW(s_x2t[tp][:], qt * 4 + sub, tp)
                x2t_free[tp] = [ts, tln]
                x2_store_toks.append(ts)

        barB = pr.barrier()
        stBW.close()
        stB.close()

        barC0 = pr.barrier()

        stC = ExitStack()
        s_wg = stC.enter_context(nc.sbuf_tensor("s_wg", [128, 8, FI], BF16, is_transpose_dest=True))
        s_wu = stC.enter_context(nc.sbuf_tensor("s_wu", [128, 8, FI], BF16, is_transpose_dest=True))
        s_wd = stC.enter_context(nc.sbuf_tensor("s_wd", [128, 22, D], BF16, is_transpose_dest=True))
        s_x2b = [stC.enter_context(nc.sbuf_tensor("s_x2b", [128, D], F32))] * 2
        s_xln = [stC.enter_context(nc.sbuf_tensor("s_xlnb", [128, D], F32))] * 2
        s_xq = stC.enter_context(nc.sbuf_tensor("s_xqC", [128, D], BF16))
        s_xqf = [stC.enter_context(nc.sbuf_tensor(f"s_xqf{i}", [128, 8, 128], BF16,
                                                  is_transpose_dest=True)) for i in range(2)]
        s_h = stC.enter_context(nc.sbuf_tensor("s_h", [128, FI], F32))
        s_hq = stC.enter_context(nc.sbuf_tensor("s_hq", [128, FI], BF16))
        s_hqT = [stC.enter_context(nc.sbuf_tensor(f"s_hqT{i}", [128, 22, 128], BF16,
                                                  is_transpose_dest=True)) for i in range(2)]
        s_sil = [stC.enter_context(nc.sbuf_tensor(f"s_sil{i}", [128, 512], F32)) for i in range(2)]
        s_outt = [stC.enter_context(nc.sbuf_tensor("s_outt", [128, D], F32))] * 2
        s_am2 = stC.enter_context(nc.sbuf_tensor("s_am2", [128, FI // 16], F32))
        s_sc2 = stC.enter_context(nc.sbuf_tensor("s_sc2", [128, FI // 16], F32))
        s_rs2 = stC.enter_context(nc.sbuf_tensor("s_rs2", [128, FI // 16], F32))
        s_au2 = stC.enter_context(nc.sbuf_tensor("s_au2", [128, FI], F32))
        s_L2 = stC.enter_context(nc.sbuf_tensor("s_L2", [128, FI], BF16))
        s_t12 = stC.enter_context(nc.sbuf_tensor("s_t12", [128, FI], BF16))
        pG = [stC.enter_context(nc.psum_tensor(f"pG{i}", [128, 512], F32)) for i in range(2)]
        pU = [stC.enter_context(nc.psum_tensor(f"pU{i}", [128, 512], F32)) for i in range(2)]
        pD = [stC.enter_context(nc.psum_tensor(f"pD{i}", [128, 512], F32)) for i in range(2)]
        tC = dict(am=s_am2, sc=s_sc2, rs=s_rs2, au=s_au2, L=s_L2, t1=s_t12)

        wtr = []
        wqdep = barC0 + [wq_store_toks[-1]]
        for fc in range(8):
            wtr.append(DMA("sync", lambda e, fc=fc: e.dma_start(
                out=s_wg[:, fc, :], in_=wgq_n[:, fc * 128:(fc + 1) * 128], transpose=True),
                "wtr", deps=wqdep if fc == 0 else ()))
            wtr.append(DMA("sync", lambda e, fc=fc: e.dma_start(
                out=s_wu[:, fc, :], in_=wuq_n[:, fc * 128:(fc + 1) * 128], transpose=True), "wtr"))
        for ic in range(22):
            wtr.append(DMA("sync", lambda e, ic=ic: e.dma_start(
                out=s_wd[:, ic, :], in_=wdq_n[:, ic * 128:(ic + 1) * 128], transpose=True), "wtr"))
        wtr_tok = wtr[-1]

        x2b_free = {0: barC0, 1: barC0}
        xln_free = {0: barC0, 1: barC0}
        xq_free = [()]
        xqscr_freeC = {0: (), 1: ()}
        hqscr_freeC = {0: (), 1: ()}
        xqf_free = {0: (), 1: ()}
        hq_free = [()]
        hqT_free = {0: (), 1: ()}
        sil_free = {0: (), 1: ()}
        outt_free = {0: (), 1: ()}
        gup_free = {0: (), 1: ()}
        pD_free = {0: (), 1: ()}
        out_toks = []
        st1 = {}
        st2 = {}

        def stage1(tt):
            pc_ = tt % 2
            txl = DMA("sync", lambda e, tt=tt, pc_=pc_: e.dma_start(
                out=s_xln[pc_][:], in_=xlnscr[tt * 128:(tt + 1) * 128, :]), "cl0",
                deps=list(xln_free[0]) + [xln_store[-1]])
            tq, tsgx = emit_quant(s_xln[pc_][:], s_xq[:], D, tC, [txl] + list(xq_free[0]))
            tqs = DMA("gpsimd", lambda e, pc_=pc_: e.dma_start(out=xqscr[pc_], in_=s_xq[:]),
                      f"xqs{pc_}", deps=[tq] + list(xqscr_freeC[pc_]))
            xq_free[0] = [tqs]
            tqt = DMA("sync", lambda e, pc_=pc_: e.dma_start(
                out=s_xqf[pc_][:], in_=xqscr[pc_], transpose=True), f"xqt{pc_}",
                deps=[tqs] + list(xqf_free[pc_]))
            xqscr_freeC[pc_] = [tqt]
            xln_free[0] = [tq, tsgx]
            st1[tt] = tqt

        def stage2(tt):
            pc_ = tt % 2
            tqt = st1.pop(tt)
            h_toks = []
            for ic in range(6):
                iw = 512 if ic < 5 else 256
                isl = slice(ic * 512, ic * 512 + iw)
                gp = ic % 2
                tg = None
                for fc in range(8):
                    d = ([tqt, wtr_tok] + list(gup_free[gp])) if fc == 0 else ()
                    tg = OP("tensor", lambda e, fc=fc, isl=isl, iw=iw, gp=gp, pc_=pc_: e.matmul(
                        pG[gp][:, 0:iw], s_xqf[pc_][:, fc, :], s_wg[:, fc, isl],
                        start=(fc == 0), stop=(fc == 7)), deps=d)
                tu = None
                for fc in range(8):
                    tu = OP("tensor", lambda e, fc=fc, isl=isl, iw=iw, gp=gp, pc_=pc_: e.matmul(
                        pU[gp][:, 0:iw], s_xqf[pc_][:, fc, :], s_wu[:, fc, isl],
                        start=(fc == 0), stop=(fc == 7)))
                tsl = OP("scalar", lambda e, iw=iw, gp=gp: e.activation(
                    out=s_sil[gp][:, 0:iw], in_=pG[gp][:, 0:iw], func=AF.Silu),
                    deps=[tg] + list(sil_free[gp]))
                th = OP("vector", lambda e, isl=isl, iw=iw, gp=gp: e.tensor_mul(
                    out=s_h[:, isl], in0=s_sil[gp][:, 0:iw], in1=pU[gp][:, 0:iw]),
                    deps=[tsl, tu] + (list(hq_free[0]) if ic == 0 else []))
                gup_free[gp] = [th]
                sil_free[gp] = [th]
                h_toks.append(th)
            xqf_free[pc_] = [h_toks[-1]]
            tqh, tsgh = emit_quant(s_h[:], s_hq[:], FI, tC, [h_toks[-1]])
            tqhs = DMA("gpsimd", lambda e, pc_=pc_: e.dma_start(out=hqscr[pc_], in_=s_hq[:]),
                       f"hqs{pc_}", deps=[tqh] + list(hqscr_freeC[pc_]))
            hq_free[0] = [tqhs, tsgh]
            tqht = DMA("sync", lambda e, pc_=pc_: e.dma_start(
                out=s_hqT[pc_][:], in_=hqscr[pc_], transpose=True), f"hqt{pc_}",
                deps=[tqhs] + list(hqT_free[pc_]))
            hqscr_freeC[pc_] = [tqht]
            st2[tt] = tqht

        def stage3(tt):
            pc_ = tt % 2
            tqht = st2.pop(tt)
            tx2 = DMA("sync", lambda e, tt=tt: e.dma_start(
                out=s_x2b[0][:], in_=x2scr[tt * 128:(tt + 1) * 128, :]), "cx0",
                deps=x2b_free[0])
            dtoks = []
            for dh in range(2):
                dsl = slice(dh * 512, (dh + 1) * 512)
                td = None
                for ic in range(22):
                    d = ([tqht] + list(pD_free[dh])) if ic == 0 else ()
                    td = OP("tensor", lambda e, ic=ic, dsl=dsl, dh=dh, pc_=pc_: e.matmul(
                        pD[dh][:, :], s_hqT[pc_][:, ic, :], s_wd[:, ic, dsl],
                        start=(ic == 0), stop=(ic == 21)), deps=d)
                ta2 = OP("vector", lambda e, dsl=dsl, dh=dh: e.tensor_add(
                    out=s_outt[0][:, dsl], in0=pD[dh][:, :], in1=s_x2b[0][:, dsl]),
                    deps=[td, tx2] + (list(outt_free[0]) if dh == 0 else []))
                pD_free[dh] = [ta2]
                dtoks.append(ta2)
            hqT_free[pc_] = [dtoks[-1]]
            x2b_free[0] = [dtoks[-1]]
            to = DMA("gpsimd", lambda e, tt=tt: e.dma_start(
                out=out[tt * 128:(tt + 1) * 128, :], in_=s_outt[0][:]), "outs",
                deps=dtoks)
            outt_free[0] = [to]
            out_toks.append(to)

        for it in range(18):
            if it < 16:
                stage1(it)
            if 1 <= it < 17:
                stage2(it - 1)
            if it >= 2:
                stage3(it - 2)

        OP("vector", lambda e: e.engine_nop(), deps=[out_toks[-1]])
        stC.close()
        pr.build(block)
    return nc



def _qrow_idx(half):
    tiles = TILES_BY_POS[half]
    return np.concatenate([np.arange(t * 512, (t + 1) * 512) for t in tiles])


def _rope_tables():
    j = np.arange(16, dtype=np.float64)
    inv = 10000.0 ** (-2.0 * j / 32.0)
    pos = np.arange(S, dtype=np.float64)
    ang = pos[None, :] * inv[:, None]
    return (np.cos(ang).astype(np.float32), np.sin(ang).astype(np.float32), inv)


def _masks_for(half):
    m = np.zeros((NPOS, 8, 128, QT), dtype=np.float32)
    q = np.arange(QT)
    k = np.arange(128)
    for p in range(NPOS):
        e = EXT[p]
        t = TILES_BY_POS[half][p]
        r = 4 * t + 4
        q0 = t * 512
        for j in range(8):
            kc = e - 8 + j
            if kc < r - 4:
                m[p, j] = 1.0
            elif kc < r:
                m[p, j] = ((q0 + q[None, :]) >= (kc * 128 + k[:, None])).astype(np.float32)
    return m


def _deinterleave_rows(w, dhr, nh):
    w3 = w.reshape(nh, dhr, -1)
    return (np.ascontiguousarray(w3[:, 0::2, :].reshape(nh * (dhr // 2), -1)),
            np.ascontiguousarray(w3[:, 1::2, :].reshape(nh * (dhr // 2), -1)))


def kernel(L0_attn_x, L0_attn_W_DQ, L0_attn_W_UQ, L0_attn_W_QR, L0_attn_W_DKV,
           L0_attn_W_UK, L0_attn_W_UV, L0_attn_W_KR, L0_attn_W_O,
           L0_ffn_w_gate, L0_ffn_w_up, L0_ffn_w_down):
    X = np.ascontiguousarray(np.asarray(L0_attn_x, dtype=np.float32))
    T = lambda a: np.ascontiguousarray(np.asarray(a, dtype=np.float32).T)
    wdq_t = T(L0_attn_W_DQ)
    wuq_t = T(L0_attn_W_UQ)
    wdkv_t = T(L0_attn_W_DKV)
    wuk_t = T(L0_attn_W_UK)
    wuv_t = T(L0_attn_W_UV)
    wo_t = T(L0_attn_W_O)
    qr1, qr2 = _deinterleave_rows(np.asarray(L0_attn_W_QR, np.float32), DHR, H)
    wqr1_t, wqr2_t = T(qr1), T(qr2)
    kr1, kr2 = _deinterleave_rows(np.asarray(L0_attn_W_KR, np.float32), DHR, 1)
    wkr1_t, wkr2_t = T(kr1), T(kr2)
    wg_n = np.ascontiguousarray(np.asarray(L0_ffn_w_gate, np.float32))
    wu_n = np.ascontiguousarray(np.asarray(L0_ffn_w_up, np.float32))
    wd_n = np.ascontiguousarray(np.asarray(L0_ffn_w_down, np.float32))

    cosk, sink, inv = _rope_tables()
    consts = np.zeros((128, 16), dtype=np.float32)
    consts[:, 0] = SC
    consts[:, 1] = EPS
    consts[:, 10] = -0.5
    consts[:, 11] = -1.0

    in_maps = []
    qidx = {}
    for c in range(8):
        b, half = c // 2, c % 2
        qidx[c] = _qrow_idx(half)
        pos = qidx[c].astype(np.float64)
        ang = pos[None, :] * np.tile(inv, 8)[:, None]
        in_maps.append(dict(
            x_full=X[b], xq_in=np.ascontiguousarray(X[b][qidx[c]]),
            wdq_t=wdq_t, wuq_t=wuq_t, wqr1_t=wqr1_t, wqr2_t=wqr2_t,
            wdkv_t=wdkv_t, wuk_t=wuk_t, wuv_t=wuv_t,
            wkr1_t=wkr1_t, wkr2_t=wkr2_t, wo_t=wo_t,
            wg_n=wg_n, wu_n=wu_n, wd_n=wd_n,
            cosk=cosk, sink=sink,
            c8q=np.cos(ang).astype(np.float32),
            s8q=np.sin(ang).astype(np.float32),
            masks_in=_masks_for(half), consts=consts,
        ))

    nc = build_program()
    res = run_bass_kernel_spmd(nc, in_maps, core_ids=list(range(8)))
    full = np.empty((B, S, D), dtype=np.float32)
    for c in range(8):
        full[c // 2][qidx[c]] = np.asarray(res.results[c]["out"], dtype=np.float32)
    if DEBUG:
        kernel.debug_results = res.results
    return full


# version 27
# speedup vs baseline: 1.0953x; 1.0270x over previous
import os
import numpy as np
import ml_dtypes
import concourse.bass as bass
import concourse.mybir as mybir
from concourse.bass_utils import run_bass_kernel_spmd
from contextlib import ExitStack

F32 = mybir.dt.float32
BF16 = mybir.dt.bfloat16
AF = mybir.ActivationFunctionType
ALU = mybir.AluOpType
AX = mybir.AxisListType

B, S, D = 4, 4096, 1024
H, DH, DHR = 16, 64, 32
DLAT = 512
FI = 2816
NQ = 2048
QT = 512
NPOS = 4
EXT = (8, 16, 24, 32)
TILES_BY_POS = {0: [0, 3, 4, 7], 1: [1, 2, 5, 6]}
DQK = 96
EPS = 1e-5

SC = float(np.float32(1.0) / np.sqrt(np.float32(96.0)))
CLIP_HI = float(np.exp(np.float64(np.float32(80.0) * np.float32(SC))))
CLIP_LO = float(np.exp(np.float64(np.float32(-80.0) * np.float32(SC))))

DEBUG = os.environ.get("MK_DEBUG", "0") == "1"


class Pipe:
    ENGS = ("sync", "scalar", "vector", "tensor", "gpsimd")

    def __init__(self, nc, stack):
        self.nc = nc
        self.stack = stack
        self.ops = {e: [] for e in self.ENGS}
        self.csem = {e: stack.enter_context(nc.semaphore(f"C{e[:2]}")) for e in self.ENGS}
        self.ccnt = {e: 0 for e in self.ENGS}
        self.dsems = {}

    def _dsem(self, name):
        if name not in self.dsems:
            self.dsems[name] = [self.stack.enter_context(self.nc.semaphore(f"D{name}")), 0]
        return self.dsems[name]

    force_serial = False

    def op(self, engine, fn, deps=()):
        if self.force_serial:
            deps = list(deps) + self.barrier()
        self.ccnt[engine] += 1
        self.ops[engine].append((fn, tuple(deps), self.csem[engine], 1))
        return (self.csem[engine], self.ccnt[engine], engine)

    def dma(self, engine, fn, stream, deps=()):
        if self.force_serial:
            deps = list(deps) + self.barrier()
        s = self._dsem(stream)
        s[1] += 16
        self.ops[engine].append((fn, tuple(deps), s[0], 16))
        return (s[0], s[1], None)

    def last(self, engine):
        return (self.csem[engine], self.ccnt[engine], engine)

    def barrier(self):
        toks = []
        for e in self.ENGS:
            if self.ccnt[e]:
                toks.append((self.csem[e], self.ccnt[e], e))
        for name, (sem, cnt) in self.dsems.items():
            if cnt:
                toks.append((sem, cnt, None))
        return toks

    self_wait = False
    self_wait_lag = 1
    self_wait_engines = ("scalar", "vector", "gpsimd")

    def build(self, block):
        def body_for(name):
            def run(eng):
                waited = {}
                own = 0
                own_sem = self.csem[name]
                sw = self.self_wait and name in self.self_wait_engines
                lag = self.self_wait_lag
                for (fn, deps, sem, inc) in self.ops[name]:
                    if sw and own - lag + 1 > 0 and sem is own_sem:
                        eng.wait_ge(own_sem, own - lag + 1)
                    for (dsem, dval, deng) in deps:
                        if deng == name:
                            continue
                        k = id(dsem)
                        if waited.get(k, 0) >= dval:
                            continue
                        eng.wait_ge(dsem, dval)
                        waited[k] = dval
                    fn(eng).then_inc(sem, inc)
                    if sem is own_sem:
                        own += inc
            return run
        for name, meth in (("sync", block.sync), ("scalar", block.scalar),
                           ("vector", block.vector), ("tensor", block.tensor),
                           ("gpsimd", block.gpsimd)):
            if self.ops[name]:
                meth(body_for(name))


def build_program():
    nc = bass.Bass("TRN2", target_bir_lowering=False)
    DI = nc.dram_tensor

    x_full = DI("x_full", [S, D], F32, kind="ExternalInput").ap()
    xq_in = DI("xq_in", [NQ, D], F32, kind="ExternalInput").ap()
    wdq_t = DI("wdq_t", [D, DLAT], F32, kind="ExternalInput").ap()
    wuq_t = DI("wuq_t", [DLAT, D], F32, kind="ExternalInput").ap()
    wqr1_t = DI("wqr1_t", [DLAT, 256], F32, kind="ExternalInput").ap()
    wqr2_t = DI("wqr2_t", [DLAT, 256], F32, kind="ExternalInput").ap()
    wdkv_t = DI("wdkv_t", [D, DLAT], F32, kind="ExternalInput").ap()
    wuk_t = DI("wuk_t", [DLAT, D], F32, kind="ExternalInput").ap()
    wuv_t = DI("wuv_t", [DLAT, D], F32, kind="ExternalInput").ap()
    wkr1_t = DI("wkr1_t", [D, 16], F32, kind="ExternalInput").ap()
    wkr2_t = DI("wkr2_t", [D, 16], F32, kind="ExternalInput").ap()
    wo_t = DI("wo_t", [D, D], F32, kind="ExternalInput").ap()
    wg_n = DI("wg_n", [FI, D], F32, kind="ExternalInput").ap()
    wu_n = DI("wu_n", [FI, D], F32, kind="ExternalInput").ap()
    wd_n = DI("wd_n", [D, FI], F32, kind="ExternalInput").ap()
    cosk = DI("cosk", [16, S], F32, kind="ExternalInput").ap()
    sink = DI("sink", [16, S], F32, kind="ExternalInput").ap()
    c8q = DI("c8q", [128, NQ], F32, kind="ExternalInput").ap()
    s8q = DI("s8q", [128, NQ], F32, kind="ExternalInput").ap()
    masks_in = DI("masks_in", [NPOS, 8, 128, QT], F32, kind="ExternalInput").ap()
    consts = DI("consts", [128, 16], F32, kind="ExternalInput").ap()

    out = DI("out", [NQ, D], F32, kind="ExternalOutput").ap()

    skind = "ExternalOutput" if DEBUG else "Internal"
    kscr = DI("kscr", [D, S], BF16, kind=skind).ap()
    krscr = DI("krscr", [32, S], BF16, kind=skind).ap()
    vscr = DI("vscr", [D, S], BF16, kind="Internal").ap()
    vtok = DI("vtok", [S, D], BF16, kind=skind).ap()
    qscr = DI("qscr", [H, DQK, NQ], BF16, kind=skind).ap()
    x2scr = DI("x2scr", [NQ, D], F32, kind=skind).ap()
    xlnscr = DI("xlnscr", [NQ, D], F32, kind="Internal").ap()
    xnscr = DI("xnscr", [2, 512, D], BF16, kind="Internal").ap()
    xqscr = DI("xqscr", [2, 128, D], BF16, kind="Internal").ap()
    hqscr = DI("hqscr", [2, 128, FI], BF16, kind="Internal").ap()
    rcpscr = DI("rcpscr", [1, QT], F32, kind="Internal").ap()
    wgq_n = DI("wgq_n", [FI, D], BF16, kind=skind).ap()
    wuq_n = DI("wuq_n", [FI, D], BF16, kind="Internal").ap()
    wdq_n = DI("wdq_n", [D, FI], BF16, kind="Internal").ap()

    stP = ExitStack()
    with stP:
        s_c = stP.enter_context(nc.sbuf_tensor("s_c", [128, 16], F32))
        s_id = stP.enter_context(nc.sbuf_tensor("s_id", [128, 128], F32))
        block = stP.enter_context(nc.Block())
        pr = Pipe(nc, stP)
        pr.force_serial = os.environ.get("MK_SER", "") == "all"
        pr.self_wait = os.environ.get("MK_SE", "1") == "1"
        pr.self_wait_lag = int(os.environ.get("MK_SWD", "1"))
        pr.self_wait_engines = tuple(os.environ.get("MK_SWE", "vector,gpsimd").split(","))
        OP, DMA = pr.op, pr.dma

        tok_c = DMA("sync", lambda e: e.dma_start(out=s_c[:], in_=consts), "cst")
        OP("gpsimd", lambda e: e.memset(s_id[:], 0.0))
        tok_id = OP("gpsimd", lambda e: e.affine_select(
            out=s_id[:], in_=s_id[:], compare_op=ALU.not_equal, fill=1.0,
            base=0, pattern=[[-1, 128]], channel_multiplier=1))

        stA = ExitStack()
        s_x = [stA.enter_context(nc.sbuf_tensor(f"s_x{i}", [128, 4, D], F32)) for i in range(2)]
        s_xn = stA.enter_context(nc.sbuf_tensor("s_xn", [128, 4, D], BF16))
        s_xnf = [stA.enter_context(nc.sbuf_tensor(f"s_xnf{i}", [128, 8, 512], BF16,
                                                  is_transpose_dest=True)) for i in range(2)]
        s_wdq = stA.enter_context(nc.sbuf_tensor("s_wdq", [128, 8, DLAT], BF16))
        s_wuq = stA.enter_context(nc.sbuf_tensor("s_wuq", [128, 4, D], BF16))
        s_wqr1 = stA.enter_context(nc.sbuf_tensor("s_wqr1", [128, 4, 256], BF16))
        s_wqr2 = stA.enter_context(nc.sbuf_tensor("s_wqr2", [128, 4, 256], BF16))
        s_wdkv = stA.enter_context(nc.sbuf_tensor("s_wdkv", [128, 8, DLAT], BF16))
        s_wuk = stA.enter_context(nc.sbuf_tensor("s_wuk", [128, 4, D], BF16))
        s_wuv = stA.enter_context(nc.sbuf_tensor("s_wuv", [128, 4, D], BF16))
        s_wkr1 = stA.enter_context(nc.sbuf_tensor("s_wkr1", [128, 8, 16], BF16))
        s_wkr2 = stA.enter_context(nc.sbuf_tensor("s_wkr2", [128, 8, 16], BF16))
        s_ckv = stA.enter_context(nc.sbuf_tensor("s_ckv", [128, 4, 512], BF16))
        s_tmp = [stA.enter_context(nc.sbuf_tensor(f"s_tmpA{i}", [128, 512], BF16)) for i in range(2)]
        s_cs = stA.enter_context(nc.sbuf_tensor("s_cs", [48, S], F32))
        s_c8q = stA.enter_context(nc.sbuf_tensor("s_c8q", [128, NQ], F32))
        s_s8q = stA.enter_context(nc.sbuf_tensor("s_s8q", [128, NQ], F32))
        s_bn6 = [stA.enter_context(nc.sbuf_tensor(f"s_bn6{i}", [128, 12], F32)) for i in range(2)]
        s_mv = [stA.enter_context(nc.sbuf_tensor(f"s_mv{i}", [128, 2], F32)) for i in range(2)]
        s_sd = [stA.enter_context(nc.sbuf_tensor(f"s_sd{i}", [128, 1], F32)) for i in range(2)]
        s_rsd = [stA.enter_context(nc.sbuf_tensor(f"s_rsd{i}", [128, 1], F32)) for i in range(2)]
        s_ta = stA.enter_context(nc.sbuf_tensor("s_ta", [128, 512], F32))
        s_tb = stA.enter_context(nc.sbuf_tensor("s_tb", [128, 512], F32))
        s_ra = stA.enter_context(nc.sbuf_tensor("s_ra", [128, 512], BF16))
        s_rb = stA.enter_context(nc.sbuf_tensor("s_rb", [128, 512], BF16))
        pM = [stA.enter_context(nc.psum_tensor(f"pM{i}", [128, 512], F32)) for i in range(2)]
        pQ1 = stA.enter_context(nc.psum_tensor("pQ1", [128, 512], F32))
        pQ2 = stA.enter_context(nc.psum_tensor("pQ2", [128, 512], F32))


        def wload(dst, src, pat, **kw):
            return DMA("gpsimd", lambda e: e.dma_start(out=dst[:], in_=src.rearrange(pat, **kw)), "wl")
        wload_all = [
            wload(s_wdq, wdq_t, "(fc p) l -> p fc l", p=128),
            wload(s_wuq, wuq_t, "(lc p) o -> p lc o", p=128),
            wload(s_wqr1, wqr1_t, "(lc p) o -> p lc o", p=128),
            wload(s_wqr2, wqr2_t, "(lc p) o -> p lc o", p=128),
            wload(s_wdkv, wdkv_t, "(fc p) l -> p fc l", p=128),
            wload(s_wuk, wuk_t, "(lc p) o -> p lc o", p=128),
            wload(s_wuv, wuv_t, "(lc p) o -> p lc o", p=128),
            wload(s_wkr1, wkr1_t, "(fc p) o -> p fc o", p=128),
            wload(s_wkr2, wkr2_t, "(fc p) o -> p fc o", p=128),
            DMA("sync", lambda e: e.dma_start(out=s_cs[0:16, :], in_=cosk), "tbl"),
            DMA("sync", lambda e: e.dma_start(out=s_cs[32:48, :], in_=sink), "tbl"),
            DMA("sync", lambda e: e.dma_start(out=s_c8q[:], in_=c8q), "tbl"),
            DMA("sync", lambda e: e.dma_start(out=s_s8q[:], in_=s8q), "tbl"),
        ]

        def emit_quant(x_ap, q_ap, Fw, t, dep_in):
            g = emit_quant_gen(x_ap, q_ap, Fw, t, dep_in)
            r = None
            while True:
                try:
                    r = g.send(None) or r
                except StopIteration:
                    break
            return r

        def emit_quant_gen(x_ap, q_ap, Fw, t, dep_in):
            nb = Fw // 16
            P_ = x_ap.shape[0] if hasattr(x_ap, "shape") else 128
            P_ = x_ap.shape[0]
            x3 = x_ap.rearrange("p (b w) -> p b w", w=16)
            t_amx = OP("vector", lambda e: e.tensor_reduce(
                out=t["am"][0:P_, 0:nb], in_=x3, axis=AX.X, op=ALU.max,
                apply_absolute_value=True), deps=dep_in)
            yield
            OP("vector", lambda e: e.tensor_scalar(
                out=t["sc"][0:P_, 0:nb], in0=t["am"][0:P_, 0:nb], scalar1=1.0 / 12.0,
                scalar2=5e-13, op0=ALU.mult, op1=ALU.max), deps=[t_amx])
            t_r2 = OP("vector", lambda e: e.reciprocal(
                out=t["rs"][0:P_, 0:nb], in_=t["sc"][0:P_, 0:nb]))
            yield
            rsb = t["rs"][0:P_, 0:nb].rearrange("p (b one) -> p b one", one=1).broadcast_to([P_, nb, 16])
            scb = t["sc"][0:P_, 0:nb].rearrange("p (b one) -> p b one", one=1).broadcast_to([P_, nb, 16])
            au = t["au"][0:P_, 0:Fw]
            au3 = au.rearrange("p (b w) -> p b w", w=16)
            t_abs = OP("scalar", lambda e: e.activation(out=au, in_=x_ap, func=AF.Abs),
                       deps=dep_in)
            yield
            OP("vector", lambda e: e.tensor_tensor(out=au3, in0=au3, in1=rsb, op=ALU.mult),
               deps=[t_abs, t_r2])
            yield
            L = t["L"][0:P_, 0:Fw]
            t1 = t["t1"][0:P_, 0:Fw]

            def cmp(dst, thr):
                OP("vector", lambda e, dst=dst, thr=thr: e.tensor_scalar(
                    out=dst, in0=au, scalar1=float(thr), scalar2=None, op0=ALU.is_gt))
            cmp(L, 0.5)
            yield
            for thr in (1.5, 2.5, 3.5):
                cmp(t1, thr)
                yield
                OP("vector", lambda e: e.tensor_add(out=L, in0=L, in1=t1))
                yield
            for thr, w in ((5.0, 2.0), (7.0, 2.0), (10.0, 4.0)):
                cmp(t1, thr)
                yield
                OP("vector", lambda e, w=w: e.scalar_tensor_tensor(
                    out=L, in0=t1, scalar=w, in1=L, op0=ALU.mult, op1=ALU.add))
                yield
            t_lastL = pr.last("vector")
            tsg = OP("scalar", lambda e: e.activation(out=t1, in_=x_ap, func=AF.Sign),
                     deps=[t_lastL])
            yield
            OP("vector", lambda e: e.tensor_mul(out=L, in0=L, in1=t1), deps=[tsg])
            yield
            tq = OP("vector", lambda e: e.tensor_tensor(
                out=q_ap.rearrange("p (b w) -> p b w", w=16),
                in0=L.rearrange("p (b w) -> p b w", w=16), in1=scb, op=ALU.mult))
            yield (tq, tsg)

        ln_exp_war = {0: (), 1: ()}

        def emit_ln_sub(x_ap, xn_ap, dep_in, lp):
            OP("vector", lambda e, x_ap=x_ap, lp=lp: e.bn_stats(
                out=s_bn6[lp][:, 0:6], in_=x_ap[:, 0:512]), deps=list(dep_in))
            OP("vector", lambda e, x_ap=x_ap, lp=lp: e.bn_stats(
                out=s_bn6[lp][:, 6:12], in_=x_ap[:, 512:1024]))
            t2_ = OP("vector", lambda e, lp=lp: e.bn_aggr(out=s_mv[lp][:], in_=s_bn6[lp][:]),
                     deps=list(ln_exp_war[lp]))
            t3_ = OP("scalar", lambda e, lp=lp: e.activation(
                out=s_sd[lp][:], in_=s_mv[lp][:, 1:2], func=AF.Ln, bias=s_c[:, 1:2]),
                deps=[t2_, tok_c])
            t4_ = OP("scalar", lambda e, lp=lp: e.activation(
                out=s_rsd[lp][:], in_=s_sd[lp][:], func=AF.Exp, scale=s_c[:, 10:11]))
            ln_exp_war[lp] = [t4_]
            tn = OP("vector", lambda e, x_ap=x_ap, xn_ap=xn_ap, lp=lp: e.tensor_scalar(
                out=xn_ap, in0=x_ap, scalar1=s_mv[lp][:, 0:1], scalar2=s_rsd[lp][:],
                op0=ALU.subtract, op1=ALU.mult), deps=[t4_])
            return tn

        xload_free = {0: (), 1: ()}
        xnf_free = {0: (), 1: ()}
        xnscr_free = {0: (), 1: ()}
        xn_free = [()]
        evac_tok = {0: None, 1: None}
        tmp_free = {0: (), 1: ()}
        kv_store_toks = []
        q_store_toks = []
        rope_war = [()]
        mm_g = [0]

        def mm_group(lhs_list, rhs_list, extra_deps):
            g = mm_g[0]
            mm_g[0] += 1
            p = pM[g % 2]
            dep_war = [evac_tok[g % 2]] if evac_tok[g % 2] else []
            n = len(lhs_list)
            tok = None
            for i, (lf, rf) in enumerate(zip(lhs_list, rhs_list)):
                d = (list(extra_deps) + dep_war) if i == 0 else ()
                tok = OP("tensor", lambda e, lf=lf, rf=rf, i=i, n=n, p=p: e.matmul(
                    p[:, :], lf, rf, start=(i == 0), stop=(i == n - 1)), deps=d)
            return g, p, tok

        def stageA1(ti):
            par = ti % 2
            is_q = ti >= 8
            src_ = x_full[(ti * 512):(ti + 1) * 512, :] if not is_q else \
                xq_in[(ti - 8) * 512:(ti - 7) * 512, :]
            xl = DMA("sync", lambda e, src_=src_, par=par: e.dma_start(
                out=s_x[par][:], in_=src_.rearrange("(s p) d -> p s d", p=128)),
                f"xl{par}", deps=xload_free[par])
            last_ln = None
            for sub in range(4):
                d = [xl] + (list(xn_free[0]) if sub == 0 else [])
                last_ln = emit_ln_sub(s_x[par][:, sub, :], s_xn[:, sub, :], d, sub % 2)
            xload_free[par] = [last_ln]
            xst = DMA("gpsimd", lambda e, par=par: e.dma_start(
                out=xnscr[par].rearrange("(s p) d -> p s d", p=128), in_=s_xn[:]),
                f"xns{par}", deps=[last_ln] + list(xnscr_free[par]))
            xn_free[0] = [xst]
            xt = DMA("sync", lambda e, par=par: e.dma_start(
                out=s_xnf[par][:], in_=xnscr[par], transpose=True),
                f"xnt{par}", deps=[xst] + list(xnf_free[par]))
            xnscr_free[par] = [xt]
            return xt

        def stageA2(ti, xt):
            par = ti % 2
            is_q = ti >= 8
            t0 = (ti * 512) if not is_q else (ti - 8) * 512
            deps0 = [xt] + (wload_all if ti == 0 else [])
            wdown = s_wdkv if not is_q else s_wdq
            ck_toks = []
            for oc in range(4):
                g, p, tk = mm_group(
                    [wdown[:, fc, oc * 128:(oc + 1) * 128] for fc in range(8)],
                    [s_xnf[par][:, fc, :] for fc in range(8)], deps0)
                te = OP("vector", lambda e, p=p, oc=oc: e.tensor_copy(out=s_ckv[:, oc, :], in_=p[:, :]),
                        deps=[tk])
                evac_tok[g % 2] = te
                ck_toks.append(te)
            ck_dep = [ck_toks[-1]]
            if not is_q:
                tkr = None
                for half_i, wkr in ((0, s_wkr1), (1, s_wkr2)):
                    rows = slice(0, 16) if half_i == 0 else slice(32, 48)
                    for fc in range(8):
                        d = (deps0 + list(rope_war[0])) if (half_i == 0 and fc == 0) else ()
                        tkr = OP("tensor", lambda e, fc=fc, rows=rows, wkr=wkr, par=par: e.matmul(
                            pQ1[rows, :], wkr[:, fc, :], s_xnf[par][:, fc, :],
                            start=(fc == 0), stop=(fc == 7)), deps=d)
                ck_ = s_cs[0:16, t0:t0 + 512]
                sk_ = s_cs[32:48, t0:t0 + 512]
                OP("vector", lambda e, ck_=ck_: e.tensor_mul(out=s_ta[0:16, :], in0=pQ1[0:16, :], in1=ck_),
                   deps=[tkr] + list(rope_war[0]))
                OP("vector", lambda e, sk_=sk_: e.tensor_mul(out=s_tb[0:16, :], in0=pQ1[32:48, :], in1=sk_))
                OP("vector", lambda e: e.tensor_sub(out=s_ra[0:16, :], in0=s_ta[0:16, :], in1=s_tb[0:16, :]))
                OP("vector", lambda e, sk_=sk_: e.tensor_mul(out=s_ta[0:16, :], in0=pQ1[0:16, :], in1=sk_))
                OP("vector", lambda e, ck_=ck_: e.tensor_mul(out=s_tb[0:16, :], in0=pQ1[32:48, :], in1=ck_))
                trb = OP("vector", lambda e: e.tensor_add(out=s_rb[0:16, :], in0=s_ta[0:16, :], in1=s_tb[0:16, :]))
                d1 = DMA("gpsimd", lambda e, t0=t0: e.dma_start(
                    out=krscr[0:16, t0:t0 + 512], in_=s_ra[0:16, :]), "krst", deps=[trb])
                d2 = DMA("gpsimd", lambda e, t0=t0: e.dma_start(
                    out=krscr[16:32, t0:t0 + 512], in_=s_rb[0:16, :]), "krst", deps=[trb])
                rope_war[0] = [d1, d2]
                kv_store_toks.extend([d1, d2])
                for (wsb, scr, stream) in ((s_wuk, kscr, "kst"), (s_wuv, vscr, "vst")):
                    for oc8 in range(8):
                        g, p, tk = mm_group(
                            [wsb[:, lc, oc8 * 128:(oc8 + 1) * 128] for lc in range(4)],
                            [s_ckv[:, lc, :] for lc in range(4)], ck_dep)
                        te = OP("vector", lambda e, p=p, g=g: e.tensor_copy(out=s_tmp[g % 2][:], in_=p[:, :]),
                                deps=[tk] + list(tmp_free[g % 2]))
                        evac_tok[g % 2] = te
                        td = DMA("gpsimd", lambda e, scr=scr, oc8=oc8, t0=t0, g=g: e.dma_start(
                            out=scr[oc8 * 128:(oc8 + 1) * 128, t0:t0 + 512], in_=s_tmp[g % 2][:]),
                            stream, deps=[te])
                        tmp_free[g % 2] = [td]
                        kv_store_toks.append(td)
            else:
                for oc2 in range(2):
                    tq1 = None
                    for pnum, wq_, pp in ((0, s_wqr1, pQ1), (1, s_wqr2, pQ2)):
                        for lc in range(4):
                            d = (ck_dep + list(rope_war[0])) if (pnum == 0 and lc == 0) else ()
                            tq1 = OP("tensor", lambda e, oc2=oc2, lc=lc, wq_=wq_, pp=pp: e.matmul(
                                pp[:, :], wq_[:, lc, oc2 * 128:(oc2 + 1) * 128], s_ckv[:, lc, :],
                                start=(lc == 0), stop=(lc == 3)), deps=d)
                    cq_ = s_c8q[:, t0:t0 + 512]
                    sq_ = s_s8q[:, t0:t0 + 512]
                    OP("vector", lambda e, cq_=cq_: e.tensor_mul(out=s_ta[:], in0=pQ1[:, :], in1=cq_),
                       deps=[tq1] + list(rope_war[0]))
                    OP("vector", lambda e, sq_=sq_: e.tensor_mul(out=s_tb[:], in0=pQ2[:, :], in1=sq_))
                    OP("vector", lambda e: e.tensor_sub(out=s_ra[:], in0=s_ta[:], in1=s_tb[:]))
                    OP("vector", lambda e, sq_=sq_: e.tensor_mul(out=s_ta[:], in0=pQ1[:, :], in1=sq_))
                    OP("vector", lambda e, cq_=cq_: e.tensor_mul(out=s_tb[:], in0=pQ2[:, :], in1=cq_))
                    trb = OP("vector", lambda e: e.tensor_add(out=s_rb[:], in0=s_ta[:], in1=s_tb[:]))
                    d1 = DMA("gpsimd", lambda e, oc2=oc2, t0=t0: e.dma_start(
                        out=qscr[oc2 * 8:(oc2 + 1) * 8, 64:80, t0:t0 + 512], in_=s_ra[:]),
                        "qst", deps=[trb])
                    d2 = DMA("gpsimd", lambda e, oc2=oc2, t0=t0: e.dma_start(
                        out=qscr[oc2 * 8:(oc2 + 1) * 8, 80:96, t0:t0 + 512], in_=s_rb[:]),
                        "qst", deps=[trb])
                    rope_war[0] = [d1, d2]
                    q_store_toks.extend([d1, d2])
                for oc8 in range(8):
                    g, p, tk = mm_group(
                        [s_wuq[:, lc, oc8 * 128:(oc8 + 1) * 128] for lc in range(4)],
                        [s_ckv[:, lc, :] for lc in range(4)], ck_dep)
                    te = OP("vector", lambda e, p=p, g=g: e.tensor_copy(out=s_tmp[g % 2][:], in_=p[:, :]),
                            deps=[tk] + list(tmp_free[g % 2]))
                    evac_tok[g % 2] = te
                    td = DMA("gpsimd", lambda e, oc8=oc8, t0=t0, g=g: e.dma_start(
                        out=qscr[2 * oc8:2 * oc8 + 2, 0:64, t0:t0 + 512], in_=s_tmp[g % 2][:]),
                        "qst", deps=[te])
                    tmp_free[g % 2] = [td]
                    q_store_toks.append(td)
            xnf_free[par] = [pr.last("tensor")]
            

        xt_of = {}
        for ti in range(13):
            if ti < 12:
                xt_of[ti] = stageA1(ti)
            if ti >= 1:
                stageA2(ti - 1, xt_of.pop(ti - 1))

        barA = pr.barrier()
        stA.close()

        stV = ExitStack()
        s_vtmp = [stV.enter_context(nc.sbuf_tensor(f"s_vtmp{i}", [128, D], BF16,
                                                   is_transpose_dest=True)) for i in range(2)]
        vt_prev = {0: barA, 1: barA}
        vtok_toks = []
        for kc in range(32):
            pv_ = kc % 2
            tl = DMA("sync", lambda e, kc=kc, pv_=pv_: e.dma_start(
                out=s_vtmp[pv_][:], in_=vscr[:, kc * 128:(kc + 1) * 128], transpose=True),
                f"vtt{pv_}", deps=vt_prev[pv_])
            ts = DMA("gpsimd", lambda e, kc=kc, pv_=pv_: e.dma_start(
                out=vtok[kc * 128:(kc + 1) * 128, :], in_=s_vtmp[pv_][:]), "vts", deps=[tl])
            vt_prev[pv_] = [ts]
            vtok_toks.append(ts)
        stV.close()

        stB = ExitStack()
        s_k = [stB.enter_context(nc.sbuf_tensor(f"s_k{i}", [96, S], BF16)) for i in range(2)]
        s_q = [stB.enter_context(nc.sbuf_tensor(f"s_q{i}", [96, NQ], BF16)) for i in range(2)]
        s_v1 = [stB.enter_context(nc.sbuf_tensor(f"s_v1{i}", [128, 32, 65], BF16)) for i in range(2)]
        s_p = [stB.enter_context(nc.sbuf_tensor(f"s_p{i}", [128, 1024], BF16)) for i in range(3)]
        s_attn = stB.enter_context(nc.sbuf_tensor("s_attn", [64, H, NQ], BF16))
        s_masks = stB.enter_context(nc.sbuf_tensor("s_masks", [128, NPOS, 8, QT], BF16))
        s_rcp = stB.enter_context(nc.sbuf_tensor("s_rcp", [1, QT], F32))
        s_bc = stB.enter_context(nc.sbuf_tensor("s_bc", [64, QT], F32))
        s_ao = stB.enter_context(nc.sbuf_tensor("s_ao", [128, 8, QT], F32))

        stBq = ExitStack()
        WQF = 2816
        s_wf = stBq.enter_context(nc.sbuf_tensor("s_wf", [128, WQF], F32))
        s_am = stBq.enter_context(nc.sbuf_tensor("s_am", [128, WQF // 16], F32))
        s_sc = stBq.enter_context(nc.sbuf_tensor("s_sc", [128, WQF // 16], F32))
        s_rs = stBq.enter_context(nc.sbuf_tensor("s_rs", [128, WQF // 16], F32))
        s_au = stBq.enter_context(nc.sbuf_tensor("s_au", [128, WQF], F32))
        s_L = stBq.enter_context(nc.sbuf_tensor("s_L", [128, WQF], BF16))
        s_t1 = stBq.enter_context(nc.sbuf_tensor("s_t1", [128, WQF], BF16))
        s_qq = stBq.enter_context(nc.sbuf_tensor("s_qq", [128, WQF], BF16))
        tQ = dict(am=s_am, sc=s_sc, rs=s_rs, au=s_au, L=s_L, t1=s_t1)

        wq_batches = []
        for (src_, dst_, nch, Fw) in ((wg_n, wgq_n, FI // 128, D),
                                      (wu_n, wuq_n, FI // 128, D),
                                      (wd_n, wdq_n, D // 128, FI)):
            per = WQF // Fw
            ch = 0
            while ch < nch:
                n_ = min(per, nch - ch)
                wq_batches.append((src_, dst_, [128 * (ch + i) for i in range(n_)], Fw))
                ch += n_
        wq_prev = [()]
        wq_store_toks = []

        def wq_driver():
            for (src_, dst_, r0s, Fw) in wq_batches:
                tl = None
                for i, r0 in enumerate(r0s):
                    tl = DMA("sync", lambda e, src_=src_, r0=r0, i=i, Fw=Fw: e.dma_start(
                        out=s_wf[:, i * Fw:(i + 1) * Fw], in_=src_[r0:r0 + 128, :]), "wf",
                        deps=wq_prev[0] if i == 0 else ())
                Ft = Fw * len(r0s)
                res = None
                gen = emit_quant_gen(s_wf[:, 0:Ft], s_qq[:, 0:Ft], Ft, tQ, [tl])
                while True:
                    try:
                        r = gen.send(None)
                        if r is not None:
                            res = r
                        yield
                    except StopIteration:
                        break
                tq, tsg = res
                ts = None
                for i, r0 in enumerate(r0s):
                    ts = DMA("gpsimd", lambda e, dst_=dst_, r0=r0, i=i, Fw=Fw: e.dma_start(
                        out=dst_[r0:r0 + 128, :], in_=s_qq[:, i * Fw:(i + 1) * Fw]), "wqs",
                        deps=[tq] if i == 0 else ())
                wq_prev[0] = [tq, tsg, ts]
                wq_store_toks.append(ts)
                yield

        wq_it = [wq_driver()]

        def emit_wq_some(n):
            for _ in range(n):
                try:
                    next(wq_it[0])
                except StopIteration:
                    return

        stBp = ExitStack()
        pAT = [stBp.enter_context(nc.psum_tensor(f"pAT{i}", [128, 512], F32)) for i in range(2)]
        pS = [stBp.enter_context(nc.psum_tensor(f"pS{i}", [128, 1024], F32)) for i in range(3)]

        tmask = DMA("gpsimd", lambda e: e.dma_start(
            out=s_masks[:], in_=masks_in.rearrange("a b p q -> p a b q")), "bmk", deps=barA)
        tv1 = [OP("gpsimd", lambda e, i=i: e.memset(s_v1[i][:], 1.0), deps=barA) for i in range(2)]

        kv_bar = [kv_store_toks[-1], q_store_toks[-1], vtok_toks[-1]]
        head_free = {0: [tv1[0]] + kv_bar, 1: [tv1[1]] + kv_bar}
        head_toks = {}

        def load_head(h, par, deps):
            DMA("sync", lambda e, h=h, par=par: e.dma_start(
                out=s_k[par][0:64, :], in_=kscr[h * 64:(h + 1) * 64, :]), f"bk{par}", deps=deps)
            DMA("sync", lambda e, par=par: e.dma_start(out=s_k[par][64:96, :], in_=krscr), f"bk{par}")
            DMA("sync", lambda e, h=h, par=par: e.dma_start(out=s_q[par][:], in_=qscr[h]), f"bk{par}")
            t4_ = DMA("sync", lambda e, h=h, par=par: e.dma_start(
                out=s_v1[par][:, :, 0:64],
                in_=vtok[:, h * 64:(h + 1) * 64].rearrange("(kc p) o -> p kc o", p=128)), f"bk{par}")
            return [t4_]

        head_toks[0] = load_head(0, 0, head_free[0])

        pS_free = {0: (), 1: (), 2: ()}
        pP_free = {0: (), 1: (), 2: ()}
        pAT_free = {0: (), 1: ()}
        bc_free = [()]
        bc2_free = [()]
        bc3_free = [()]
        at_i = [0]

        for h in range(16):
            par = h % 2
            if h + 1 < 16:
                head_toks[h + 1] = load_head(h + 1, (h + 1) % 2, head_free[(h + 1) % 2])
            kload = head_toks[h]
            last_S = None
            last_PV = None
            groups = []
            for pos in range(NPOS):
                ai = at_i[0] % 2
                at_i[0] += 1
                ng = EXT[pos] // 2
                for g in range(ng):
                    groups.append((pos, g, ai))
            LOOK = 2
            gi = [0]
            tm_of = {}

            def emit_S(idx):
                pos, g, ai = groups[idx]
                ext = EXT[pos]
                qsl = slice(pos * QT, (pos + 1) * QT)
                gp = idx % 3
                st_ = None
                for kk in range(2):
                    kc = 2 * g + kk
                    d = (list(pS_free[gp]) + list(kload)) if kk == 0 else ()
                    st_ = OP("tensor", lambda e, kc=kc, gp=gp, kk=kk, par=par, qsl=qsl: e.matmul(
                        pS[gp][:, kk * 512:(kk + 1) * 512],
                        s_k[par][:, kc * 128:(kc + 1) * 128], s_q[par][:, qsl],
                        start=True, stop=True), deps=d)
                texp = OP("scalar", lambda e, gp=gp: e.activation(
                    out=s_p[gp][:], in_=pS[gp][:, :], func=AF.Exp, scale=s_c[:, 0:1]),
                    deps=[st_, tok_c] + list(pP_free[gp]))
                pS_free[gp] = [texp]
                tm = OP("vector", lambda e, gp=gp: e.tensor_scalar(
                    out=s_p[gp][:], in0=s_p[gp][:], scalar1=CLIP_HI, scalar2=CLIP_LO,
                    op0=ALU.min, op1=ALU.max), deps=[texp])
                j0 = 2 * g - (ext - 8)
                if j0 >= 0:
                    tm = OP("vector", lambda e, gp=gp, pos=pos, j0=j0: e.tensor_mul(
                        out=s_p[gp][:], in0=s_p[gp][:],
                        in1=s_masks[:, pos, j0:j0 + 2, :].rearrange("p a q -> p (a q)")),
                        deps=[tmask])
                tm_of[idx] = tm
                return st_

            def emit_PV(idx):
                pos, g, ai = groups[idx]
                ext = EXT[pos]
                gp = idx % 3
                tm = tm_of.pop(idx)
                pv_tok = None
                for kk in range(2):
                    kc = 2 * g + kk
                    d = [tm] if kk == 0 else ()
                    if kc == 0:
                        d = [tm] + list(pAT_free[ai])
                    pv_tok = OP("tensor", lambda e, kc=kc, gp=gp, kk=kk, par=par, ai=ai, ext=ext: e.matmul(
                        pAT[ai][0:65, :], s_v1[par][:, kc, :],
                        s_p[gp][:, kk * 512:(kk + 1) * 512],
                        start=(kc == 0), stop=(kc == ext - 1)), deps=d)
                pP_free[gp] = [pv_tok]
                if 2 * g + 1 == ext - 1:
                    emit_divide(pos, ai, pv_tok)
                return pv_tok

            def emit_divide(pos, ai, pv_tok):
                qsl = slice(pos * QT, (pos + 1) * QT)
                trc0 = OP("scalar", lambda e, ai=ai: e.activation(
                    out=s_rcp[:], in_=pAT[ai][64:65, :], func=AF.Ln),
                    deps=[pv_tok] + list(bc_free[0]))
                trc = OP("scalar", lambda e: e.activation(
                    out=s_rcp[:], in_=s_rcp[:], func=AF.Exp, scale=s_c[0:1, 11:12]),
                    deps=[tok_c])
                tst = DMA("gpsimd", lambda e: e.dma_start(out=rcpscr, in_=s_rcp[:]),
                          "rcs", deps=[trc] + list(bc2_free[0]))
                bc_free[0] = [tst]
                tld = DMA("gpsimd", lambda e: e.dma_start(
                    out=s_bc[:], in_=rcpscr.broadcast_to([64, QT])), "rcl",
                    deps=[tst] + list(bc3_free[0]))
                bc2_free[0] = [tld]
                tdv = OP("vector", lambda e, ai=ai, h=h, qsl=qsl: e.tensor_mul(
                    out=s_attn[:, h, qsl], in0=pAT[ai][0:64, :], in1=s_bc[:]), deps=[tld])
                pAT_free[ai] = [tdv]
                bc3_free[0] = [tdv]

            n_g = len(groups)
            for idx in range(n_g + LOOK):
                if idx < n_g:
                    last_S = emit_S(idx)
                if idx >= LOOK:
                    last_PV = emit_PV(idx - LOOK)
                emit_wq_some(1)
            head_free[par] = [last_S, last_PV]
            if h == 15:
                emit_wq_some(9999)

        div_last = pr.last("vector")
        stBp.close()
        stBq.close()
        stBW = ExitStack()
        s_xq4 = [stBW.enter_context(nc.sbuf_tensor(f"s_xq4{i}", [128, D], F32)) for i in range(2)]
        s_x2t = [stBW.enter_context(nc.sbuf_tensor(f"s_x2t{i}", [128, D], F32)) for i in range(2)]
        s_xlnW = stBW.enter_context(nc.sbuf_tensor("s_xlnW", [128, D], F32))
        s_bnW = [stBW.enter_context(nc.sbuf_tensor(f"s_bnW{i}", [128, 12], F32)) for i in range(2)]
        s_mvW = [stBW.enter_context(nc.sbuf_tensor(f"s_mvW{i}", [128, 2], F32)) for i in range(2)]
        s_sdW = [stBW.enter_context(nc.sbuf_tensor(f"s_sdW{i}", [128, 1], F32)) for i in range(2)]
        s_rsdW = [stBW.enter_context(nc.sbuf_tensor(f"s_rsdW{i}", [128, 1], F32)) for i in range(2)]
        s_wo = stBW.enter_context(nc.sbuf_tensor("s_wo", [64, H, D], BF16))
        lnW_war = {0: (), 1: ()}
        xlnW_free = [()]
        xln_store = []

        def emit_lnW(x_ap, tt, lp):
            OP("vector", lambda e, x_ap=x_ap, lp=lp: e.bn_stats(
                out=s_bnW[lp][:, 0:6], in_=x_ap[:, 0:512]))
            OP("vector", lambda e, x_ap=x_ap, lp=lp: e.bn_stats(
                out=s_bnW[lp][:, 6:12], in_=x_ap[:, 512:1024]))
            t2_ = OP("vector", lambda e, lp=lp: e.bn_aggr(out=s_mvW[lp][:], in_=s_bnW[lp][:]),
                     deps=list(lnW_war[lp]))
            t3_ = OP("scalar", lambda e, lp=lp: e.activation(
                out=s_sdW[lp][:], in_=s_mvW[lp][:, 1:2], func=AF.Ln, bias=s_c[:, 1:2]),
                deps=[t2_, tok_c])
            t4_ = OP("scalar", lambda e, lp=lp: e.activation(
                out=s_rsdW[lp][:], in_=s_sdW[lp][:], func=AF.Exp, scale=s_c[:, 10:11]))
            lnW_war[lp] = [t4_]
            tn = OP("vector", lambda e, x_ap=x_ap, lp=lp: e.tensor_scalar(
                out=s_xlnW[:], in0=x_ap, scalar1=s_mvW[lp][:, 0:1], scalar2=s_rsdW[lp][:],
                op0=ALU.subtract, op1=ALU.mult), deps=[t4_] + list(xlnW_free[0]))
            ts_ = DMA("gpsimd", lambda e, tt=tt: e.dma_start(
                out=xlnscr[tt * 128:(tt + 1) * 128, :], in_=s_xlnW[:]), "xlns", deps=[tn])
            xlnW_free[0] = [ts_]
            xln_store.append(ts_)
            return tn
        two = DMA("gpsimd", lambda e: e.dma_start(
            out=s_wo[:], in_=wo_t.rearrange("(h p) d -> p h d", p=64)), "bmk",
            deps=list(barA) + list(wq_prev[0]))
        pW = [stBW.enter_context(nc.psum_tensor(f"pW{i}", [128, 512], F32)) for i in range(2)]
        pT = [stBW.enter_context(nc.psum_tensor(f"pTr{i}", [128, 1024], F32)) for i in range(2)]

        x2_store_toks = []
        wo_evac = {0: (), 1: ()}
        pT_free = {0: (), 1: ()}
        xq4_free = {0: (), 1: ()}
        x2t_free = {0: (), 1: ()}
        for qt in range(NPOS):
            qsl = slice(qt * QT, (qt + 1) * QT)
            ao_toks = []
            for dc in range(8):
                wp = dc % 2
                tk = None
                for hh in range(16):
                    d = ([div_last, two] + list(wo_evac[wp])) if hh == 0 else ()
                    tk = OP("tensor", lambda e, hh=hh, dc=dc, qsl=qsl, wp=wp: e.matmul(
                        pW[wp][:, :], s_wo[:, hh, dc * 128:(dc + 1) * 128], s_attn[:, hh, qsl],
                        start=(hh == 0), stop=(hh == 15)), deps=d)
                te = OP("scalar", lambda e, dc=dc, wp=wp: e.copy(out=s_ao[:, dc, :], in_=pW[wp][:, :]),
                        deps=[tk])
                wo_evac[wp] = [te]
                ao_toks.append(te)
            for sub in range(4):
                tp = sub % 2
                txq = DMA("sync", lambda e, qt=qt, sub=sub, tp=tp: e.dma_start(
                    out=s_xq4[tp][:],
                    in_=xq_in[qt * 512 + sub * 128:qt * 512 + (sub + 1) * 128, :]),
                    f"bxq{tp}", deps=list(xq4_free[tp]) + list(wq_prev[0]))
                tt_ = None
                for dc in range(8):
                    d = [ao_toks[dc], tok_id] + (list(pT_free[tp]) if dc == 0 else [])
                    tt_ = OP("tensor", lambda e, dc=dc, sub=sub, tp=tp: e.transpose(
                        out=pT[tp][:, dc * 128:(dc + 1) * 128],
                        in_=s_ao[:, dc, sub * 128:(sub + 1) * 128], identity=s_id[:]),
                        deps=d)
                ta = OP("vector", lambda e, tp=tp: e.tensor_add(
                    out=s_x2t[tp][:], in0=pT[tp][:, :], in1=s_xq4[tp][:]),
                    deps=[tt_, txq] + list(x2t_free[tp]))
                pT_free[tp] = [ta]
                xq4_free[tp] = [ta]
                ts = DMA("gpsimd", lambda e, qt=qt, sub=sub, tp=tp: e.dma_start(
                    out=x2scr[qt * 512 + sub * 128: qt * 512 + (sub + 1) * 128, :],
                    in_=s_x2t[tp][:]), "x2s", deps=[ta])
                tln = emit_lnW(s_x2t[tp][:], qt * 4 + sub, tp)
                x2t_free[tp] = [ts, tln]
                x2_store_toks.append(ts)

        barB = pr.barrier()
        stBW.close()
        stB.close()

        barC0 = pr.barrier()

        stC = ExitStack()
        s_wg = stC.enter_context(nc.sbuf_tensor("s_wg", [128, 8, FI], BF16, is_transpose_dest=True))
        s_wu = stC.enter_context(nc.sbuf_tensor("s_wu", [128, 8, FI], BF16, is_transpose_dest=True))
        s_wd = stC.enter_context(nc.sbuf_tensor("s_wd", [128, 22, D], BF16, is_transpose_dest=True))
        s_x2b = [stC.enter_context(nc.sbuf_tensor("s_x2b", [128, D], F32))] * 2
        s_xln = [stC.enter_context(nc.sbuf_tensor("s_xlnb", [128, D], F32))] * 2
        s_xq = stC.enter_context(nc.sbuf_tensor("s_xqC", [128, D], BF16))
        s_xqf = [stC.enter_context(nc.sbuf_tensor(f"s_xqf{i}", [128, 8, 128], BF16,
                                                  is_transpose_dest=True)) for i in range(2)]
        s_h = stC.enter_context(nc.sbuf_tensor("s_h", [128, FI], F32))
        s_hq = stC.enter_context(nc.sbuf_tensor("s_hq", [128, FI], BF16))
        s_hqT = [stC.enter_context(nc.sbuf_tensor(f"s_hqT{i}", [128, 22, 128], BF16,
                                                  is_transpose_dest=True)) for i in range(2)]
        s_sil = [stC.enter_context(nc.sbuf_tensor(f"s_sil{i}", [128, 512], F32)) for i in range(2)]
        s_outt = [stC.enter_context(nc.sbuf_tensor("s_outt", [128, D], F32))] * 2
        s_am2 = stC.enter_context(nc.sbuf_tensor("s_am2", [128, FI // 16], F32))
        s_sc2 = stC.enter_context(nc.sbuf_tensor("s_sc2", [128, FI // 16], F32))
        s_rs2 = stC.enter_context(nc.sbuf_tensor("s_rs2", [128, FI // 16], F32))
        s_au2 = stC.enter_context(nc.sbuf_tensor("s_au2", [128, FI], F32))
        s_L2 = stC.enter_context(nc.sbuf_tensor("s_L2", [128, FI], BF16))
        s_t12 = stC.enter_context(nc.sbuf_tensor("s_t12", [128, FI], BF16))
        pG = [stC.enter_context(nc.psum_tensor(f"pG{i}", [128, 512], F32)) for i in range(2)]
        pU = [stC.enter_context(nc.psum_tensor(f"pU{i}", [128, 512], F32)) for i in range(2)]
        pD = [stC.enter_context(nc.psum_tensor(f"pD{i}", [128, 512], F32)) for i in range(2)]
        tC = dict(am=s_am2, sc=s_sc2, rs=s_rs2, au=s_au2, L=s_L2, t1=s_t12)

        wtr = []
        wqdep = barC0 + [wq_store_toks[-1]]
        for fc in range(8):
            wtr.append(DMA("sync", lambda e, fc=fc: e.dma_start(
                out=s_wg[:, fc, :], in_=wgq_n[:, fc * 128:(fc + 1) * 128], transpose=True),
                "wtr", deps=wqdep if fc == 0 else ()))
            wtr.append(DMA("sync", lambda e, fc=fc: e.dma_start(
                out=s_wu[:, fc, :], in_=wuq_n[:, fc * 128:(fc + 1) * 128], transpose=True), "wtr"))
        for ic in range(22):
            wtr.append(DMA("sync", lambda e, ic=ic: e.dma_start(
                out=s_wd[:, ic, :], in_=wdq_n[:, ic * 128:(ic + 1) * 128], transpose=True), "wtr"))
        wtr_tok = wtr[-1]

        x2b_free = {0: barC0, 1: barC0}
        xln_free = {0: barC0, 1: barC0}
        xq_free = [()]
        xqscr_freeC = {0: (), 1: ()}
        hqscr_freeC = {0: (), 1: ()}
        xqf_free = {0: (), 1: ()}
        hq_free = [()]
        hqT_free = {0: (), 1: ()}
        sil_free = {0: (), 1: ()}
        outt_free = {0: (), 1: ()}
        gup_free = {0: (), 1: ()}
        pD_free = {0: (), 1: ()}
        out_toks = []
        st1 = {}
        st2 = {}

        def stage1(tt):
            pc_ = tt % 2
            txl = DMA("sync", lambda e, tt=tt, pc_=pc_: e.dma_start(
                out=s_xln[pc_][:], in_=xlnscr[tt * 128:(tt + 1) * 128, :]), "cl0",
                deps=list(xln_free[0]) + [xln_store[-1]])
            tq, tsgx = emit_quant(s_xln[pc_][:], s_xq[:], D, tC, [txl] + list(xq_free[0]))
            tqs = DMA("gpsimd", lambda e, pc_=pc_: e.dma_start(out=xqscr[pc_], in_=s_xq[:]),
                      f"xqs{pc_}", deps=[tq] + list(xqscr_freeC[pc_]))
            xq_free[0] = [tqs]
            tqt = DMA("sync", lambda e, pc_=pc_: e.dma_start(
                out=s_xqf[pc_][:], in_=xqscr[pc_], transpose=True), f"xqt{pc_}",
                deps=[tqs] + list(xqf_free[pc_]))
            xqscr_freeC[pc_] = [tqt]
            xln_free[0] = [tq, tsgx]
            st1[tt] = tqt

        def stage2(tt):
            pc_ = tt % 2
            tqt = st1.pop(tt)
            h_toks = []
            for ic in range(6):
                iw = 512 if ic < 5 else 256
                isl = slice(ic * 512, ic * 512 + iw)
                gp = ic % 2
                tg = None
                for fc in range(8):
                    d = ([tqt, wtr_tok] + list(gup_free[gp])) if fc == 0 else ()
                    tg = OP("tensor", lambda e, fc=fc, isl=isl, iw=iw, gp=gp, pc_=pc_: e.matmul(
                        pG[gp][:, 0:iw], s_xqf[pc_][:, fc, :], s_wg[:, fc, isl],
                        start=(fc == 0), stop=(fc == 7)), deps=d)
                tu = None
                for fc in range(8):
                    tu = OP("tensor", lambda e, fc=fc, isl=isl, iw=iw, gp=gp, pc_=pc_: e.matmul(
                        pU[gp][:, 0:iw], s_xqf[pc_][:, fc, :], s_wu[:, fc, isl],
                        start=(fc == 0), stop=(fc == 7)))
                tsl = OP("scalar", lambda e, iw=iw, gp=gp: e.activation(
                    out=s_sil[gp][:, 0:iw], in_=pG[gp][:, 0:iw], func=AF.Silu),
                    deps=[tg] + list(sil_free[gp]))
                th = OP("vector", lambda e, isl=isl, iw=iw, gp=gp: e.tensor_mul(
                    out=s_h[:, isl], in0=s_sil[gp][:, 0:iw], in1=pU[gp][:, 0:iw]),
                    deps=[tsl, tu] + (list(hq_free[0]) if ic == 0 else []))
                gup_free[gp] = [th]
                sil_free[gp] = [th]
                h_toks.append(th)
            xqf_free[pc_] = [h_toks[-1]]
            tqh, tsgh = emit_quant(s_h[:], s_hq[:], FI, tC, [h_toks[-1]])
            tqhs = DMA("gpsimd", lambda e, pc_=pc_: e.dma_start(out=hqscr[pc_], in_=s_hq[:]),
                       f"hqs{pc_}", deps=[tqh] + list(hqscr_freeC[pc_]))
            hq_free[0] = [tqhs, tsgh]
            tqht = DMA("sync", lambda e, pc_=pc_: e.dma_start(
                out=s_hqT[pc_][:], in_=hqscr[pc_], transpose=True), f"hqt{pc_}",
                deps=[tqhs] + list(hqT_free[pc_]))
            hqscr_freeC[pc_] = [tqht]
            st2[tt] = tqht

        def stage3(tt):
            pc_ = tt % 2
            tqht = st2.pop(tt)
            tx2 = DMA("sync", lambda e, tt=tt: e.dma_start(
                out=s_x2b[0][:], in_=x2scr[tt * 128:(tt + 1) * 128, :]), "cx0",
                deps=x2b_free[0])
            dtoks = []
            for dh in range(2):
                dsl = slice(dh * 512, (dh + 1) * 512)
                td = None
                for ic in range(22):
                    d = ([tqht] + list(pD_free[dh])) if ic == 0 else ()
                    td = OP("tensor", lambda e, ic=ic, dsl=dsl, dh=dh, pc_=pc_: e.matmul(
                        pD[dh][:, :], s_hqT[pc_][:, ic, :], s_wd[:, ic, dsl],
                        start=(ic == 0), stop=(ic == 21)), deps=d)
                ta2 = OP("vector", lambda e, dsl=dsl, dh=dh: e.tensor_add(
                    out=s_outt[0][:, dsl], in0=pD[dh][:, :], in1=s_x2b[0][:, dsl]),
                    deps=[td, tx2] + (list(outt_free[0]) if dh == 0 else []))
                pD_free[dh] = [ta2]
                dtoks.append(ta2)
            hqT_free[pc_] = [dtoks[-1]]
            x2b_free[0] = [dtoks[-1]]
            to = DMA("gpsimd", lambda e, tt=tt: e.dma_start(
                out=out[tt * 128:(tt + 1) * 128, :], in_=s_outt[0][:]), "outs",
                deps=dtoks)
            outt_free[0] = [to]
            out_toks.append(to)

        for it in range(18):
            if it < 16:
                stage1(it)
            if 1 <= it < 17:
                stage2(it - 1)
            if it >= 2:
                stage3(it - 2)

        OP("vector", lambda e: e.engine_nop(), deps=[out_toks[-1]])
        stC.close()
        pr.build(block)
    return nc



def _qrow_idx(half):
    tiles = TILES_BY_POS[half]
    return np.concatenate([np.arange(t * 512, (t + 1) * 512) for t in tiles])


def _rope_tables():
    j = np.arange(16, dtype=np.float64)
    inv = 10000.0 ** (-2.0 * j / 32.0)
    pos = np.arange(S, dtype=np.float64)
    ang = pos[None, :] * inv[:, None]
    return (np.cos(ang).astype(np.float32), np.sin(ang).astype(np.float32), inv)


def _masks_for(half):
    m = np.zeros((NPOS, 8, 128, QT), dtype=np.float32)
    q = np.arange(QT)
    k = np.arange(128)
    for p in range(NPOS):
        e = EXT[p]
        t = TILES_BY_POS[half][p]
        r = 4 * t + 4
        q0 = t * 512
        for j in range(8):
            kc = e - 8 + j
            if kc < r - 4:
                m[p, j] = 1.0
            elif kc < r:
                m[p, j] = ((q0 + q[None, :]) >= (kc * 128 + k[:, None])).astype(np.float32)
    return m


def _deinterleave_rows(w, dhr, nh):
    w3 = w.reshape(nh, dhr, -1)
    return (np.ascontiguousarray(w3[:, 0::2, :].reshape(nh * (dhr // 2), -1)),
            np.ascontiguousarray(w3[:, 1::2, :].reshape(nh * (dhr // 2), -1)))


def kernel(L0_attn_x, L0_attn_W_DQ, L0_attn_W_UQ, L0_attn_W_QR, L0_attn_W_DKV,
           L0_attn_W_UK, L0_attn_W_UV, L0_attn_W_KR, L0_attn_W_O,
           L0_ffn_w_gate, L0_ffn_w_up, L0_ffn_w_down):
    X = np.ascontiguousarray(np.asarray(L0_attn_x, dtype=np.float32))
    T = lambda a: np.ascontiguousarray(np.asarray(a, dtype=np.float32).T)
    wdq_t = T(L0_attn_W_DQ)
    wuq_t = T(L0_attn_W_UQ)
    wdkv_t = T(L0_attn_W_DKV)
    wuk_t = T(L0_attn_W_UK)
    wuv_t = T(L0_attn_W_UV)
    wo_t = T(L0_attn_W_O)
    qr1, qr2 = _deinterleave_rows(np.asarray(L0_attn_W_QR, np.float32), DHR, H)
    wqr1_t, wqr2_t = T(qr1), T(qr2)
    kr1, kr2 = _deinterleave_rows(np.asarray(L0_attn_W_KR, np.float32), DHR, 1)
    wkr1_t, wkr2_t = T(kr1), T(kr2)
    wg_n = np.ascontiguousarray(np.asarray(L0_ffn_w_gate, np.float32))
    wu_n = np.ascontiguousarray(np.asarray(L0_ffn_w_up, np.float32))
    wd_n = np.ascontiguousarray(np.asarray(L0_ffn_w_down, np.float32))

    cosk, sink, inv = _rope_tables()
    consts = np.zeros((128, 16), dtype=np.float32)
    consts[:, 0] = SC
    consts[:, 1] = EPS
    consts[:, 10] = -0.5
    consts[:, 11] = -1.0

    in_maps = []
    qidx = {}
    for c in range(8):
        b, half = c // 2, c % 2
        qidx[c] = _qrow_idx(half)
        pos = qidx[c].astype(np.float64)
        ang = pos[None, :] * np.tile(inv, 8)[:, None]
        in_maps.append(dict(
            x_full=X[b], xq_in=np.ascontiguousarray(X[b][qidx[c]]),
            wdq_t=wdq_t, wuq_t=wuq_t, wqr1_t=wqr1_t, wqr2_t=wqr2_t,
            wdkv_t=wdkv_t, wuk_t=wuk_t, wuv_t=wuv_t,
            wkr1_t=wkr1_t, wkr2_t=wkr2_t, wo_t=wo_t,
            wg_n=wg_n, wu_n=wu_n, wd_n=wd_n,
            cosk=cosk, sink=sink,
            c8q=np.cos(ang).astype(np.float32),
            s8q=np.sin(ang).astype(np.float32),
            masks_in=_masks_for(half), consts=consts,
        ))

    nc = build_program()
    res = run_bass_kernel_spmd(nc, in_maps, core_ids=list(range(8)))
    full = np.empty((B, S, D), dtype=np.float32)
    for c in range(8):
        full[c // 2][qidx[c]] = np.asarray(res.results[c]["out"], dtype=np.float32)
    if DEBUG:
        kernel.debug_results = res.results
    return full


# version 28
# speedup vs baseline: 1.1071x; 1.0108x over previous
import os
import numpy as np
import ml_dtypes
import concourse.bass as bass
import concourse.mybir as mybir
from concourse.bass_utils import run_bass_kernel_spmd
from contextlib import ExitStack

F32 = mybir.dt.float32
BF16 = mybir.dt.bfloat16
AF = mybir.ActivationFunctionType
ALU = mybir.AluOpType
AX = mybir.AxisListType

B, S, D = 4, 4096, 1024
H, DH, DHR = 16, 64, 32
DLAT = 512
FI = 2816
NQ = 2048
QT = 512
NPOS = 4
EXT = (8, 16, 24, 32)
TILES_BY_POS = {0: [0, 3, 4, 7], 1: [1, 2, 5, 6]}
DQK = 96
EPS = 1e-5

SC = float(np.float32(1.0) / np.sqrt(np.float32(96.0)))
CLIP_HI = float(np.exp(np.float64(np.float32(80.0) * np.float32(SC))))
CLIP_LO = float(np.exp(np.float64(np.float32(-80.0) * np.float32(SC))))

DEBUG = os.environ.get("MK_DEBUG", "0") == "1"


class Pipe:
    ENGS = ("sync", "scalar", "vector", "tensor", "gpsimd")

    def __init__(self, nc, stack):
        self.nc = nc
        self.stack = stack
        self.ops = {e: [] for e in self.ENGS}
        self.csem = {e: stack.enter_context(nc.semaphore(f"C{e[:2]}")) for e in self.ENGS}
        self.ccnt = {e: 0 for e in self.ENGS}
        self.dsems = {}

    def _dsem(self, name):
        if name not in self.dsems:
            self.dsems[name] = [self.stack.enter_context(self.nc.semaphore(f"D{name}")), 0]
        return self.dsems[name]

    force_serial = False

    def op(self, engine, fn, deps=()):
        if self.force_serial:
            deps = list(deps) + self.barrier()
        self.ccnt[engine] += 1
        self.ops[engine].append((fn, tuple(deps), self.csem[engine], 1))
        return (self.csem[engine], self.ccnt[engine], engine)

    def dma(self, engine, fn, stream, deps=()):
        if self.force_serial:
            deps = list(deps) + self.barrier()
        s = self._dsem(stream)
        s[1] += 16
        self.ops[engine].append((fn, tuple(deps), s[0], 16))
        return (s[0], s[1], None)

    def last(self, engine):
        return (self.csem[engine], self.ccnt[engine], engine)

    def barrier(self):
        toks = []
        for e in self.ENGS:
            if self.ccnt[e]:
                toks.append((self.csem[e], self.ccnt[e], e))
        for name, (sem, cnt) in self.dsems.items():
            if cnt:
                toks.append((sem, cnt, None))
        return toks

    self_wait = False
    self_wait_lag = 1
    self_wait_engines = ("scalar", "vector", "gpsimd")

    def build(self, block):
        def body_for(name):
            def run(eng):
                waited = {}
                own = 0
                own_sem = self.csem[name]
                sw = self.self_wait and name in self.self_wait_engines
                lag = self.self_wait_lag
                for (fn, deps, sem, inc) in self.ops[name]:
                    if sw and own - lag + 1 > 0 and sem is own_sem:
                        eng.wait_ge(own_sem, own - lag + 1)
                    for (dsem, dval, deng) in deps:
                        if deng == name:
                            continue
                        k = id(dsem)
                        if waited.get(k, 0) >= dval:
                            continue
                        eng.wait_ge(dsem, dval)
                        waited[k] = dval
                    fn(eng).then_inc(sem, inc)
                    if sem is own_sem:
                        own += inc
            return run
        for name, meth in (("sync", block.sync), ("scalar", block.scalar),
                           ("vector", block.vector), ("tensor", block.tensor),
                           ("gpsimd", block.gpsimd)):
            if self.ops[name]:
                meth(body_for(name))


def build_program():
    nc = bass.Bass("TRN2", target_bir_lowering=False)
    DI = nc.dram_tensor

    x_full = DI("x_full", [S, D], F32, kind="ExternalInput").ap()
    xq_in = DI("xq_in", [NQ, D], F32, kind="ExternalInput").ap()
    wdq_t = DI("wdq_t", [D, DLAT], F32, kind="ExternalInput").ap()
    wuq_t = DI("wuq_t", [DLAT, D], F32, kind="ExternalInput").ap()
    wqr1_t = DI("wqr1_t", [DLAT, 256], F32, kind="ExternalInput").ap()
    wqr2_t = DI("wqr2_t", [DLAT, 256], F32, kind="ExternalInput").ap()
    wdkv_t = DI("wdkv_t", [D, DLAT], F32, kind="ExternalInput").ap()
    wuk_t = DI("wuk_t", [DLAT, D], F32, kind="ExternalInput").ap()
    wuv_t = DI("wuv_t", [DLAT, D], F32, kind="ExternalInput").ap()
    wkr1_t = DI("wkr1_t", [D, 16], F32, kind="ExternalInput").ap()
    wkr2_t = DI("wkr2_t", [D, 16], F32, kind="ExternalInput").ap()
    wo_t = DI("wo_t", [D, D], F32, kind="ExternalInput").ap()
    wg_n = DI("wg_n", [FI, D], F32, kind="ExternalInput").ap()
    wu_n = DI("wu_n", [FI, D], F32, kind="ExternalInput").ap()
    wd_n = DI("wd_n", [D, FI], F32, kind="ExternalInput").ap()
    cosk = DI("cosk", [16, S], F32, kind="ExternalInput").ap()
    sink = DI("sink", [16, S], F32, kind="ExternalInput").ap()
    c8q = DI("c8q", [128, NQ], F32, kind="ExternalInput").ap()
    s8q = DI("s8q", [128, NQ], F32, kind="ExternalInput").ap()
    masks_in = DI("masks_in", [NPOS, 8, 128, QT], F32, kind="ExternalInput").ap()
    consts = DI("consts", [128, 16], F32, kind="ExternalInput").ap()

    out = DI("out", [NQ, D], F32, kind="ExternalOutput").ap()

    skind = "ExternalOutput" if DEBUG else "Internal"
    kscr = DI("kscr", [D, S], BF16, kind=skind).ap()
    krscr = DI("krscr", [32, S], BF16, kind=skind).ap()
    vscr = DI("vscr", [D, S], BF16, kind="Internal").ap()
    vtok = DI("vtok", [S, D], BF16, kind=skind).ap()
    qscr = DI("qscr", [H, DQK, NQ], BF16, kind=skind).ap()
    x2scr = DI("x2scr", [NQ, D], F32, kind=skind).ap()
    xlnscr = DI("xlnscr", [NQ, D], F32, kind="Internal").ap()
    xnscr = DI("xnscr", [2, 512, D], BF16, kind="Internal").ap()
    xqscr = DI("xqscr", [2, 128, D], BF16, kind="Internal").ap()
    hqscr = DI("hqscr", [2, 128, FI], BF16, kind="Internal").ap()
    rcpscr = DI("rcpscr", [1, QT], F32, kind="Internal").ap()
    wgq_n = DI("wgq_n", [FI, D], BF16, kind=skind).ap()
    wuq_n = DI("wuq_n", [FI, D], BF16, kind="Internal").ap()
    wdq_n = DI("wdq_n", [D, FI], BF16, kind="Internal").ap()

    stP = ExitStack()
    with stP:
        s_c = stP.enter_context(nc.sbuf_tensor("s_c", [128, 16], F32))
        s_id = stP.enter_context(nc.sbuf_tensor("s_id", [128, 128], F32))
        block = stP.enter_context(nc.Block())
        pr = Pipe(nc, stP)
        pr.force_serial = os.environ.get("MK_SER", "") == "all"
        pr.self_wait = os.environ.get("MK_SE", "1") == "1"
        pr.self_wait_lag = int(os.environ.get("MK_SWD", "1"))
        pr.self_wait_engines = tuple(os.environ.get("MK_SWE", "vector,gpsimd").split(","))
        OP, DMA = pr.op, pr.dma

        tok_c = DMA("sync", lambda e: e.dma_start(out=s_c[:], in_=consts), "cst")
        OP("gpsimd", lambda e: e.memset(s_id[:], 0.0))
        tok_id = OP("gpsimd", lambda e: e.affine_select(
            out=s_id[:], in_=s_id[:], compare_op=ALU.not_equal, fill=1.0,
            base=0, pattern=[[-1, 128]], channel_multiplier=1))

        stA = ExitStack()
        s_x = [stA.enter_context(nc.sbuf_tensor(f"s_x{i}", [128, 4, D], F32)) for i in range(2)]
        s_xn = stA.enter_context(nc.sbuf_tensor("s_xn", [128, 4, D], BF16))
        s_xnf = [stA.enter_context(nc.sbuf_tensor(f"s_xnf{i}", [128, 8, 512], BF16,
                                                  is_transpose_dest=True)) for i in range(2)]
        s_wdq = stA.enter_context(nc.sbuf_tensor("s_wdq", [128, 8, DLAT], BF16))
        s_wuq = stA.enter_context(nc.sbuf_tensor("s_wuq", [128, 4, D], BF16))
        s_wqr1 = stA.enter_context(nc.sbuf_tensor("s_wqr1", [128, 4, 256], BF16))
        s_wqr2 = stA.enter_context(nc.sbuf_tensor("s_wqr2", [128, 4, 256], BF16))
        s_wdkv = stA.enter_context(nc.sbuf_tensor("s_wdkv", [128, 8, DLAT], BF16))
        s_wuk = stA.enter_context(nc.sbuf_tensor("s_wuk", [128, 4, D], BF16))
        s_wuv = stA.enter_context(nc.sbuf_tensor("s_wuv", [128, 4, D], BF16))
        s_wkr1 = stA.enter_context(nc.sbuf_tensor("s_wkr1", [128, 8, 16], BF16))
        s_wkr2 = stA.enter_context(nc.sbuf_tensor("s_wkr2", [128, 8, 16], BF16))
        s_ckv = stA.enter_context(nc.sbuf_tensor("s_ckv", [128, 4, 512], BF16))
        s_tmp = [stA.enter_context(nc.sbuf_tensor(f"s_tmpA{i}", [128, 512], BF16)) for i in range(2)]
        s_cs = stA.enter_context(nc.sbuf_tensor("s_cs", [48, S], F32))
        s_c8q = stA.enter_context(nc.sbuf_tensor("s_c8q", [128, NQ], F32))
        s_s8q = stA.enter_context(nc.sbuf_tensor("s_s8q", [128, NQ], F32))
        s_bn6 = [stA.enter_context(nc.sbuf_tensor(f"s_bn6{i}", [128, 12], F32)) for i in range(2)]
        s_mv = [stA.enter_context(nc.sbuf_tensor(f"s_mv{i}", [128, 2], F32)) for i in range(2)]
        s_sd = [stA.enter_context(nc.sbuf_tensor(f"s_sd{i}", [128, 1], F32)) for i in range(2)]
        s_rsd = [stA.enter_context(nc.sbuf_tensor(f"s_rsd{i}", [128, 1], F32)) for i in range(2)]
        s_ta = stA.enter_context(nc.sbuf_tensor("s_ta", [128, 512], F32))
        s_tb = stA.enter_context(nc.sbuf_tensor("s_tb", [128, 512], F32))
        s_ra = stA.enter_context(nc.sbuf_tensor("s_ra", [128, 512], BF16))
        s_rb = stA.enter_context(nc.sbuf_tensor("s_rb", [128, 512], BF16))
        pM = [stA.enter_context(nc.psum_tensor(f"pM{i}", [128, 512], F32)) for i in range(2)]
        pQ1 = stA.enter_context(nc.psum_tensor("pQ1", [128, 512], F32))
        pQ2 = stA.enter_context(nc.psum_tensor("pQ2", [128, 512], F32))


        def wload(dst, src, pat, **kw):
            return DMA("gpsimd", lambda e: e.dma_start(out=dst[:], in_=src.rearrange(pat, **kw)), "wl")
        wload_all = [
            wload(s_wdq, wdq_t, "(fc p) l -> p fc l", p=128),
            wload(s_wuq, wuq_t, "(lc p) o -> p lc o", p=128),
            wload(s_wqr1, wqr1_t, "(lc p) o -> p lc o", p=128),
            wload(s_wqr2, wqr2_t, "(lc p) o -> p lc o", p=128),
            wload(s_wdkv, wdkv_t, "(fc p) l -> p fc l", p=128),
            wload(s_wuk, wuk_t, "(lc p) o -> p lc o", p=128),
            wload(s_wuv, wuv_t, "(lc p) o -> p lc o", p=128),
            wload(s_wkr1, wkr1_t, "(fc p) o -> p fc o", p=128),
            wload(s_wkr2, wkr2_t, "(fc p) o -> p fc o", p=128),
            DMA("sync", lambda e: e.dma_start(out=s_cs[0:16, :], in_=cosk), "tbl"),
            DMA("sync", lambda e: e.dma_start(out=s_cs[32:48, :], in_=sink), "tbl"),
            DMA("sync", lambda e: e.dma_start(out=s_c8q[:], in_=c8q), "tbl"),
            DMA("sync", lambda e: e.dma_start(out=s_s8q[:], in_=s8q), "tbl"),
        ]

        def emit_quant(x_ap, q_ap, Fw, t, dep_in):
            g = emit_quant_gen(x_ap, q_ap, Fw, t, dep_in)
            r = None
            while True:
                try:
                    r = g.send(None) or r
                except StopIteration:
                    break
            return r

        def emit_quant_gen(x_ap, q_ap, Fw, t, dep_in):
            nb = Fw // 16
            P_ = x_ap.shape[0] if hasattr(x_ap, "shape") else 128
            P_ = x_ap.shape[0]
            x3 = x_ap.rearrange("p (b w) -> p b w", w=16)
            t_amx = OP("vector", lambda e: e.tensor_reduce(
                out=t["am"][0:P_, 0:nb], in_=x3, axis=AX.X, op=ALU.max,
                apply_absolute_value=True), deps=dep_in)
            yield
            OP("vector", lambda e: e.tensor_scalar(
                out=t["sc"][0:P_, 0:nb], in0=t["am"][0:P_, 0:nb], scalar1=1.0 / 12.0,
                scalar2=5e-13, op0=ALU.mult, op1=ALU.max), deps=[t_amx])
            t_r2 = OP("vector", lambda e: e.reciprocal(
                out=t["rs"][0:P_, 0:nb], in_=t["sc"][0:P_, 0:nb]))
            yield
            rsb = t["rs"][0:P_, 0:nb].rearrange("p (b one) -> p b one", one=1).broadcast_to([P_, nb, 16])
            scb = t["sc"][0:P_, 0:nb].rearrange("p (b one) -> p b one", one=1).broadcast_to([P_, nb, 16])
            au = t["au"][0:P_, 0:Fw]
            au3 = au.rearrange("p (b w) -> p b w", w=16)
            t_abs = OP("scalar", lambda e: e.activation(out=au, in_=x_ap, func=AF.Abs),
                       deps=dep_in)
            yield
            OP("vector", lambda e: e.tensor_tensor(out=au3, in0=au3, in1=rsb, op=ALU.mult),
               deps=[t_abs, t_r2])
            yield
            L = t["L"][0:P_, 0:Fw]
            t1 = t["t1"][0:P_, 0:Fw]

            def cmp(dst, thr):
                OP("vector", lambda e, dst=dst, thr=thr: e.tensor_scalar(
                    out=dst, in0=au, scalar1=float(thr), scalar2=None, op0=ALU.is_gt))
            cmp(L, 0.5)
            yield
            for thr in (1.5, 2.5, 3.5):
                cmp(t1, thr)
                yield
                OP("vector", lambda e: e.tensor_add(out=L, in0=L, in1=t1))
                yield
            for thr, w in ((5.0, 2.0), (7.0, 2.0), (10.0, 4.0)):
                cmp(t1, thr)
                yield
                OP("vector", lambda e, w=w: e.scalar_tensor_tensor(
                    out=L, in0=t1, scalar=w, in1=L, op0=ALU.mult, op1=ALU.add))
                yield
            t_lastL = pr.last("vector")
            tsg = OP("scalar", lambda e: e.activation(out=t1, in_=x_ap, func=AF.Sign),
                     deps=[t_lastL])
            yield
            OP("vector", lambda e: e.tensor_mul(out=L, in0=L, in1=t1), deps=[tsg])
            yield
            tq = OP("vector", lambda e: e.tensor_tensor(
                out=q_ap.rearrange("p (b w) -> p b w", w=16),
                in0=L.rearrange("p (b w) -> p b w", w=16), in1=scb, op=ALU.mult))
            yield (tq, tsg)

        ln_exp_war = {0: (), 1: ()}

        def emit_ln_sub(x_ap, xn_ap, dep_in, lp):
            OP("vector", lambda e, x_ap=x_ap, lp=lp: e.bn_stats(
                out=s_bn6[lp][:, 0:6], in_=x_ap[:, 0:512]), deps=list(dep_in))
            OP("vector", lambda e, x_ap=x_ap, lp=lp: e.bn_stats(
                out=s_bn6[lp][:, 6:12], in_=x_ap[:, 512:1024]))
            t2_ = OP("vector", lambda e, lp=lp: e.bn_aggr(out=s_mv[lp][:], in_=s_bn6[lp][:]),
                     deps=list(ln_exp_war[lp]))
            t3_ = OP("scalar", lambda e, lp=lp: e.activation(
                out=s_sd[lp][:], in_=s_mv[lp][:, 1:2], func=AF.Ln, bias=s_c[:, 1:2]),
                deps=[t2_, tok_c])
            t4_ = OP("scalar", lambda e, lp=lp: e.activation(
                out=s_rsd[lp][:], in_=s_sd[lp][:], func=AF.Exp, scale=s_c[:, 10:11]))
            ln_exp_war[lp] = [t4_]
            tn = OP("vector", lambda e, x_ap=x_ap, xn_ap=xn_ap, lp=lp: e.tensor_scalar(
                out=xn_ap, in0=x_ap, scalar1=s_mv[lp][:, 0:1], scalar2=s_rsd[lp][:],
                op0=ALU.subtract, op1=ALU.mult), deps=[t4_])
            return tn

        xload_free = {0: (), 1: ()}
        xnf_free = {0: (), 1: ()}
        xnscr_free = {0: (), 1: ()}
        xn_free = [()]
        evac_tok = {0: None, 1: None}
        tmp_free = {0: (), 1: ()}
        kv_store_toks = []
        q_store_toks = []
        rope_war = [()]
        mm_g = [0]

        def mm_group(lhs_list, rhs_list, extra_deps):
            g = mm_g[0]
            mm_g[0] += 1
            p = pM[g % 2]
            dep_war = [evac_tok[g % 2]] if evac_tok[g % 2] else []
            n = len(lhs_list)
            tok = None
            for i, (lf, rf) in enumerate(zip(lhs_list, rhs_list)):
                d = (list(extra_deps) + dep_war) if i == 0 else ()
                tok = OP("tensor", lambda e, lf=lf, rf=rf, i=i, n=n, p=p: e.matmul(
                    p[:, :], lf, rf, start=(i == 0), stop=(i == n - 1)), deps=d)
            return g, p, tok

        def stageA1(ti):
            par = ti % 2
            is_q = ti >= 8
            src_ = x_full[(ti * 512):(ti + 1) * 512, :] if not is_q else \
                xq_in[(ti - 8) * 512:(ti - 7) * 512, :]
            xl = DMA("sync", lambda e, src_=src_, par=par: e.dma_start(
                out=s_x[par][:], in_=src_.rearrange("(s p) d -> p s d", p=128)),
                f"xl{par}", deps=xload_free[par])
            last_ln = None
            for sub in range(4):
                d = [xl] + (list(xn_free[0]) if sub == 0 else [])
                last_ln = emit_ln_sub(s_x[par][:, sub, :], s_xn[:, sub, :], d, sub % 2)
            xload_free[par] = [last_ln]
            xst = DMA("gpsimd", lambda e, par=par: e.dma_start(
                out=xnscr[par].rearrange("(s p) d -> p s d", p=128), in_=s_xn[:]),
                f"xns{par}", deps=[last_ln] + list(xnscr_free[par]))
            xn_free[0] = [xst]
            xt = DMA("sync", lambda e, par=par: e.dma_start(
                out=s_xnf[par][:], in_=xnscr[par], transpose=True),
                f"xnt{par}", deps=[xst] + list(xnf_free[par]))
            xnscr_free[par] = [xt]
            return xt

        def stageA2(ti, xt):
            par = ti % 2
            is_q = ti >= 8
            t0 = (ti * 512) if not is_q else (ti - 8) * 512
            deps0 = [xt] + (wload_all if ti == 0 else [])
            wdown = s_wdkv if not is_q else s_wdq
            ck_toks = []
            for oc in range(4):
                g, p, tk = mm_group(
                    [wdown[:, fc, oc * 128:(oc + 1) * 128] for fc in range(8)],
                    [s_xnf[par][:, fc, :] for fc in range(8)], deps0)
                te = OP("scalar", lambda e, p=p, oc=oc: e.copy(out=s_ckv[:, oc, :], in_=p[:, :]),
                        deps=[tk])
                evac_tok[g % 2] = te
                ck_toks.append(te)
            ck_dep = [ck_toks[-1]]
            if not is_q:
                tkr = None
                for half_i, wkr in ((0, s_wkr1), (1, s_wkr2)):
                    rows = slice(0, 16) if half_i == 0 else slice(32, 48)
                    for fc in range(8):
                        d = (deps0 + list(rope_war[0])) if (half_i == 0 and fc == 0) else ()
                        tkr = OP("tensor", lambda e, fc=fc, rows=rows, wkr=wkr, par=par: e.matmul(
                            pQ1[rows, :], wkr[:, fc, :], s_xnf[par][:, fc, :],
                            start=(fc == 0), stop=(fc == 7)), deps=d)
                ck_ = s_cs[0:16, t0:t0 + 512]
                sk_ = s_cs[32:48, t0:t0 + 512]
                OP("vector", lambda e, ck_=ck_: e.tensor_mul(out=s_ta[0:16, :], in0=pQ1[0:16, :], in1=ck_),
                   deps=[tkr] + list(rope_war[0]))
                OP("vector", lambda e, sk_=sk_: e.tensor_mul(out=s_tb[0:16, :], in0=pQ1[32:48, :], in1=sk_))
                OP("vector", lambda e: e.tensor_sub(out=s_ra[0:16, :], in0=s_ta[0:16, :], in1=s_tb[0:16, :]))
                OP("vector", lambda e, sk_=sk_: e.tensor_mul(out=s_ta[0:16, :], in0=pQ1[0:16, :], in1=sk_))
                OP("vector", lambda e, ck_=ck_: e.tensor_mul(out=s_tb[0:16, :], in0=pQ1[32:48, :], in1=ck_))
                trb = OP("vector", lambda e: e.tensor_add(out=s_rb[0:16, :], in0=s_ta[0:16, :], in1=s_tb[0:16, :]))
                d1 = DMA("gpsimd", lambda e, t0=t0: e.dma_start(
                    out=krscr[0:16, t0:t0 + 512], in_=s_ra[0:16, :]), "krst", deps=[trb])
                d2 = DMA("gpsimd", lambda e, t0=t0: e.dma_start(
                    out=krscr[16:32, t0:t0 + 512], in_=s_rb[0:16, :]), "krst", deps=[trb])
                rope_war[0] = [d1, d2]
                kv_store_toks.extend([d1, d2])
                for (wsb, scr, stream) in ((s_wuk, kscr, "kst"), (s_wuv, vscr, "vst")):
                    for oc8 in range(8):
                        g, p, tk = mm_group(
                            [wsb[:, lc, oc8 * 128:(oc8 + 1) * 128] for lc in range(4)],
                            [s_ckv[:, lc, :] for lc in range(4)], ck_dep)
                        te = OP("scalar", lambda e, p=p, g=g: e.copy(out=s_tmp[g % 2][:], in_=p[:, :]),
                                deps=[tk] + list(tmp_free[g % 2]))
                        evac_tok[g % 2] = te
                        td = DMA("gpsimd", lambda e, scr=scr, oc8=oc8, t0=t0, g=g: e.dma_start(
                            out=scr[oc8 * 128:(oc8 + 1) * 128, t0:t0 + 512], in_=s_tmp[g % 2][:]),
                            stream, deps=[te])
                        tmp_free[g % 2] = [td]
                        kv_store_toks.append(td)
            else:
                for oc2 in range(2):
                    tq1 = None
                    for pnum, wq_, pp in ((0, s_wqr1, pQ1), (1, s_wqr2, pQ2)):
                        for lc in range(4):
                            d = (ck_dep + list(rope_war[0])) if (pnum == 0 and lc == 0) else ()
                            tq1 = OP("tensor", lambda e, oc2=oc2, lc=lc, wq_=wq_, pp=pp: e.matmul(
                                pp[:, :], wq_[:, lc, oc2 * 128:(oc2 + 1) * 128], s_ckv[:, lc, :],
                                start=(lc == 0), stop=(lc == 3)), deps=d)
                    cq_ = s_c8q[:, t0:t0 + 512]
                    sq_ = s_s8q[:, t0:t0 + 512]
                    OP("vector", lambda e, cq_=cq_: e.tensor_mul(out=s_ta[:], in0=pQ1[:, :], in1=cq_),
                       deps=[tq1] + list(rope_war[0]))
                    OP("vector", lambda e, sq_=sq_: e.tensor_mul(out=s_tb[:], in0=pQ2[:, :], in1=sq_))
                    OP("vector", lambda e: e.tensor_sub(out=s_ra[:], in0=s_ta[:], in1=s_tb[:]))
                    OP("vector", lambda e, sq_=sq_: e.tensor_mul(out=s_ta[:], in0=pQ1[:, :], in1=sq_))
                    OP("vector", lambda e, cq_=cq_: e.tensor_mul(out=s_tb[:], in0=pQ2[:, :], in1=cq_))
                    trb = OP("vector", lambda e: e.tensor_add(out=s_rb[:], in0=s_ta[:], in1=s_tb[:]))
                    d1 = DMA("gpsimd", lambda e, oc2=oc2, t0=t0: e.dma_start(
                        out=qscr[oc2 * 8:(oc2 + 1) * 8, 64:80, t0:t0 + 512], in_=s_ra[:]),
                        "qst", deps=[trb])
                    d2 = DMA("gpsimd", lambda e, oc2=oc2, t0=t0: e.dma_start(
                        out=qscr[oc2 * 8:(oc2 + 1) * 8, 80:96, t0:t0 + 512], in_=s_rb[:]),
                        "qst", deps=[trb])
                    rope_war[0] = [d1, d2]
                    q_store_toks.extend([d1, d2])
                for oc8 in range(8):
                    g, p, tk = mm_group(
                        [s_wuq[:, lc, oc8 * 128:(oc8 + 1) * 128] for lc in range(4)],
                        [s_ckv[:, lc, :] for lc in range(4)], ck_dep)
                    te = OP("scalar", lambda e, p=p, g=g: e.copy(out=s_tmp[g % 2][:], in_=p[:, :]),
                            deps=[tk] + list(tmp_free[g % 2]))
                    evac_tok[g % 2] = te
                    td = DMA("gpsimd", lambda e, oc8=oc8, t0=t0, g=g: e.dma_start(
                        out=qscr[2 * oc8:2 * oc8 + 2, 0:64, t0:t0 + 512], in_=s_tmp[g % 2][:]),
                        "qst", deps=[te])
                    tmp_free[g % 2] = [td]
                    q_store_toks.append(td)
            xnf_free[par] = [pr.last("tensor")]
            

        xt_of = {}
        for ti in range(13):
            if ti < 12:
                xt_of[ti] = stageA1(ti)
            if ti >= 1:
                stageA2(ti - 1, xt_of.pop(ti - 1))

        barA = pr.barrier()
        stA.close()

        stV = ExitStack()
        s_vtmp = [stV.enter_context(nc.sbuf_tensor(f"s_vtmp{i}", [128, D], BF16,
                                                   is_transpose_dest=True)) for i in range(2)]
        vt_prev = {0: barA, 1: barA}
        vtok_toks = []
        for kc in range(32):
            pv_ = kc % 2
            tl = DMA("sync", lambda e, kc=kc, pv_=pv_: e.dma_start(
                out=s_vtmp[pv_][:], in_=vscr[:, kc * 128:(kc + 1) * 128], transpose=True),
                f"vtt{pv_}", deps=vt_prev[pv_])
            ts = DMA("gpsimd", lambda e, kc=kc, pv_=pv_: e.dma_start(
                out=vtok[kc * 128:(kc + 1) * 128, :], in_=s_vtmp[pv_][:]), "vts", deps=[tl])
            vt_prev[pv_] = [ts]
            vtok_toks.append(ts)
        stV.close()

        stB = ExitStack()
        s_k = [stB.enter_context(nc.sbuf_tensor(f"s_k{i}", [96, S], BF16)) for i in range(2)]
        s_q = [stB.enter_context(nc.sbuf_tensor(f"s_q{i}", [96, NQ], BF16)) for i in range(2)]
        s_v1 = [stB.enter_context(nc.sbuf_tensor(f"s_v1{i}", [128, 32, 65], BF16)) for i in range(2)]
        s_p = [stB.enter_context(nc.sbuf_tensor(f"s_p{i}", [128, 1024], BF16)) for i in range(3)]
        s_attn = stB.enter_context(nc.sbuf_tensor("s_attn", [64, H, NQ], BF16))
        s_masks = stB.enter_context(nc.sbuf_tensor("s_masks", [128, NPOS, 8, QT], BF16))
        s_rcp = stB.enter_context(nc.sbuf_tensor("s_rcp", [1, QT], F32))
        s_bc = stB.enter_context(nc.sbuf_tensor("s_bc", [64, QT], F32))
        s_ao = stB.enter_context(nc.sbuf_tensor("s_ao", [128, 8, QT], F32))

        stBq = ExitStack()
        WQF = 2816
        s_wf = stBq.enter_context(nc.sbuf_tensor("s_wf", [128, WQF], F32))
        s_am = stBq.enter_context(nc.sbuf_tensor("s_am", [128, WQF // 16], F32))
        s_sc = stBq.enter_context(nc.sbuf_tensor("s_sc", [128, WQF // 16], F32))
        s_rs = stBq.enter_context(nc.sbuf_tensor("s_rs", [128, WQF // 16], F32))
        s_au = stBq.enter_context(nc.sbuf_tensor("s_au", [128, WQF], F32))
        s_L = stBq.enter_context(nc.sbuf_tensor("s_L", [128, WQF], BF16))
        s_t1 = stBq.enter_context(nc.sbuf_tensor("s_t1", [128, WQF], BF16))
        s_qq = stBq.enter_context(nc.sbuf_tensor("s_qq", [128, WQF], BF16))
        tQ = dict(am=s_am, sc=s_sc, rs=s_rs, au=s_au, L=s_L, t1=s_t1)

        wq_batches = []
        for (src_, dst_, nch, Fw) in ((wg_n, wgq_n, FI // 128, D),
                                      (wu_n, wuq_n, FI // 128, D),
                                      (wd_n, wdq_n, D // 128, FI)):
            per = WQF // Fw
            ch = 0
            while ch < nch:
                n_ = min(per, nch - ch)
                wq_batches.append((src_, dst_, [128 * (ch + i) for i in range(n_)], Fw))
                ch += n_
        wq_prev = [()]
        wq_store_toks = []

        def wq_driver():
            for (src_, dst_, r0s, Fw) in wq_batches:
                tl = None
                for i, r0 in enumerate(r0s):
                    tl = DMA("sync", lambda e, src_=src_, r0=r0, i=i, Fw=Fw: e.dma_start(
                        out=s_wf[:, i * Fw:(i + 1) * Fw], in_=src_[r0:r0 + 128, :]), "wf",
                        deps=wq_prev[0] if i == 0 else ())
                Ft = Fw * len(r0s)
                res = None
                gen = emit_quant_gen(s_wf[:, 0:Ft], s_qq[:, 0:Ft], Ft, tQ, [tl])
                while True:
                    try:
                        r = gen.send(None)
                        if r is not None:
                            res = r
                        yield
                    except StopIteration:
                        break
                tq, tsg = res
                ts = None
                for i, r0 in enumerate(r0s):
                    ts = DMA("gpsimd", lambda e, dst_=dst_, r0=r0, i=i, Fw=Fw: e.dma_start(
                        out=dst_[r0:r0 + 128, :], in_=s_qq[:, i * Fw:(i + 1) * Fw]), "wqs",
                        deps=[tq] if i == 0 else ())
                wq_prev[0] = [tq, tsg, ts]
                wq_store_toks.append(ts)
                yield

        wq_it = [wq_driver()]

        def emit_wq_some(n):
            for _ in range(n):
                try:
                    next(wq_it[0])
                except StopIteration:
                    return

        stBp = ExitStack()
        pAT = [stBp.enter_context(nc.psum_tensor(f"pAT{i}", [128, 512], F32)) for i in range(2)]
        pS = [stBp.enter_context(nc.psum_tensor(f"pS{i}", [128, 1024], F32)) for i in range(3)]

        tmask = DMA("gpsimd", lambda e: e.dma_start(
            out=s_masks[:], in_=masks_in.rearrange("a b p q -> p a b q")), "bmk", deps=barA)
        tv1 = [OP("gpsimd", lambda e, i=i: e.memset(s_v1[i][:], 1.0), deps=barA) for i in range(2)]

        kv_bar = [kv_store_toks[-1], q_store_toks[-1], vtok_toks[-1]]
        head_free = {0: [tv1[0]] + kv_bar, 1: [tv1[1]] + kv_bar}
        head_toks = {}

        def load_head(h, par, deps):
            DMA("sync", lambda e, h=h, par=par: e.dma_start(
                out=s_k[par][0:64, :], in_=kscr[h * 64:(h + 1) * 64, :]), f"bk{par}", deps=deps)
            DMA("sync", lambda e, par=par: e.dma_start(out=s_k[par][64:96, :], in_=krscr), f"bk{par}")
            DMA("sync", lambda e, h=h, par=par: e.dma_start(out=s_q[par][:], in_=qscr[h]), f"bk{par}")
            t4_ = DMA("sync", lambda e, h=h, par=par: e.dma_start(
                out=s_v1[par][:, :, 0:64],
                in_=vtok[:, h * 64:(h + 1) * 64].rearrange("(kc p) o -> p kc o", p=128)), f"bk{par}")
            return [t4_]

        head_toks[0] = load_head(0, 0, head_free[0])

        pS_free = {0: (), 1: (), 2: ()}
        pP_free = {0: (), 1: (), 2: ()}
        pAT_free = {0: (), 1: ()}
        bc_free = [()]
        bc2_free = [()]
        bc3_free = [()]
        at_i = [0]

        for h in range(16):
            par = h % 2
            if h + 1 < 16:
                head_toks[h + 1] = load_head(h + 1, (h + 1) % 2, head_free[(h + 1) % 2])
            kload = head_toks[h]
            last_S = None
            last_PV = None
            groups = []
            for pos in range(NPOS):
                ai = at_i[0] % 2
                at_i[0] += 1
                ng = EXT[pos] // 2
                for g in range(ng):
                    groups.append((pos, g, ai))
            LOOK = 2
            gi = [0]
            tm_of = {}

            def emit_S(idx):
                pos, g, ai = groups[idx]
                ext = EXT[pos]
                qsl = slice(pos * QT, (pos + 1) * QT)
                gp = idx % 3
                st_ = None
                for kk in range(2):
                    kc = 2 * g + kk
                    d = (list(pS_free[gp]) + list(kload)) if kk == 0 else ()
                    st_ = OP("tensor", lambda e, kc=kc, gp=gp, kk=kk, par=par, qsl=qsl: e.matmul(
                        pS[gp][:, kk * 512:(kk + 1) * 512],
                        s_k[par][:, kc * 128:(kc + 1) * 128], s_q[par][:, qsl],
                        start=True, stop=True), deps=d)
                texp = OP("scalar", lambda e, gp=gp: e.activation(
                    out=s_p[gp][:], in_=pS[gp][:, :], func=AF.Exp, scale=s_c[:, 0:1]),
                    deps=[st_, tok_c] + list(pP_free[gp]))
                pS_free[gp] = [texp]
                tm = OP("vector", lambda e, gp=gp: e.tensor_scalar(
                    out=s_p[gp][:], in0=s_p[gp][:], scalar1=CLIP_HI, scalar2=CLIP_LO,
                    op0=ALU.min, op1=ALU.max), deps=[texp])
                j0 = 2 * g - (ext - 8)
                if j0 >= 0:
                    tm = OP("vector", lambda e, gp=gp, pos=pos, j0=j0: e.tensor_mul(
                        out=s_p[gp][:], in0=s_p[gp][:],
                        in1=s_masks[:, pos, j0:j0 + 2, :].rearrange("p a q -> p (a q)")),
                        deps=[tmask])
                tm_of[idx] = tm
                return st_

            def emit_PV(idx):
                pos, g, ai = groups[idx]
                ext = EXT[pos]
                gp = idx % 3
                tm = tm_of.pop(idx)
                pv_tok = None
                for kk in range(2):
                    kc = 2 * g + kk
                    d = [tm] if kk == 0 else ()
                    if kc == 0:
                        d = [tm] + list(pAT_free[ai])
                    pv_tok = OP("tensor", lambda e, kc=kc, gp=gp, kk=kk, par=par, ai=ai, ext=ext: e.matmul(
                        pAT[ai][0:65, :], s_v1[par][:, kc, :],
                        s_p[gp][:, kk * 512:(kk + 1) * 512],
                        start=(kc == 0), stop=(kc == ext - 1)), deps=d)
                pP_free[gp] = [pv_tok]
                if 2 * g + 1 == ext - 1:
                    emit_divide(pos, ai, pv_tok)
                return pv_tok

            def emit_divide(pos, ai, pv_tok):
                qsl = slice(pos * QT, (pos + 1) * QT)
                trc0 = OP("scalar", lambda e, ai=ai: e.activation(
                    out=s_rcp[:], in_=pAT[ai][64:65, :], func=AF.Ln),
                    deps=[pv_tok] + list(bc_free[0]))
                trc = OP("scalar", lambda e: e.activation(
                    out=s_rcp[:], in_=s_rcp[:], func=AF.Exp, scale=s_c[0:1, 11:12]),
                    deps=[tok_c])
                tst = DMA("gpsimd", lambda e: e.dma_start(out=rcpscr, in_=s_rcp[:]),
                          "rcs", deps=[trc] + list(bc2_free[0]))
                bc_free[0] = [tst]
                tld = DMA("gpsimd", lambda e: e.dma_start(
                    out=s_bc[:], in_=rcpscr.broadcast_to([64, QT])), "rcl",
                    deps=[tst] + list(bc3_free[0]))
                bc2_free[0] = [tld]
                tdv = OP("vector", lambda e, ai=ai, h=h, qsl=qsl: e.tensor_mul(
                    out=s_attn[:, h, qsl], in0=pAT[ai][0:64, :], in1=s_bc[:]), deps=[tld])
                pAT_free[ai] = [tdv]
                bc3_free[0] = [tdv]

            n_g = len(groups)
            for idx in range(n_g + LOOK):
                if idx < n_g:
                    last_S = emit_S(idx)
                if idx >= LOOK:
                    last_PV = emit_PV(idx - LOOK)
                emit_wq_some(1)
            head_free[par] = [last_S, last_PV]
            if h == 15:
                emit_wq_some(9999)

        div_last = pr.last("vector")
        stBp.close()
        stBq.close()
        stBW = ExitStack()
        s_xq4 = [stBW.enter_context(nc.sbuf_tensor(f"s_xq4{i}", [128, D], F32)) for i in range(2)]
        s_x2t = [stBW.enter_context(nc.sbuf_tensor(f"s_x2t{i}", [128, D], F32)) for i in range(2)]
        s_xlnW = stBW.enter_context(nc.sbuf_tensor("s_xlnW", [128, D], F32))
        s_bnW = [stBW.enter_context(nc.sbuf_tensor(f"s_bnW{i}", [128, 12], F32)) for i in range(2)]
        s_mvW = [stBW.enter_context(nc.sbuf_tensor(f"s_mvW{i}", [128, 2], F32)) for i in range(2)]
        s_sdW = [stBW.enter_context(nc.sbuf_tensor(f"s_sdW{i}", [128, 1], F32)) for i in range(2)]
        s_rsdW = [stBW.enter_context(nc.sbuf_tensor(f"s_rsdW{i}", [128, 1], F32)) for i in range(2)]
        s_wo = stBW.enter_context(nc.sbuf_tensor("s_wo", [64, H, D], BF16))
        lnW_war = {0: (), 1: ()}
        xlnW_free = [()]
        xln_store = []

        def emit_lnW(x_ap, tt, lp):
            OP("vector", lambda e, x_ap=x_ap, lp=lp: e.bn_stats(
                out=s_bnW[lp][:, 0:6], in_=x_ap[:, 0:512]))
            OP("vector", lambda e, x_ap=x_ap, lp=lp: e.bn_stats(
                out=s_bnW[lp][:, 6:12], in_=x_ap[:, 512:1024]))
            t2_ = OP("vector", lambda e, lp=lp: e.bn_aggr(out=s_mvW[lp][:], in_=s_bnW[lp][:]),
                     deps=list(lnW_war[lp]))
            t3_ = OP("scalar", lambda e, lp=lp: e.activation(
                out=s_sdW[lp][:], in_=s_mvW[lp][:, 1:2], func=AF.Ln, bias=s_c[:, 1:2]),
                deps=[t2_, tok_c])
            t4_ = OP("scalar", lambda e, lp=lp: e.activation(
                out=s_rsdW[lp][:], in_=s_sdW[lp][:], func=AF.Exp, scale=s_c[:, 10:11]))
            lnW_war[lp] = [t4_]
            tn = OP("vector", lambda e, x_ap=x_ap, lp=lp: e.tensor_scalar(
                out=s_xlnW[:], in0=x_ap, scalar1=s_mvW[lp][:, 0:1], scalar2=s_rsdW[lp][:],
                op0=ALU.subtract, op1=ALU.mult), deps=[t4_] + list(xlnW_free[0]))
            ts_ = DMA("gpsimd", lambda e, tt=tt: e.dma_start(
                out=xlnscr[tt * 128:(tt + 1) * 128, :], in_=s_xlnW[:]), "xlns", deps=[tn])
            xlnW_free[0] = [ts_]
            xln_store.append(ts_)
            return tn
        two = DMA("gpsimd", lambda e: e.dma_start(
            out=s_wo[:], in_=wo_t.rearrange("(h p) d -> p h d", p=64)), "bmk",
            deps=list(barA) + list(wq_prev[0]))
        pW = [stBW.enter_context(nc.psum_tensor(f"pW{i}", [128, 512], F32)) for i in range(2)]
        pT = [stBW.enter_context(nc.psum_tensor(f"pTr{i}", [128, 1024], F32)) for i in range(2)]

        x2_store_toks = []
        wo_evac = {0: (), 1: ()}
        pT_free = {0: (), 1: ()}
        xq4_free = {0: (), 1: ()}
        x2t_free = {0: (), 1: ()}
        for qt in range(NPOS):
            qsl = slice(qt * QT, (qt + 1) * QT)
            ao_toks = []
            for dc in range(8):
                wp = dc % 2
                tk = None
                for hh in range(16):
                    d = ([div_last, two] + list(wo_evac[wp])) if hh == 0 else ()
                    tk = OP("tensor", lambda e, hh=hh, dc=dc, qsl=qsl, wp=wp: e.matmul(
                        pW[wp][:, :], s_wo[:, hh, dc * 128:(dc + 1) * 128], s_attn[:, hh, qsl],
                        start=(hh == 0), stop=(hh == 15)), deps=d)
                te = OP("scalar", lambda e, dc=dc, wp=wp: e.copy(out=s_ao[:, dc, :], in_=pW[wp][:, :]),
                        deps=[tk])
                wo_evac[wp] = [te]
                ao_toks.append(te)
            for sub in range(4):
                tp = sub % 2
                txq = DMA("sync", lambda e, qt=qt, sub=sub, tp=tp: e.dma_start(
                    out=s_xq4[tp][:],
                    in_=xq_in[qt * 512 + sub * 128:qt * 512 + (sub + 1) * 128, :]),
                    f"bxq{tp}", deps=list(xq4_free[tp]) + list(wq_prev[0]))
                tt_ = None
                for dc in range(8):
                    d = [ao_toks[dc], tok_id] + (list(pT_free[tp]) if dc == 0 else [])
                    tt_ = OP("tensor", lambda e, dc=dc, sub=sub, tp=tp: e.transpose(
                        out=pT[tp][:, dc * 128:(dc + 1) * 128],
                        in_=s_ao[:, dc, sub * 128:(sub + 1) * 128], identity=s_id[:]),
                        deps=d)
                ta = OP("vector", lambda e, tp=tp: e.tensor_add(
                    out=s_x2t[tp][:], in0=pT[tp][:, :], in1=s_xq4[tp][:]),
                    deps=[tt_, txq] + list(x2t_free[tp]))
                pT_free[tp] = [ta]
                xq4_free[tp] = [ta]
                ts = DMA("gpsimd", lambda e, qt=qt, sub=sub, tp=tp: e.dma_start(
                    out=x2scr[qt * 512 + sub * 128: qt * 512 + (sub + 1) * 128, :],
                    in_=s_x2t[tp][:]), "x2s", deps=[ta])
                tln = emit_lnW(s_x2t[tp][:], qt * 4 + sub, tp)
                x2t_free[tp] = [ts, tln]
                x2_store_toks.append(ts)

        barB = pr.barrier()
        stBW.close()
        stB.close()

        barC0 = pr.barrier()

        stC = ExitStack()
        s_wg = stC.enter_context(nc.sbuf_tensor("s_wg", [128, 8, FI], BF16, is_transpose_dest=True))
        s_wu = stC.enter_context(nc.sbuf_tensor("s_wu", [128, 8, FI], BF16, is_transpose_dest=True))
        s_wd = stC.enter_context(nc.sbuf_tensor("s_wd", [128, 22, D], BF16, is_transpose_dest=True))
        s_x2b = [stC.enter_context(nc.sbuf_tensor("s_x2b", [128, D], F32))] * 2
        s_xln = [stC.enter_context(nc.sbuf_tensor("s_xlnb", [128, D], F32))] * 2
        s_xq = stC.enter_context(nc.sbuf_tensor("s_xqC", [128, D], BF16))
        s_xqf = [stC.enter_context(nc.sbuf_tensor(f"s_xqf{i}", [128, 8, 128], BF16,
                                                  is_transpose_dest=True)) for i in range(2)]
        s_h = stC.enter_context(nc.sbuf_tensor("s_h", [128, FI], F32))
        s_hq = stC.enter_context(nc.sbuf_tensor("s_hq", [128, FI], BF16))
        s_hqT = [stC.enter_context(nc.sbuf_tensor(f"s_hqT{i}", [128, 22, 128], BF16,
                                                  is_transpose_dest=True)) for i in range(2)]
        s_sil = [stC.enter_context(nc.sbuf_tensor(f"s_sil{i}", [128, 512], F32)) for i in range(2)]
        s_outt = [stC.enter_context(nc.sbuf_tensor("s_outt", [128, D], F32))] * 2
        s_am2 = stC.enter_context(nc.sbuf_tensor("s_am2", [128, FI // 16], F32))
        s_sc2 = stC.enter_context(nc.sbuf_tensor("s_sc2", [128, FI // 16], F32))
        s_rs2 = stC.enter_context(nc.sbuf_tensor("s_rs2", [128, FI // 16], F32))
        s_au2 = stC.enter_context(nc.sbuf_tensor("s_au2", [128, FI], F32))
        s_L2 = stC.enter_context(nc.sbuf_tensor("s_L2", [128, FI], BF16))
        s_t12 = stC.enter_context(nc.sbuf_tensor("s_t12", [128, FI], BF16))
        pG = [stC.enter_context(nc.psum_tensor(f"pG{i}", [128, 512], F32)) for i in range(2)]
        pU = [stC.enter_context(nc.psum_tensor(f"pU{i}", [128, 512], F32)) for i in range(2)]
        pD = [stC.enter_context(nc.psum_tensor(f"pD{i}", [128, 512], F32)) for i in range(2)]
        tC = dict(am=s_am2, sc=s_sc2, rs=s_rs2, au=s_au2, L=s_L2, t1=s_t12)

        wtr = []
        wqdep = barC0 + [wq_store_toks[-1]]
        for fc in range(8):
            wtr.append(DMA("sync", lambda e, fc=fc: e.dma_start(
                out=s_wg[:, fc, :], in_=wgq_n[:, fc * 128:(fc + 1) * 128], transpose=True),
                "wtr", deps=wqdep if fc == 0 else ()))
            wtr.append(DMA("sync", lambda e, fc=fc: e.dma_start(
                out=s_wu[:, fc, :], in_=wuq_n[:, fc * 128:(fc + 1) * 128], transpose=True), "wtr"))
        for ic in range(22):
            wtr.append(DMA("sync", lambda e, ic=ic: e.dma_start(
                out=s_wd[:, ic, :], in_=wdq_n[:, ic * 128:(ic + 1) * 128], transpose=True), "wtr"))
        wtr_tok = wtr[-1]

        x2b_free = {0: barC0, 1: barC0}
        xln_free = {0: barC0, 1: barC0}
        xq_free = [()]
        xqscr_freeC = {0: (), 1: ()}
        hqscr_freeC = {0: (), 1: ()}
        xqf_free = {0: (), 1: ()}
        hq_free = [()]
        hqT_free = {0: (), 1: ()}
        sil_free = {0: (), 1: ()}
        outt_free = {0: (), 1: ()}
        gup_free = {0: (), 1: ()}
        pD_free = {0: (), 1: ()}
        out_toks = []
        st1 = {}
        st2 = {}

        def stage1(tt):
            pc_ = tt % 2
            txl = DMA("sync", lambda e, tt=tt, pc_=pc_: e.dma_start(
                out=s_xln[pc_][:], in_=xlnscr[tt * 128:(tt + 1) * 128, :]), "cl0",
                deps=list(xln_free[0]) + [xln_store[-1]])
            tq, tsgx = emit_quant(s_xln[pc_][:], s_xq[:], D, tC, [txl] + list(xq_free[0]))
            tqs = DMA("gpsimd", lambda e, pc_=pc_: e.dma_start(out=xqscr[pc_], in_=s_xq[:]),
                      f"xqs{pc_}", deps=[tq] + list(xqscr_freeC[pc_]))
            xq_free[0] = [tqs]
            tqt = DMA("sync", lambda e, pc_=pc_: e.dma_start(
                out=s_xqf[pc_][:], in_=xqscr[pc_], transpose=True), f"xqt{pc_}",
                deps=[tqs] + list(xqf_free[pc_]))
            xqscr_freeC[pc_] = [tqt]
            xln_free[0] = [tq, tsgx]
            st1[tt] = tqt

        def stage2(tt):
            pc_ = tt % 2
            tqt = st1.pop(tt)
            h_toks = []
            for ic in range(6):
                iw = 512 if ic < 5 else 256
                isl = slice(ic * 512, ic * 512 + iw)
                gp = ic % 2
                tg = None
                for fc in range(8):
                    d = ([tqt, wtr_tok] + list(gup_free[gp])) if fc == 0 else ()
                    tg = OP("tensor", lambda e, fc=fc, isl=isl, iw=iw, gp=gp, pc_=pc_: e.matmul(
                        pG[gp][:, 0:iw], s_xqf[pc_][:, fc, :], s_wg[:, fc, isl],
                        start=(fc == 0), stop=(fc == 7)), deps=d)
                tu = None
                for fc in range(8):
                    tu = OP("tensor", lambda e, fc=fc, isl=isl, iw=iw, gp=gp, pc_=pc_: e.matmul(
                        pU[gp][:, 0:iw], s_xqf[pc_][:, fc, :], s_wu[:, fc, isl],
                        start=(fc == 0), stop=(fc == 7)))
                tsl = OP("scalar", lambda e, iw=iw, gp=gp: e.activation(
                    out=s_sil[gp][:, 0:iw], in_=pG[gp][:, 0:iw], func=AF.Silu),
                    deps=[tg] + list(sil_free[gp]))
                th = OP("vector", lambda e, isl=isl, iw=iw, gp=gp: e.tensor_mul(
                    out=s_h[:, isl], in0=s_sil[gp][:, 0:iw], in1=pU[gp][:, 0:iw]),
                    deps=[tsl, tu] + (list(hq_free[0]) if ic == 0 else []))
                gup_free[gp] = [th]
                sil_free[gp] = [th]
                h_toks.append(th)
            xqf_free[pc_] = [h_toks[-1]]
            tqh, tsgh = emit_quant(s_h[:], s_hq[:], FI, tC, [h_toks[-1]])
            tqhs = DMA("gpsimd", lambda e, pc_=pc_: e.dma_start(out=hqscr[pc_], in_=s_hq[:]),
                       f"hqs{pc_}", deps=[tqh] + list(hqscr_freeC[pc_]))
            hq_free[0] = [tqhs, tsgh]
            tqht = DMA("sync", lambda e, pc_=pc_: e.dma_start(
                out=s_hqT[pc_][:], in_=hqscr[pc_], transpose=True), f"hqt{pc_}",
                deps=[tqhs] + list(hqT_free[pc_]))
            hqscr_freeC[pc_] = [tqht]
            st2[tt] = tqht

        def stage3(tt):
            pc_ = tt % 2
            tqht = st2.pop(tt)
            tx2 = DMA("sync", lambda e, tt=tt: e.dma_start(
                out=s_x2b[0][:], in_=x2scr[tt * 128:(tt + 1) * 128, :]), "cx0",
                deps=x2b_free[0])
            dtoks = []
            for dh in range(2):
                dsl = slice(dh * 512, (dh + 1) * 512)
                td = None
                for ic in range(22):
                    d = ([tqht] + list(pD_free[dh])) if ic == 0 else ()
                    td = OP("tensor", lambda e, ic=ic, dsl=dsl, dh=dh, pc_=pc_: e.matmul(
                        pD[dh][:, :], s_hqT[pc_][:, ic, :], s_wd[:, ic, dsl],
                        start=(ic == 0), stop=(ic == 21)), deps=d)
                ta2 = OP("vector", lambda e, dsl=dsl, dh=dh: e.tensor_add(
                    out=s_outt[0][:, dsl], in0=pD[dh][:, :], in1=s_x2b[0][:, dsl]),
                    deps=[td, tx2] + (list(outt_free[0]) if dh == 0 else []))
                pD_free[dh] = [ta2]
                dtoks.append(ta2)
            hqT_free[pc_] = [dtoks[-1]]
            x2b_free[0] = [dtoks[-1]]
            to = DMA("gpsimd", lambda e, tt=tt: e.dma_start(
                out=out[tt * 128:(tt + 1) * 128, :], in_=s_outt[0][:]), "outs",
                deps=dtoks)
            outt_free[0] = [to]
            out_toks.append(to)

        for it in range(18):
            if it < 16:
                stage1(it)
            if 1 <= it < 17:
                stage2(it - 1)
            if it >= 2:
                stage3(it - 2)

        OP("vector", lambda e: e.engine_nop(), deps=[out_toks[-1]])
        stC.close()
        pr.build(block)
    return nc



def _qrow_idx(half):
    tiles = TILES_BY_POS[half]
    return np.concatenate([np.arange(t * 512, (t + 1) * 512) for t in tiles])


def _rope_tables():
    j = np.arange(16, dtype=np.float64)
    inv = 10000.0 ** (-2.0 * j / 32.0)
    pos = np.arange(S, dtype=np.float64)
    ang = pos[None, :] * inv[:, None]
    return (np.cos(ang).astype(np.float32), np.sin(ang).astype(np.float32), inv)


def _masks_for(half):
    m = np.zeros((NPOS, 8, 128, QT), dtype=np.float32)
    q = np.arange(QT)
    k = np.arange(128)
    for p in range(NPOS):
        e = EXT[p]
        t = TILES_BY_POS[half][p]
        r = 4 * t + 4
        q0 = t * 512
        for j in range(8):
            kc = e - 8 + j
            if kc < r - 4:
                m[p, j] = 1.0
            elif kc < r:
                m[p, j] = ((q0 + q[None, :]) >= (kc * 128 + k[:, None])).astype(np.float32)
    return m


def _deinterleave_rows(w, dhr, nh):
    w3 = w.reshape(nh, dhr, -1)
    return (np.ascontiguousarray(w3[:, 0::2, :].reshape(nh * (dhr // 2), -1)),
            np.ascontiguousarray(w3[:, 1::2, :].reshape(nh * (dhr // 2), -1)))


def kernel(L0_attn_x, L0_attn_W_DQ, L0_attn_W_UQ, L0_attn_W_QR, L0_attn_W_DKV,
           L0_attn_W_UK, L0_attn_W_UV, L0_attn_W_KR, L0_attn_W_O,
           L0_ffn_w_gate, L0_ffn_w_up, L0_ffn_w_down):
    X = np.ascontiguousarray(np.asarray(L0_attn_x, dtype=np.float32))
    T = lambda a: np.ascontiguousarray(np.asarray(a, dtype=np.float32).T)
    wdq_t = T(L0_attn_W_DQ)
    wuq_t = T(L0_attn_W_UQ)
    wdkv_t = T(L0_attn_W_DKV)
    wuk_t = T(L0_attn_W_UK)
    wuv_t = T(L0_attn_W_UV)
    wo_t = T(L0_attn_W_O)
    qr1, qr2 = _deinterleave_rows(np.asarray(L0_attn_W_QR, np.float32), DHR, H)
    wqr1_t, wqr2_t = T(qr1), T(qr2)
    kr1, kr2 = _deinterleave_rows(np.asarray(L0_attn_W_KR, np.float32), DHR, 1)
    wkr1_t, wkr2_t = T(kr1), T(kr2)
    wg_n = np.ascontiguousarray(np.asarray(L0_ffn_w_gate, np.float32))
    wu_n = np.ascontiguousarray(np.asarray(L0_ffn_w_up, np.float32))
    wd_n = np.ascontiguousarray(np.asarray(L0_ffn_w_down, np.float32))

    cosk, sink, inv = _rope_tables()
    consts = np.zeros((128, 16), dtype=np.float32)
    consts[:, 0] = SC
    consts[:, 1] = EPS
    consts[:, 10] = -0.5
    consts[:, 11] = -1.0

    in_maps = []
    qidx = {}
    for c in range(8):
        b, half = c // 2, c % 2
        qidx[c] = _qrow_idx(half)
        pos = qidx[c].astype(np.float64)
        ang = pos[None, :] * np.tile(inv, 8)[:, None]
        in_maps.append(dict(
            x_full=X[b], xq_in=np.ascontiguousarray(X[b][qidx[c]]),
            wdq_t=wdq_t, wuq_t=wuq_t, wqr1_t=wqr1_t, wqr2_t=wqr2_t,
            wdkv_t=wdkv_t, wuk_t=wuk_t, wuv_t=wuv_t,
            wkr1_t=wkr1_t, wkr2_t=wkr2_t, wo_t=wo_t,
            wg_n=wg_n, wu_n=wu_n, wd_n=wd_n,
            cosk=cosk, sink=sink,
            c8q=np.cos(ang).astype(np.float32),
            s8q=np.sin(ang).astype(np.float32),
            masks_in=_masks_for(half), consts=consts,
        ))

    nc = build_program()
    res = run_bass_kernel_spmd(nc, in_maps, core_ids=list(range(8)))
    full = np.empty((B, S, D), dtype=np.float32)
    for c in range(8):
        full[c // 2][qidx[c]] = np.asarray(res.results[c]["out"], dtype=np.float32)
    if DEBUG:
        kernel.debug_results = res.results
    return full
